# version 02
# baseline (speedup 1.0000x reference)
import math
import numpy as np
import ml_dtypes
import concourse.bass as bass
import concourse.mybir as mybir
from concourse.bass_utils import run_bass_kernel_spmd

F32 = mybir.dt.float32
BF16 = mybir.dt.bfloat16
I32 = mybir.dt.int32
U32 = mybir.dt.uint32
A = mybir.AluOpType
AF = mybir.ActivationFunctionType
AX = mybir.AxisListType
BF = ml_dtypes.bfloat16

NCORES = 8
L = 16384
DM = 1024
TOWN = L // NCORES
HALO = 512
TEXT = TOWN + HALO
SEXT = TOWN + 384
MEXT = TOWN + 128
NQT = MEXT // 128
NIN = 2816
NINC = NIN // 128
FSL = 256
NFS = TOWN // FSL
SHN = 79744


class Sem:
    def __init__(self, nc, es, name):
        self.h = es.enter_context(nc.semaphore(name))
        self.v = 0

    def inc(self, instr, n=1):
        instr.then_inc(self.h, n)
        self.v += n
        return instr

    def dinc(self, instr):
        return self.inc(instr, 16)

    def wait(self, eng, val=None):
        eng.wait_ge(self.h, self.v if val is None else val)


def build_ffn_layer(exitv, dbg=False):
    from contextlib import ExitStack
    nc = bass.Bass("TRN2", target_bir_lowering=False, debug=False)
    es = ExitStack()

    yin = nc.dram_tensor("yin", [DM, TEXT], F32, kind="ExternalInput").ap()
    ynext = nc.dram_tensor("ynext", [DM, TOWN], F32, kind="ExternalOutput").ap()
    if exitv:
        xr = nc.dram_tensor("xr", [DM, TOWN], F32, kind="ExternalInput").ap()
        xnext = nc.dram_tensor("xnext", [DM, TOWN], F32, kind="ExternalOutput").ap()
    wqkv = nc.dram_tensor("wqkv", [DM, 3072], F32, kind="ExternalInput").ap()
    wo = nc.dram_tensor("wo", [DM, DM], F32, kind="ExternalInput").ap()
    w1p = nc.dram_tensor("w1p", [DM, 2 * NIN], F32, kind="ExternalInput").ap()
    w2p = nc.dram_tensor("w2p", [NIN, DM], F32, kind="ExternalInput").ap()
    b1p = nc.dram_tensor("b1p", [128, 2 * NINC], F32, kind="ExternalInput").ap()
    b2v = nc.dram_tensor("b2v", [128, 8], F32, kind="ExternalInput").ap()
    qsv = nc.dram_tensor("qsv", [128, 1], F32, kind="ExternalInput").ap()
    ksv = nc.dram_tensor("ksv", [128, 1], F32, kind="ExternalInput").ap()
    lngb = nc.dram_tensor("lngb", [128, 16], F32, kind="ExternalInput").ap()
    csq = nc.dram_tensor("csq", [128, SEXT], BF16, kind="ExternalInput").ap()
    ssq = nc.dram_tensor("ssq", [128, SEXT], BF16, kind="ExternalInput").ap()
    csk = nc.dram_tensor("csk", [128, SEXT], BF16, kind="ExternalInput").ap()
    ssk = nc.dram_tensor("ssk", [128, SEXT], BF16, kind="ExternalInput").ap()
    rmat = nc.dram_tensor("rmat", [128, 128], BF16, kind="ExternalInput").ap()
    ob2d = nc.dram_tensor("ob2d", [128, 2], BF16, kind="ExternalInput").ap()
    ob2td = nc.dram_tensor("ob2td", [2, 128], F32, kind="ExternalInput").ap()
    iden = nc.dram_tensor("iden", [128, 128], BF16, kind="ExternalInput").ap()
    masks = nc.dram_tensor("masks", [NQT * 128, 384], BF16, kind="ExternalInput").ap()
    ymid = nc.dram_tensor("ymid", [DM, MEXT], F32)
    if dbg:
        ymdbg = nc.dram_tensor("ymdbg", [DM, MEXT], F32, kind="ExternalOutput").ap()

    T = lambda name, shape, dt: es.enter_context(nc.sbuf_tensor(name, shape, dt))

    QSC = T("QSC", [128, 1], F32)
    KSC = T("KSC", [128, 1], F32)
    LGB = T("LGB", [128, 16], F32)
    B1 = T("B1", [128, 2 * NINC], F32)
    B2 = T("B2", [128, 8], F32)
    OB2 = T("OB2", [128, 2], BF16)
    OB2T = T("OB2T", [2, 128], F32)
    OBC = T("OBC", [1, 128], F32)
    OMB = T("OMB", [128, 1], BF16)
    RM = T("RM", [128, 128], BF16)
    IDN = T("IDN", [128, 128], BF16)
    SH = T("SH", [128, SHN], BF16)
    YS32 = T("YS32", [128, 8 * 264], F32)
    ROWSB = T("ROWSB", [2, 512], F32)
    ROWSC = T("ROWSC", [2, 512], F32)
    ROWS = T("ROWS", [1, 2048], F32)
    TA = T("TA", [128, 512], F32)
    RSS = T("RSS", [128, 512], F32)
    P32 = T("P32", [128, 384], BF16)
    PN = T("PN", [128, 384], BF16)
    PTs = T("PTs", [128, 384], BF16)
    DEN = T("DEN", [128, 1], F32)
    REC = T("REC", [128, 1], F32)
    NM = T("NM", [128, 1], F32)
    EPS1 = T("EPS1", [1, 1], F32)
    AO = T("AO", [128, 8 * 128], BF16)
    YOLD = T("YOLD", [128, 8 * 128], F32)
    XRB = T("XRB", [128, FSL], F32)
    XOUT = T("XOUT", [128, FSL], F32)
    XN = T("XN", [128, 8 * FSL], BF16)
    YMB = T("YMB", [128, 8 * FSL], BF16)
    SQB = T("SQB", [128, 8 * FSL], BF16)

    def sh(off, n):
        return SH[:, off:off + n]

    YB_O = 0
    WQKV_O = YB_O + 8 * TEXT
    WO_O = WQKV_O + 8 * 3072
    KR_O = WO_O + 8 * 1024
    VR_O = KR_O + 6 * 1024
    QS_O = VR_O + 6 * 1024
    R1_O = QS_O + 8 * 512
    R2_O = R1_O + 512
    R3_O = R2_O + 512
    TBL_O = R3_O + 512
    MSK_O = TBL_O + 4 * 512
    assert MSK_O + NQT * 384 <= SHN
    W1_O = 0
    W2_O = W1_O + 8 * 2 * NIN
    GH_O = W2_O + NINC * 1024
    AH_O = GH_O + NINC * FSL
    assert AH_O + NINC * FSL <= SHN

    ps = [es.enter_context(nc.psum_tensor(f"ps{i}", [128, 512], F32)) for i in range(4)]
    pS = es.enter_context(nc.psum_tensor("pS", [128, 384], F32))
    pT = es.enter_context(nc.psum_tensor("pT", [128, 384], BF16))
    pV = es.enter_context(nc.psum_tensor("pV", [64, 128], F32))
    pR = es.enter_context(nc.psum_tensor("pR", [128, 512], F32))

    sL = Sem(nc, es, "sL")
    sG = Sem(nc, es, "sG")
    sPE = Sem(nc, es, "sPE")
    sV = Sem(nc, es, "sV")
    sA = Sem(nc, es, "sA")
    sST = Sem(nc, es, "sST")

    es.enter_context(nc.Block())
    sync, ten, vec, act, gp = nc.sync, nc.tensor, nc.vector, nc.scalar, nc.gpsimd
    ALL = (sync, ten, vec, act, gp)

    def wv(*engs):
        for e in engs:
            sV.wait(e)

    def wpe(*engs):
        for e in engs:
            sPE.wait(e)

    def wa(*engs):
        for e in engs:
            sA.wait(e)

    def mmgroup(pdst, tiles, inc=True):
        n = len(tiles)
        for i, (lhsT, rhs) in enumerate(tiles):
            instr = ten.matmul(pdst, lhsT, rhs, start=(i == 0), stop=(i == n - 1))
        if inc:
            sPE.inc(instr)
        return instr

    def ld8(dst_off, n, t):
        dst = sh(dst_off, 8 * n).rearrange("p (c n) -> p c n", c=8)
        return gp.dma_start(out=dst, in_=t.rearrange("(c p) n -> p c n", p=128))

    sG.dinc(ld8(WQKV_O, 3072, wqkv))
    sG.dinc(ld8(WO_O, 1024, wo))
    sG.dinc(ld8(YB_O, TEXT, yin))
    sL.dinc(sync.dma_start(out=sh(MSK_O, NQT * 384).rearrange("p (q n) -> p q n", q=NQT),
                           in_=masks.rearrange("(q p) n -> p q n", p=128)))
    sL.dinc(sync.dma_start(out=RM[:], in_=rmat))
    sL.dinc(sync.dma_start(out=IDN[:], in_=iden))
    sL.dinc(sync.dma_start(out=QSC[:], in_=qsv))
    sL.dinc(sync.dma_start(out=KSC[:], in_=ksv))
    sL.dinc(sync.dma_start(out=LGB[:], in_=lngb))
    sL.dinc(sync.dma_start(out=B1[:], in_=b1p))
    sL.dinc(sync.dma_start(out=B2[:], in_=b2v))
    sV.inc(vec.memset(OBC[:], 1.0))
    sV.inc(vec.memset(EPS1[:], 1e-5))
    sV.inc(vec.memset(OMB[:], 1.0 / 1024.0))
    sL.dinc(sync.dma_start(out=OB2[:], in_=ob2d))
    sL.dinc(sync.dma_start(out=OB2T[:], in_=ob2td))
    for e in ALL:
        sL.wait(e)
        sG.wait(e)
        sV.wait(e)

    def ybv(c, a, w, shifted):
        off = YB_O + c * TEXT + a - (1 if (shifted and c >= 4) else 0)
        return SH[:, off:off + w]

    def rope_side(src, w, cs_off, ss_off, scale_col, pieces, pewait=None):
        raw = sh(R1_O, 512)[:, 0:w]
        qn = sh(R2_O, 512)[:, 0:w]
        sq = sh(R3_O, 512)[:, 0:w]
        if pewait is None:
            wpe(vec)
        else:
            sPE.wait(vec, pewait)
        sV.inc(vec.tensor_copy(raw, src[:, 0:w]))
        rec = sV.v
        wv(act)
        sA.inc(act.square(sq, raw))
        wa(ten)
        wv(ten)
        sPE.inc(ten.matmul(pR[0:2, 0:w], OB2[:], sq, start=True, stop=True))
        wpe(act)
        sA.inc(act.sqrt(ROWSB[0:2, 0:w], pR[0:2, 0:w]))
        wa(vec)
        sV.inc(vec.tensor_scalar_max(ROWSB[0:2, 0:w], ROWSB[0:2, 0:w], 1e-12))
        wv(vec)
        sV.inc(vec.reciprocal(ROWSC[0:2, 0:w], ROWSB[0:2, 0:w]))
        wv(ten)
        wa(ten)
        sPE.inc(ten.matmul(pR[0:128, 0:w], OB2T[:], ROWSC[0:2, 0:w], start=True, stop=True))
        wpe(vec)
        sV.inc(vec.scalar_tensor_tensor(qn, raw, scale_col, pR[:, 0:w], A.mult, A.mult))
        wv(ten)
        sPE.inc(ten.matmul(pR[0:128, 0:w], RM[:], qn, start=True, stop=True))
        sV.inc(vec.tensor_mul(raw, qn, sh(cs_off, 512)[:, 0:w]))
        wpe(vec)
        wv(vec)
        sV.inc(vec.tensor_mul(qn, pR[:, 0:w], sh(ss_off, 512)[:, 0:w]))
        wv(vec)
        for (doff, coff, cw) in pieces:
            sV.inc(vec.tensor_add(sh(doff, cw), raw[:, coff:coff + cw], qn[:, coff:coff + cw]))
        return rec

    taslot = {}
    slotv = {}
    qslabs = [(128, 512), (640, 512), (1152, 512), (1664, 512), (2176, 384)]

    for (a, w) in qslabs:
        tcol = a - 128
        t0 = tcol // 128
        ntile = w // 128
        wv(sync)
        for k, tb in enumerate((csq, ssq, csk, ssk)):
            sL.dinc(sync.dma_start(out=sh(TBL_O + k * 512, 512)[:, 0:w], in_=tb[:, tcol:tcol + w]))
        sL.wait(ten)
        sL.wait(vec)
        for ti in range(ntile):
            slot = (t0 + ti) % 6
            for half in range(2):
                pp = ps[half]
                sV.wait(ten, slotv.get(half, 0))
                mmgroup(pp[:, 0:512],
                        [(ybv(kc, a + ti * 128, 128, True),
                          sh(WQKV_O + kc * 3072 + 2048 + half * 512, 512)) for kc in range(8)])
                wpe(vec)
                sV.inc(vec.tensor_copy(sh(VR_O + slot * 1024 + half * 512, 512), pp[:, 0:512]))
                slotv[half] = sV.v
        pend = None
        for g in range(8):
            si = 2 + (g % 2)
            pp = ps[si]
            sV.wait(ten, slotv.get(si, 0))
            mmgroup(pp[:, 0:w],
                    [(sh(WQKV_O + kc * 3072 + g * 128, 128), ybv(kc, a, w, True)) for kc in range(8)])
            pev = sPE.v
            if pend is not None:
                gi, psrc, pw = pend
                slotv[2 + (gi % 2)] = rope_side(psrc, w, TBL_O, TBL_O + 512, QSC[:, 0:1],
                                                [(QS_O + gi * 512, 0, w)], pewait=pw)
            pend = (g, pp, pev)
        gi, psrc, pw = pend
        slotv[2 + (gi % 2)] = rope_side(psrc, w, TBL_O, TBL_O + 512, QSC[:, 0:1],
                                        [(QS_O + gi * 512, 0, w)], pewait=pw)
        pend = None
        for g in range(8):
            si = 2 + (g % 2)
            pp = ps[si]
            sV.wait(ten, slotv.get(si, 0))
            mmgroup(pp[:, 0:w],
                    [(sh(WQKV_O + kc * 3072 + 1024 + g * 128, 128), ybv(kc, a, w, True)) for kc in range(8)])
            pev = sPE.v
            if pend is not None:
                gi, psrc, pw = pend
                pieces = [(KR_O + ((t0 + ti) % 6) * 1024 + gi * 128, ti * 128, 128) for ti in range(ntile)]
                slotv[2 + (gi % 2)] = rope_side(psrc, w, TBL_O + 1024, TBL_O + 1536, KSC[:, 0:1],
                                                pieces, pewait=pw)
            pend = (g, pp, pev)
        gi, psrc, pw = pend
        pieces = [(KR_O + ((t0 + ti) % 6) * 1024 + gi * 128, ti * 128, 128) for ti in range(ntile)]
        slotv[2 + (gi % 2)] = rope_side(psrc, w, TBL_O + 1024, TBL_O + 1536, KSC[:, 0:1],
                                        pieces, pewait=pw)
        jlo = max(0, t0 - 2)
        jhi = min(NQT - 1, t0 + ntile - 3)
        for j in range(jlo, jhi + 1):
            wv(sync)
            for c in range(8):
                sL.dinc(sync.dma_start(out=YOLD[:, c * 128:(c + 1) * 128],
                                       in_=yin[c * 128:(c + 1) * 128, 384 + 128 * j:384 + 128 * (j + 1)]))
            qcol = 384 + 128 * j - a
            for h in range(16):
                g2 = h // 2
                hr = (h % 2) * 64
                wv(ten)
                wa(ten)
                for ti2 in range(3):
                    slot = (j + ti2) % 6
                    ten.matmul(pS[:, ti2 * 128:(ti2 + 1) * 128],
                               sh(QS_O + g2 * 512 + qcol, 128)[hr:hr + 64, :],
                               sh(KR_O + slot * 1024 + g2 * 128, 128)[hr:hr + 64, :],
                               start=(ti2 == 0), stop=False)
                sPE.inc(ten.matmul(pS[:, 0:384], IDN[:], sh(MSK_O + j * 384, 384), start=False, stop=True))
                wpe(vec)
                wa(vec)
                sV.inc(vec.reduce_max(NM[:], pS[:, 0:384], axis=AX.X, negate=True))
                wv(act)
                sA.inc(act.activation(out=P32[:], in_=pS[:, 0:384], func=AF.Exp,
                                      bias=NM[:, 0:1], scale=1.0, accum_out=DEN[:]))
                wa(vec)
                sV.inc(vec.reciprocal(REC[:], DEN[:]))
                wv(vec)
                wpe(vec)
                sV.inc(vec.tensor_scalar_mul(PN[:], P32[:], REC[:, 0:1]))
                wv(ten)
                for ti2 in range(3):
                    instr = ten.transpose(pT[:, ti2 * 128:(ti2 + 1) * 128], PN[:, ti2 * 128:(ti2 + 1) * 128], IDN[:])
                sPE.inc(instr)
                wpe(vec)
                sV.inc(vec.tensor_copy(PTs[:], pT[:, 0:384]))
                wv(ten)
                mmgroup(pV[:, 0:128],
                        [(sh(VR_O + ((j + ti2) % 6) * 1024 + h * 64, 64), PTs[:, ti2 * 128:(ti2 + 1) * 128])
                         for ti2 in range(3)])
                wpe(vec)
                sV.inc(vec.tensor_copy(AO[hr:hr + 64, g2 * 128:(g2 + 1) * 128], pV[:, 0:128]))
            sL.wait(vec)
            for oc in range(8):
                pp = ps[oc % 2]
                slot = oc % 4
                wv(ten)
                mmgroup(pp[:, 0:128],
                        [(sh(WO_O + kc * 1024 + oc * 128, 128), AO[:, kc * 128:(kc + 1) * 128]) for kc in range(8)])
                wpe(vec)
                if slot in taslot:
                    sST.wait(vec)
                sV.inc(vec.scalar_tensor_tensor(TA[:, slot * 128:slot * 128 + 128], pp[:, 0:128], 1.0,
                                                YOLD[:, oc * 128:(oc + 1) * 128], A.mult, A.add))
                wv(sync)
                sST.dinc(sync.dma_start(out=ymid[oc * 128:(oc + 1) * 128, 128 * j:128 * (j + 1)], in_=TA[:, slot * 128:slot * 128 + 128]))
                taslot[slot] = sST.v

    sPE.wait(gp)
    sV.wait(gp)
    sA.wait(gp)
    sG.dinc(ld8(W1_O, 2 * NIN, w1p))
    sG.dinc(gp.dma_start(out=sh(W2_O, NINC * 1024).rearrange("p (c n) -> p c n", c=NINC),
                         in_=w2p.rearrange("(c p) n -> p c n", p=128)))
    for e in (ten, vec, act, sync):
        sG.wait(e)
    sST.wait(sync)

    for s in range(NFS):
        c0 = 128 + FSL * s
        wv(sync)
        wa(sync)
        for c in range(8):
            sL.dinc(sync.dma_start(out=YS32[:, c * 264:c * 264 + FSL + 1],
                                   in_=ymid[c * 128:(c + 1) * 128, c0 - 1:c0 + FSL]))
        sL.wait(vec)
        wpe(vec)
        for half in range(2):
            off = 1 - half
            src = YS32[:].rearrange("p (c n) -> p c n", c=8)[:, 4 * half:4 * half + 4, off:off + FSL]
            dst = YMB[:].rearrange("p (c n) -> p c n", c=8)[:, 4 * half:4 * half + 4, 0:FSL]
            sV.inc(vec.tensor_copy(dst, src))
        wv(act, ten)
        sA.inc(act.square(SQB[:], YMB[:]))
        wa(ten)
        wv(ten)
        for c in range(8):
            ten.matmul(pR[0:1, 0:FSL], OMB[:], YMB[:, c * FSL:(c + 1) * FSL],
                       start=(c == 0), stop=(c == 7))
        for c in range(8):
            instr = ten.matmul(pR[0:1, FSL:2 * FSL], OMB[:], SQB[:, c * FSL:(c + 1) * FSL],
                               start=(c == 0), stop=(c == 7))
        sPE.inc(instr)
        wpe(vec)
        sV.inc(vec.tensor_copy(ROWS[0:1, 0:512], pR[0:1, 0:512]))
        wv(vec)
        sV.inc(vec.tensor_mul(ROWS[0:1, 512:512 + FSL], ROWS[0:1, 0:FSL], ROWS[0:1, 0:FSL]))
        wv(vec)
        sV.inc(vec.tensor_sub(ROWS[0:1, 512:512 + FSL], ROWS[0:1, FSL:2 * FSL], ROWS[0:1, 512:512 + FSL]))
        wv(act)
        sA.inc(act.activation(out=ROWS[0:1, 768:768 + FSL], in_=ROWS[0:1, 512:512 + FSL], func=AF.Sqrt, bias=EPS1[0:1, 0:1], scale=1.0))
        wa(vec)
        sV.inc(vec.reciprocal(ROWS[0:1, 512:512 + FSL], ROWS[0:1, 768:768 + FSL]))
        wv(ten)
        ten.matmul(pR[:, 0:FSL], OBC[:], ROWS[0:1, 0:FSL], start=True, stop=True)
        sPE.inc(ten.matmul(pR[:, FSL:2 * FSL], OBC[:], ROWS[0:1, 512:512 + FSL], start=True, stop=True))
        wpe(vec)
        sV.inc(vec.tensor_copy(RSS[:, 0:FSL], pR[:, FSL:2 * FSL]))
        for c in range(8):
            off = 1 if c < 4 else 0
            ysrc = YS32[:, c * 264 + off:c * 264 + off + FSL]
            wv(vec)
            if ("w2", 1) in taslot:
                sST.wait(vec)
            sV.inc(vec.tensor_sub(TA[:, FSL:2 * FSL], ysrc, pR[:, 0:FSL]))
            wv(vec)
            sV.inc(vec.scalar_tensor_tensor(TA[:, FSL:2 * FSL], TA[:, FSL:2 * FSL], LGB[:, c:c + 1], RSS[:, 0:FSL], A.mult, A.mult))
            wv(vec)
            wpe(vec)
            sV.inc(vec.tensor_scalar_add(XN[:, c * FSL:(c + 1) * FSL], TA[:, FSL:2 * FSL], LGB[:, 8 + c:9 + c]))
        for oc in range(2 * NINC):
            pp = ps[oc % 2]
            wv(ten)
            wa(ten)
            mmgroup(pp[:, 0:FSL],
                    [(sh(W1_O + kc * 2 * NIN + oc * 128, 128), XN[:, kc * FSL:(kc + 1) * FSL]) for kc in range(8)])
            if oc < NINC:
                wpe(vec)
                sV.inc(vec.tensor_scalar_add(sh(AH_O + oc * FSL, FSL), pp[:, 0:FSL], B1[:, oc:oc + 1]))
            else:
                wpe(act)
                sA.inc(act.activation(out=sh(GH_O + (oc - NINC) * FSL, FSL), in_=pp[:, 0:FSL],
                                      func=AF.Gelu, bias=B1[:, oc:oc + 1], scale=1.0))
        wv(vec)
        wa(vec)
        sV.inc(vec.tensor_mul(sh(AH_O, NINC * FSL), sh(AH_O, NINC * FSL), sh(GH_O, NINC * FSL)))
        for oc in range(8):
            pp = ps[2 + oc % 2]
            wv(ten)
            mmgroup(pp[:, 0:FSL],
                    [(sh(W2_O + kc * 1024 + oc * 128, 128), sh(AH_O + kc * FSL, FSL)) for kc in range(NINC)])
            wpe(vec)
            slot2 = oc % 2
            tv = TA[:, slot2 * FSL:(slot2 + 1) * FSL]
            if ("w2", slot2) in taslot:
                sST.wait(vec)
            ysrc = YS32[:, oc * 264 + 1:oc * 264 + 1 + FSL]
            sV.inc(vec.scalar_tensor_tensor(tv, pp[:, 0:FSL], 1.0, ysrc, A.mult, A.add))
            wv(vec)
            sV.inc(vec.tensor_scalar_add(tv, tv, B2[:, oc:oc + 1]))
            wv(sync)
            sST.dinc(sync.dma_start(out=ynext[oc * 128:(oc + 1) * 128, FSL * s:FSL * (s + 1)], in_=tv))
            taslot[("w2", slot2)] = sST.v
            if exitv:
                sL.dinc(sync.dma_start(out=XRB[:, 0:FSL], in_=xr[oc * 128:(oc + 1) * 128, FSL * s:FSL * (s + 1)]))
                sL.wait(vec)
                if ("xo",) in taslot:
                    sST.wait(vec)
                sV.inc(vec.tensor_add(XOUT[:, 0:FSL], tv, XRB[:, 0:FSL]))
                wv(sync)
                sST.dinc(sync.dma_start(out=xnext[oc * 128:(oc + 1) * 128, FSL * s:FSL * (s + 1)], in_=XOUT[:, 0:FSL]))
                taslot[("xo",)] = sST.v

    if dbg:
        sST.wait(gp)
        for c in range(8):
            sST.dinc(gp.dma_start(out=ymdbg[c * 128:(c + 1) * 128, :], in_=ymid[c * 128:(c + 1) * 128, :]))
    sST.wait(sync)
    sST.wait(gp)
    es.close()
    return nc


def ffn_tables(base):
    t = np.arange(base - 384, base + 2048, dtype=np.float64)
    t = np.clip(t, 0, None)
    halfd = np.arange(0, DH_ if False else 64, 2, dtype=np.float64)
    inv_freq = 1.0 / 10000.0 ** (halfd / 64.0)
    freqs = t[:, None] * inv_freq[None, :]
    freqs = np.concatenate([freqs, freqs], -1)
    base_scale = (halfd + 0.4 * 64) / (1.4 * 64)
    power = (t - L // 2) / (256 // 2)
    scale = base_scale[None, :] ** power[:, None]
    scale = np.concatenate([scale, scale], -1)
    cos, sin = np.cos(freqs), np.sin(freqs)
    csq = (cos * scale * 8.0).T
    ssq = (sin * scale * 8.0).T
    csk = (cos / scale).T
    ssk = (sin / scale).T
    out = []
    for tab in (csq, ssq, csk, ssk):
        out.append(np.tile(tab, (2, 1)).astype(BF))
    return out


def ffn_masks(base):
    j = np.arange(NQT)[:, None, None]
    p = np.arange(128)[None, :, None]
    k = np.arange(384)[None, None, :]
    tq = base - 128 + 128 * j + p
    tk = base - 384 + 128 * j + k
    valid = (tk >= 0) & (tk <= tq) & (tq - tk <= 256) & (tq >= 0)
    return np.where(valid, np.float32(0), np.float32(-1e30)).astype(BF)


def rot_mat():
    m = np.zeros((128, 128), np.float32)
    for h in range(2):
        for i in range(32):
            m[h * 64 + 32 + i, h * 64 + i] = -1.0
            m[h * 64 + i, h * 64 + 32 + i] = 1.0
    return m.astype(BF)


def f32c(x):
    return np.ascontiguousarray(x, dtype=np.float32)


def ffn_weight_maps(inp, li):
    Wqkv = f32c(inp["L0_ffn_Wqkv"][li])
    Wo = f32c(inp["L0_ffn_Wo"][li])
    W1 = inp["L0_ffn_W1"][li]
    b1 = inp["L0_ffn_b1"][li]
    W2 = inp["L0_ffn_W2"][li]
    b2 = inp["L0_ffn_b2"][li]
    NI = 2730
    W1p = np.zeros((DM, 2 * NIN), np.float32)
    W1p[:, :NI] = W1[:, :NI]
    W1p[:, NIN:NIN + NI] = W1[:, NI:]
    b1pad = np.zeros((2 * NIN,), np.float32)
    b1pad[:NI] = b1[:NI]
    b1pad[NIN:NIN + NI] = b1[NI:]
    W2p = np.zeros((NIN, DM), np.float32)
    W2p[:NI] = W2
    qs = np.tile(inp["L0_ffn_q_scale"][li], 2).astype(np.float32)[:, None]
    ks = np.tile(inp["L0_ffn_k_scale"][li], 2).astype(np.float32)[:, None]
    lngb = np.stack([inp["L0_ffn_ln_g"][li].reshape(8, 128),
                     inp["L0_ffn_ln_b"][li].reshape(8, 128)], 0).reshape(16, 128).T
    return dict(
        wqkv=Wqkv, wo=Wo,
        w1p=W1p, w2p=W2p,
        b1p=np.ascontiguousarray(b1pad.reshape(2 * NINC, 128).T),
        b2v=np.ascontiguousarray(b2.reshape(8, 128).T),
        qsv=f32c(qs), ksv=f32c(ks),
        lngb=f32c(lngb),
        rmat=rot_mat(), iden=np.eye(128, dtype=BF),
        ob2d=np.kron(np.eye(2), np.ones((64, 1))).astype(BF),
        ob2td=np.kron(np.eye(2), np.ones((1, 64))).astype(np.float32),
    )


def run_ffn_layer(progs, y_full, li, inp, wcache, exitv=False, xr_full=None):
    key = ("layer", li)
    if key not in wcache:
        wcache[key] = ffn_weight_maps(inp, li)
    wmaps = wcache[key]
    if "tbl" not in wcache:
        wcache["tbl"] = [ffn_tables(TOWN * c) for c in range(NCORES)]
        wcache["msk"] = [np.ascontiguousarray(ffn_masks(TOWN * c).reshape(NQT * 128, 384)) for c in range(NCORES)]
    yext = np.zeros((NCORES, DM, TEXT), np.float32)
    for c in range(NCORES):
        b = TOWN * c
        lo = max(0, b - HALO)
        yext[c, :, HALO - (b - lo):] = y_full[:, lo:b + TOWN]
    in_maps = []
    for c in range(NCORES):
        m = dict(yin=yext[c], **wmaps)
        t4 = wcache["tbl"][c]
        m.update(csq=t4[0], ssq=t4[1], csk=t4[2], ssk=t4[3], masks=wcache["msk"][c])
        if exitv:
            m["xr"] = np.ascontiguousarray(xr_full[:, TOWN * c:TOWN * (c + 1)])
        in_maps.append(m)
    nc = progs["layer_exit" if exitv else "layer"]
    res = run_bass_kernel_spmd(nc, in_maps, core_ids=list(range(NCORES)))
    ynext = np.concatenate([np.asarray(r["ynext"]) for r in res.results], axis=1)
    if exitv:
        xnext = np.concatenate([np.asarray(r["xnext"]) for r in res.results], axis=1)
        return ynext, xnext
    return ynext, None


DH_ = 64

if __name__ == "__main__":
    nc = build_ffn_layer(False)
    print("built", len(list(nc.all_instructions())), "instructions")


def build_entry():
    from contextlib import ExitStack
    nc = bass.Bass("TRN2", target_bir_lowering=False, debug=False)
    es = ExitStack()
    xa = nc.dram_tensor("xa", [DM, TOWN], F32, kind="ExternalInput").ap()
    xb = nc.dram_tensor("xb", [DM, TOWN], F32, kind="ExternalInput").ap()
    x1own = nc.dram_tensor("x1own", [DM, TOWN], F32, kind="ExternalOutput").ap()
    y0own = nc.dram_tensor("y0own", [DM, TOWN], F32, kind="ExternalOutput").ap()
    ob1f = nc.dram_tensor("ob1f", [128, 1], F32, kind="ExternalInput").ap()
    obcf = nc.dram_tensor("obcf", [1, 128], F32, kind="ExternalInput").ap()

    T = lambda name, shape, dt: es.enter_context(nc.sbuf_tensor(name, shape, dt))
    XA = T("XA", [128, 8 * 512], F32)
    XB = T("XB", [128, 8 * 512], F32)
    SQ = T("SQ", [128, 8 * 512], F32)
    ROWS = T("ROWS", [1, 2048], F32)
    RSS = T("RSS", [128, 512], F32)
    OB1F = T("OB1F", [128, 1], F32)
    OBCF = T("OBCF", [1, 128], F32)
    EPS1 = T("EPS1", [1, 1], F32)
    TO = T("TO", [128, 2 * 512], F32)
    pR = es.enter_context(nc.psum_tensor("pR", [128, 512], F32))
    pQ = es.enter_context(nc.psum_tensor("pQ", [128, 512], F32))

    sL = Sem(nc, es, "sL")
    sPE = Sem(nc, es, "sPE")
    sV = Sem(nc, es, "sV")
    sA = Sem(nc, es, "sA")
    sST = Sem(nc, es, "sST")
    es.enter_context(nc.Block())
    sync, ten, vec, act, gp = nc.sync, nc.tensor, nc.vector, nc.scalar, nc.gpsimd

    sL.dinc(sync.dma_start(out=OB1F[:], in_=ob1f))
    sL.dinc(sync.dma_start(out=OBCF[:], in_=obcf))
    sV.inc(vec.memset(EPS1[:], 1e-5))
    for e in (sync, ten, vec, act):
        sL.wait(e)
        sV.wait(e)

    tsl = {}
    for s in range(4):
        c0 = 512 * s
        sV.wait(sync)
        sA.wait(sync)
        sPE.wait(sync)
        for c in range(8):
            sL.dinc(sync.dma_start(out=XA[:, c * 512:(c + 1) * 512], in_=xa[c * 128:(c + 1) * 128, c0:c0 + 512]))
            sL.dinc(sync.dma_start(out=XB[:, c * 512:(c + 1) * 512], in_=xb[c * 128:(c + 1) * 128, c0:c0 + 512]))
        sL.wait(vec)
        if "x1" in tsl:
            sST.wait(vec)
        sV.inc(vec.tensor_add(XA[:], XA[:], XB[:]))
        sV.wait(sync)
        for c in range(8):
            sST.dinc(sync.dma_start(out=x1own[c * 128:(c + 1) * 128, c0:c0 + 512], in_=XA[:, c * 512:(c + 1) * 512]))
        sync.nop()
        tsl["x1"] = sST.v
        sV.wait(act)
        sA.inc(act.square(SQ[:], XA[:]))
        sV.wait(ten)
        sA.wait(ten)
        for c in range(8):
            ten.matmul(pR[0:1, 0:512], OB1F[:], XA[:, c * 512:(c + 1) * 512], start=(c == 0), stop=(c == 7))
        for c in range(8):
            instr = ten.matmul(pQ[0:1, 0:512], OB1F[:], SQ[:, c * 512:(c + 1) * 512], start=(c == 0), stop=(c == 7))
        sPE.inc(instr)
        sPE.wait(vec)
        sV.inc(vec.tensor_scalar_mul(ROWS[0:1, 0:512], pR[0:1, 0:512], 1.0 / 1024.0))
        sV.wait(vec)
        sV.inc(vec.tensor_scalar_mul(ROWS[0:1, 512:1024], pQ[0:1, 0:512], 1.0 / 1024.0))
        sV.wait(vec)
        sV.inc(vec.tensor_mul(ROWS[0:1, 1024:1536], ROWS[0:1, 0:512], ROWS[0:1, 0:512]))
        sV.wait(vec)
        sV.inc(vec.tensor_sub(ROWS[0:1, 1024:1536], ROWS[0:1, 512:1024], ROWS[0:1, 1024:1536]))
        sV.wait(act)
        sA.inc(act.activation(out=ROWS[0:1, 1536:2048], in_=ROWS[0:1, 1024:1536], func=AF.Sqrt,
                              bias=EPS1[0:1, 0:1], scale=1.0))
        sA.wait(vec)
        sV.inc(vec.reciprocal(ROWS[0:1, 1024:1536], ROWS[0:1, 1536:2048]))
        sV.wait(ten)
        ten.matmul(pR[:, 0:512], OBCF[:], ROWS[0:1, 0:512], start=True, stop=True)
        instr = ten.matmul(pQ[:, 0:512], OBCF[:], ROWS[0:1, 1024:1536], start=True, stop=True)
        sPE.inc(instr)
        sPE.wait(vec)
        sV.inc(vec.tensor_copy(RSS[:, 0:512], pQ[:, 0:512]))
        for c in range(8):
            slot = c % 2
            tv = TO[:, slot * 512:(slot + 1) * 512]
            sV.wait(vec)
            if ("y", slot) in tsl:
                sST.wait(vec)
            sV.inc(vec.tensor_sub(tv, XA[:, c * 512:(c + 1) * 512], pR[:, 0:512]))
            sV.wait(vec)
            sV.inc(vec.tensor_mul(tv, tv, RSS[:, 0:512]))
            sV.wait(sync)
            sST.dinc(sync.dma_start(out=y0own[c * 128:(c + 1) * 128, c0:c0 + 512], in_=tv))
            sync.nop()
            tsl[("y", slot)] = sST.v

    sST.wait(sync)
    es.close()
    return nc


def entry_maps():
    return dict(ob1f=np.ones((128, 1), np.float32), obcf=np.ones((1, 128), np.float32))


def run_entry(progs, xa_full, xb_full):
    in_maps = []
    em = entry_maps()
    for c in range(NCORES):
        sl = slice(TOWN * c, TOWN * (c + 1))
        in_maps.append(dict(xa=np.ascontiguousarray(xa_full[:, sl]),
                            xb=np.ascontiguousarray(xb_full[:, sl]), **em))
    res = run_bass_kernel_spmd(progs["entry"], in_maps, core_ids=list(range(NCORES)))
    x1 = np.concatenate([np.asarray(r["x1own"]) for r in res.results], axis=1)
    y0 = np.concatenate([np.asarray(r["y0own"]) for r in res.results], axis=1)
    return x1, y0


def run_ffn_block(progs, x_full, o_full, inp, wcache):
    x1, y = run_entry(progs, x_full, o_full)
    for li in range(6):
        exitv = (li == 5)
        y, xn = run_ffn_layer(progs, y, li, inp, wcache, exitv=exitv, xr_full=x1 if exitv else None)
    return xn



TQ = 256
NQ = TOWN // TQ
NSER = 16 * (TQ + 1)


def build_mamba(phase_b, upto=9):
    from contextlib import ExitStack
    nc = bass.Bass("TRN2", target_bir_lowering=False, debug=False)
    es = ExitStack()
    T = lambda name, shape, dt: es.enter_context(nc.sbuf_tensor(name, shape, dt))

    if not phase_b:
        x2e = nc.dram_tensor("x2e", [DM, TOWN + 4], F32, kind="ExternalInput").ap()
        wiT = nc.dram_tensor("wiT", [DM, 4096], F32, kind="ExternalInput").ap()
        convw = nc.dram_tensor("convw", [128, 16 * 4], F32, kind="ExternalInput").ap()
        convb = nc.dram_tensor("convb", [128, 16], F32, kind="ExternalInput").ap()
        wxT = nc.dram_tensor("wxT", [2048, 33], F32, kind="ExternalInput").ap()
        dtw = nc.dram_tensor("dtw", [128, 16], F32, kind="ExternalInput").ap()
        dtb = nc.dram_tensor("dtb", [128, 16], F32, kind="ExternalInput").ap()
        alog = nc.dram_tensor("alog", [2048, 16], F32, kind="ExternalInput").ap()
        o_dt = nc.dram_tensor("o_dt", [2048, TOWN], F32, kind="ExternalOutput").ap()
        o_xc = nc.dram_tensor("o_xc", [2048, TOWN], BF16, kind="ExternalOutput").ap()
        o_z = nc.dram_tensor("o_z", [2048, TOWN], BF16, kind="ExternalOutput").ap()
        o_bc = nc.dram_tensor("o_bc", [32, TOWN], F32, kind="ExternalOutput").ap()
        o_hf = nc.dram_tensor("o_hf", [2048, 16], F32, kind="ExternalOutput").ap()
        o_sf = nc.dram_tensor("o_sf", [2048, 1], F32, kind="ExternalOutput").ap()
        dtin, xcin = o_dt, o_xc
    else:
        i_dt = nc.dram_tensor("i_dt", [2048, TOWN], F32, kind="ExternalInput").ap()
        i_xc = nc.dram_tensor("i_xc", [2048, TOWN], BF16, kind="ExternalInput").ap()
        i_z = nc.dram_tensor("i_z", [2048, TOWN], BF16, kind="ExternalInput").ap()
        i_bc = nc.dram_tensor("i_bc", [32, TOWN], F32, kind="ExternalInput").ap()
        i_hq = nc.dram_tensor("i_hq", [2048, 7 * 16], F32, kind="ExternalInput").ap()
        i_sq = nc.dram_tensor("i_sq", [2048, 7], F32, kind="ExternalInput").ap()
        alog = nc.dram_tensor("alog", [2048, 16], F32, kind="ExternalInput").ap()
        dvec = nc.dram_tensor("dvec", [2048, 1], F32, kind="ExternalInput").ap()
        x2own = nc.dram_tensor("x2own", [DM, TOWN], F32, kind="ExternalInput").ap()
        wopT = nc.dram_tensor("wopT", [2048, DM], F32, kind="ExternalInput").ap()
        x3own = nc.dram_tensor("x3own", [DM, TOWN], F32, kind="ExternalOutput").ap()
        dtin, xcin = i_dt, i_xc
    ob1f = nc.dram_tensor("ob1f", [128, 1], F32, kind="ExternalInput").ap()
    obcf = nc.dram_tensor("obcf", [1, 128], F32, kind="ExternalInput").ap()

    sL = Sem(nc, es, "sL")
    sG = Sem(nc, es, "sG")
    sPE = Sem(nc, es, "sPE")
    sV = Sem(nc, es, "sV")
    sA = Sem(nc, es, "sA")
    sST = Sem(nc, es, "sST")

    ABUF = T("ABUF", [128, NSER], F32)
    BBUF = T("BBUF", [128, NSER], BF16)
    HBUF = T("HBUF", [128, NSER], F32)
    BRE = T("BRE", [128, 16 * TQ], BF16)
    UB = T("UB", [128, TQ], F32)
    DTQ = T("DTQ", [128, TQ], F32)
    XCQ = T("XCQ", [128, TQ], BF16)
    AC = T("AC", [128, 16], F32)
    CARRY = T("CARRY", [128, 16 * 16], F32)
    SFA = T("SFA", [128, 16], F32)
    BCROW = T("BCROW", [1, 16 * TQ], F32)
    OB1F = T("OB1F", [128, 1], F32)
    OBCF = T("OBCF", [1, 128], F32)
    EPS1 = T("EPS1", [1, 1], F32)
    ONE1F = T("ONE1F", [128, 1], F32)

    if not phase_b:
        WI1 = T("WI1", [128, 2 * 1024], BF16)
        WX = T("WX", [128, 16 * 33], BF16)
        XLN = T("XLN", [128, 8 * 2052], BF16)
        CW = T("CW", [128, 16 * 4], F32)
        CB = T("CB", [128, 16], F32)
        DTW = T("DTW", [128, 16], F32)
        DTB = T("DTB", [128, 16], F32)
        XSL = T("XSL", [128, 8 * 520], F32)
        SQF = T("SQF", [128, 8 * 520], F32)
        ROWS = T("ROWS", [1, 2080], F32)
        RSS = T("RSS", [128, 520], F32)
        XT32 = T("XT32", [128, 2052], F32)
        DTR = T("DTR", [128, 512], F32)
        TB = T("TB", [128, 2048], F32)
        BC33 = T("BC33", [33, 512], F32)
        XCF = T("XCF", [128, 2048], BF16)
    else:
        ZQ = T("ZQ", [128, TQ], BF16)
        CRE = T("CRE", [128, 16 * TQ], BF16)
        CCROW = T("CCROW", [1, 16 * TQ], F32)
        HQ = T("HQ", [128, 7 * 16], F32)
        SQC = T("SQC", [128, 7], F32)
        TM = T("TM", [128, 16], F32)
        YG = T("YG", [128, 16 * 2048], BF16)
        PB = T("PB", [128, 16 * TQ], BF16)
        YQ = T("YQ", [128, TQ], F32)
        DV = T("DV", [128, 16], F32)
        X2S = T("X2S", [128, 512], F32)
        TO = T("TO", [128, 2 * 512], F32)
        WOPB = T("WOPB", [128, 2 * 128], BF16)

    pR = es.enter_context(nc.psum_tensor("pR", [128, 512], F32))
    pQ = es.enter_context(nc.psum_tensor("pQ", [128, 512], F32))
    pA = es.enter_context(nc.psum_tensor("pA", [128, 512], F32))
    pB = es.enter_context(nc.psum_tensor("pB", [128, 512], F32))

    es.enter_context(nc.Block())
    sync, ten, vec, act, gp = nc.sync, nc.tensor, nc.vector, nc.scalar, nc.gpsimd

    def wv(*engs):
        for e in engs: sV.wait(e)

    def wpe(*engs):
        for e in engs: sPE.wait(e)

    def wa(*engs):
        for e in engs: sA.wait(e)

    tsl = {}
    sL.dinc(sync.dma_start(out=OB1F[:], in_=ob1f))
    sL.dinc(sync.dma_start(out=OBCF[:], in_=obcf))
    sV.inc(vec.memset(EPS1[:], 1e-5))
    sV.inc(vec.memset(ONE1F[:], 1.0))
    sV.inc(vec.memset(CARRY[:], 0.0))
    sV.inc(vec.memset(SFA[:], 0.0))
    sV.inc(vec.memset(ABUF[:], 0.0))
    sV.inc(vec.memset(BBUF[:], 0.0))
    sV.inc(vec.memset(BRE[:], 0.0))
    sV.inc(vec.memset(BCROW[:], 0.0))

    if not phase_b:
        sG.dinc(gp.dma_start(out=WX[:].rearrange("p (c n) -> p c n", c=16),
                             in_=wxT.rearrange("(c p) n -> p c n", p=128)))
        sL.dinc(sync.dma_start(out=CW[:], in_=convw))
        sL.dinc(sync.dma_start(out=CB[:], in_=convb))
        sL.dinc(sync.dma_start(out=DTW[:], in_=dtw))
        sL.dinc(sync.dma_start(out=DTB[:], in_=dtb))
        for e in (sync, ten, vec, act, gp):
            sL.wait(e); sG.wait(e); sV.wait(e)

        sV.inc(vec.memset(XSL[:], 0.0))
        sV.inc(vec.memset(XLN[:], 0.0))
        sV.inc(vec.memset(XT32[:], 0.0))
        wv(vec, act, ten, sync)
        for (c0, w) in [(0, 512), (512, 512), (1024, 512), (1536, 512), (2048, 4)]:
            wv(sync); wa(sync); wpe(sync)
            for c in range(8):
                sL.dinc(sync.dma_start(out=XSL[:, c * 520:c * 520 + w],
                                       in_=x2e[c * 128:(c + 1) * 128, c0:c0 + w]))
            sL.wait(act); sL.wait(ten); sL.wait(vec)
            sA.inc(act.square(SQF[:], XSL[:]))
            wa(ten); wv(ten)
            for c in range(8):
                ten.matmul(pR[0:1, 0:w], OB1F[:], XSL[:, c * 520:c * 520 + w], start=(c == 0), stop=(c == 7))
            for c in range(8):
                instr = ten.matmul(pQ[0:1, 0:w], OB1F[:], SQF[:, c * 520:c * 520 + w], start=(c == 0), stop=(c == 7))
            sPE.inc(instr)
            wpe(vec)
            sV.inc(vec.tensor_scalar_mul(ROWS[0:1, 0:w], pR[0:1, 0:w], 1.0 / 1024.0))
            wv(vec)
            sV.inc(vec.tensor_scalar_mul(ROWS[0:1, 520:520 + w], pQ[0:1, 0:w], 1.0 / 1024.0))
            wv(vec)
            sV.inc(vec.tensor_mul(ROWS[0:1, 1040:1040 + w], ROWS[0:1, 0:w], ROWS[0:1, 0:w]))
            wv(vec)
            sV.inc(vec.tensor_sub(ROWS[0:1, 1040:1040 + w], ROWS[0:1, 520:520 + w], ROWS[0:1, 1040:1040 + w]))
            wv(act)
            sA.inc(act.activation(out=ROWS[0:1, 1560:1560 + w], in_=ROWS[0:1, 1040:1040 + w],
                                  func=AF.Sqrt, bias=EPS1[0:1, 0:1], scale=1.0))
            wa(vec)
            sV.inc(vec.reciprocal(ROWS[0:1, 1040:1040 + w], ROWS[0:1, 1560:1560 + w]))
            wv(ten)
            ten.matmul(pR[:, 0:w], OBCF[:], ROWS[0:1, 0:w], start=True, stop=True)
            instr = ten.matmul(pQ[:, 0:w], OBCF[:], ROWS[0:1, 1040:1040 + w], start=True, stop=True)
            sPE.inc(instr)
            wpe(vec)
            sV.inc(vec.tensor_copy(RSS[:, 0:w], pQ[:, 0:w]))
            for c in range(8):
                wv(vec)
                sV.inc(vec.tensor_sub(XT32[:, 0:w], XSL[:, c * 520:c * 520 + w], pR[:, 0:w]))
                wv(vec)
                sV.inc(vec.tensor_mul(XLN[:, c * 2052 + c0:c * 2052 + c0 + w], XT32[:, 0:w], RSS[:, 0:w]))

        for oc in (range(32) if upto >= 2 else []):
            wpe(gp)
            for kc in range(8):
                sG.dinc(gp.dma_start(out=WI1[:, (oc % 2) * 1024 + kc * 128:(oc % 2) * 1024 + kc * 128 + 128],
                                     in_=wiT[kc * 128:(kc + 1) * 128, oc * 128:(oc + 1) * 128]))
            sG.wait(ten)
            for ts2 in range(5):
                a0 = 512 * ts2
                w = min(512, 2052 - a0)
                pp = (pA, pB)[ts2 % 2]
                wv(ten)
                instr = None
                for kc in range(8):
                    instr = ten.matmul(pp[:, 0:w],
                                       WI1[:, (oc % 2) * 1024 + kc * 128:(oc % 2) * 1024 + kc * 128 + 128],
                                       XLN[:, kc * 2052 + a0:kc * 2052 + a0 + w],
                                       start=(kc == 0), stop=(kc == 7))
                sPE.inc(instr)
                wpe(vec)
                wa(vec)
                sV.inc(vec.tensor_copy(XT32[:, a0:a0 + w], pp[:, 0:w]))
            cc = oc % 16
            if oc < 16:
                wv(vec)
                wa(vec)
                sV.inc(vec.tensor_single_scalar(TB[:, 0:2048], XT32[:, 1:1 + 2048], CW[:, 4 * cc:4 * cc + 1], A.mult))
                for jj in range(1, 4):
                    wv(vec)
                    sV.inc(vec.scalar_tensor_tensor(TB[:, 0:2048], XT32[:, jj + 1:jj + 1 + 2048],
                                                    CW[:, 4 * cc + jj:4 * cc + jj + 1], TB[:, 0:2048], A.mult, A.add))
                wv(act)
                if "xcf" in tsl:
                    sST.wait(act)
                sA.inc(act.activation(out=XCF[:, 0:2048], in_=TB[:, 0:2048], func=AF.Silu,
                                      bias=CB[:, cc:cc + 1], scale=1.0))
                wa(sync)
                sST.dinc(sync.dma_start(out=o_xc[cc * 128:(cc + 1) * 128, :], in_=XCF[:, 0:2048]))
                sync.nop()
                tsl["xcf"] = sST.v
            else:
                wv(act)
                if "xcf" in tsl:
                    sST.wait(act)
                sA.inc(act.activation(out=XCF[:, 0:2048], in_=XT32[:, 4:4 + 2048], func=AF.Identity, scale=1.0))
                wa(sync)
                sST.dinc(sync.dma_start(out=o_z[cc * 128:(cc + 1) * 128, :], in_=XCF[:, 0:2048]))
                sync.nop()
                tsl["xcf"] = sST.v

        sST.wait(sync)
        for ts2 in (range(4) if upto >= 3 else []):
            a0 = 512 * ts2
            wpe(sync); wv(sync); wa(sync)
            instr = None
            for kc in range(16):
                sL.dinc(sync.dma_start(out=XCF[:, 0:512], in_=o_xc[kc * 128:(kc + 1) * 128, a0:a0 + 512]))
                sL.wait(ten)
                instr = ten.matmul(pR[0:33, 0:512], WX[:, kc * 33:(kc + 1) * 33], XCF[:, 0:512],
                                   start=(kc == 0), stop=(kc == 15))
                sPE.inc(instr)
                if kc < 15:
                    wpe(sync)
            wpe(vec)
            sV.inc(vec.tensor_copy(BC33[:, 0:512], pR[0:33, 0:512]))
            wv(sync)
            sST.dinc(sync.dma_start(out=o_bc[:, a0:a0 + 512], in_=BC33[1:33, 0:512]))
            wv(ten)
            instr = ten.matmul(pQ[:, 0:512], OBCF[:], BC33[0:1, 0:512], start=True, stop=True)
            sPE.inc(instr)
            wpe(vec)
            for cc in range(16):
                wv(vec)
                if "dt" in tsl:
                    sST.wait(vec)
                sV.inc(vec.tensor_scalar(DTR[:, 0:512], pQ[:, 0:512], DTW[:, cc:cc + 1], DTB[:, cc:cc + 1], A.mult, A.add))
                wv(act)
                sA.inc(act.activation(out=TB[:, 0:512], in_=DTR[:, 0:512], func=AF.Abs, scale=1.0))
                wa(act)
                sA.inc(act.activation(out=TB[:, 512:1024], in_=TB[:, 0:512], func=AF.Exp, scale=-1.0))
                wa(act)
                sA.inc(act.activation(out=TB[:, 1024:1536], in_=TB[:, 512:1024], func=AF.Ln, bias=ONE1F[:, 0:1], scale=1.0))
                wa(vec)
                sV.inc(vec.tensor_single_scalar(TB[:, 1536:2048], DTR[:, 0:512], 0.0, A.max))
                wv(vec)
                wa(vec)
                sV.inc(vec.tensor_add(DTR[:, 0:512], TB[:, 1536:2048], TB[:, 1024:1536]))
                wv(sync)
                sST.dinc(sync.dma_start(out=o_dt[cc * 128:(cc + 1) * 128, a0:a0 + 512], in_=DTR[:, 0:512]))
                sync.nop()
                tsl["dt"] = sST.v
        sST.wait(sync)

    if phase_b:
        for e in (sync, ten, vec, act, gp):
            sL.wait(e); sV.wait(e)
        for cc in range(16):
            wv(sync); wa(sync)
            sL.dinc(sync.dma_start(out=AC[:], in_=alog[cc * 128:(cc + 1) * 128, :]))
            sL.dinc(sync.dma_start(out=HQ[:], in_=i_hq[cc * 128:(cc + 1) * 128, :]))
            sL.dinc(sync.dma_start(out=SQC[:], in_=i_sq[cc * 128:(cc + 1) * 128, :]))
            sL.wait(act); sL.wait(vec)
            sA.inc(act.activation(out=AC[:], in_=AC[:], func=AF.Exp, scale=1.0))
            wa(vec)
            sV.inc(vec.tensor_scalar_mul(AC[:], AC[:], -1.0))
            for q in range(7):
                wv(act)
                sA.inc(act.activation(out=TM[:], in_=AC[:], func=AF.Exp, scale=SQC[:, q:q + 1]))
                wa(vec)
                if q == 0:
                    sV.inc(vec.tensor_copy(CARRY[:, cc * 16:(cc + 1) * 16], HQ[:, 0:16]))
                else:
                    sV.inc(vec.tensor_mul(CARRY[:, cc * 16:(cc + 1) * 16], CARRY[:, cc * 16:(cc + 1) * 16], TM[:]))
                    wv(vec)
                    sV.inc(vec.tensor_add(CARRY[:, cc * 16:(cc + 1) * 16], CARRY[:, cc * 16:(cc + 1) * 16], HQ[:, q * 16:(q + 1) * 16]))
                wv(act, vec)

    for qq in (range(NQ) if upto >= 4 else []):
        t0 = TQ * qq
        wv(sync)
        bsl = o_bc if not phase_b else i_bc
        import os as _os
        if _os.environ.get("SKIP_BCDMA"):
            sV.inc(vec.memset(BCROW[:], 0.5))
        else:
            sL.dinc(sync.dma_start(out=BCROW[0:1, :].rearrange("p (a b) -> p a b", a=16),
                                   in_=bsl[0:16, t0:t0 + TQ].rearrange("a b -> () a b")))
        if phase_b:
            sL.dinc(sync.dma_start(out=CCROW[0:1, :].rearrange("p (a b) -> p a b", a=16),
                                   in_=bsl[16:32, t0:t0 + TQ].rearrange("a b -> () a b")))
        sL.wait(ten)
        import os as _os2
        for nn in (range(16) if not _os2.environ.get("SKIP_NN") else []):
            wv(ten)
            instr = ten.matmul(pA[:, 0:TQ], OBCF[:], BCROW[0:1, nn * TQ:(nn + 1) * TQ], start=True, stop=True)
            sPE.inc(instr)
            wpe(vec)
            sV.inc(vec.tensor_copy(BRE[:, nn * TQ:(nn + 1) * TQ], pA[:, 0:TQ]))
            if phase_b:
                instr = ten.matmul(pB[:, 0:TQ], OBCF[:], CCROW[0:1, nn * TQ:(nn + 1) * TQ], start=True, stop=True)
                sPE.inc(instr)
                wpe(vec)
                sV.inc(vec.tensor_copy(CRE[:, nn * TQ:(nn + 1) * TQ], pB[:, 0:TQ]))
        for cc in (range(16) if upto >= 5 else []):
            wv(sync); wa(sync)
            sL.dinc(sync.dma_start(out=AC[:], in_=alog[cc * 128:(cc + 1) * 128, :]))
            sL.dinc(sync.dma_start(out=DTQ[:], in_=dtin[cc * 128:(cc + 1) * 128, t0:t0 + TQ]))
            sL.dinc(sync.dma_start(out=XCQ[:], in_=xcin[cc * 128:(cc + 1) * 128, t0:t0 + TQ]))
            if phase_b:
                sL.dinc(sync.dma_start(out=ZQ[:], in_=i_z[cc * 128:(cc + 1) * 128, t0:t0 + TQ]))
                sL.dinc(sync.dma_start(out=DV[:, 0:1], in_=dvec[cc * 128:(cc + 1) * 128, :]))
            sL.wait(act); sL.wait(vec)
            sA.inc(act.activation(out=AC[:], in_=AC[:], func=AF.Exp, scale=1.0))
            wa(vec)
            sV.inc(vec.tensor_scalar_mul(AC[:], AC[:], -1.0))
            wv(vec)
            sV.inc(vec.tensor_mul(UB[:], DTQ[:], XCQ[:]))
            wv(act, vec)
            sV.inc(vec.memset(ABUF[:].rearrange("p (a b) -> p a b", a=16)[:, :, 0:1], 0.0))
            for nn in range(16):
                wv(act)
                sA.inc(act.activation(out=ABUF[:, nn * (TQ + 1) + 1:(nn + 1) * (TQ + 1)],
                                      in_=DTQ[:], func=AF.Exp, scale=AC[:, nn:nn + 1]))
            if upto < 6:
                continue
            wa(vec); wv(vec)
            sV.inc(vec.tensor_tensor(BBUF[:].rearrange("p (a b) -> p a b", a=16)[:, :, 1:TQ + 1],
                                     BRE[:].rearrange("p (a b) -> p a b", a=16),
                                     UB[:].rearrange("p b -> p () b").to_broadcast([128, 16, TQ]),
                                     A.mult))
            wv(vec)
            if qq == 0 and not phase_b:
                sV.inc(vec.memset(BBUF[:].rearrange("p (a b) -> p a b", a=16)[:, :, 0:1], 0.0))
            else:
                sV.inc(vec.tensor_copy(BBUF[:].rearrange("p (a b) -> p a b", a=16)[:, :, 0:1],
                                       CARRY[:, cc * 16:(cc + 1) * 16].rearrange("p (a b) -> p a b", a=16)))
            if upto < 7:
                continue
            wv(vec); wa(vec)
            sV.inc(vec.tensor_tensor_scan(HBUF[:], ABUF[:], BBUF[:], 0.0, A.mult, A.add))
            wv(vec)
            sV.inc(vec.tensor_copy(CARRY[:, cc * 16:(cc + 1) * 16],
                                   HBUF[:].rearrange("p (a b) -> p a b", a=16)[:, :, TQ:TQ + 1]))
            if not phase_b:
                wv(vec)
                sV.inc(vec.reduce_sum(UB[:, 0:1], DTQ[:], axis=AX.X))
                wv(vec)
                sV.inc(vec.tensor_add(SFA[:, cc:cc + 1], SFA[:, cc:cc + 1], UB[:, 0:1]))
            else:
                wv(vec)
                sV.inc(vec.tensor_tensor(PB[:].rearrange("p (a b) -> p a b", a=16),
                                         HBUF[:].rearrange("p (a b) -> p a b", a=16)[:, :, 1:TQ + 1],
                                         CRE[:].rearrange("p (a b) -> p a b", a=16),
                                         A.mult))
                wv(vec)
                sV.inc(vec.reduce_sum(YQ[:], PB[:].rearrange("p (a b) -> p b a", a=16), axis=AX.X))
                wv(vec)
                sV.inc(vec.scalar_tensor_tensor(YQ[:], XCQ[:], DV[:, 0:1], YQ[:], A.mult, A.add))
                wv(act)
                sA.inc(act.activation(out=PB[:, 0:TQ], in_=ZQ[:], func=AF.Silu, scale=1.0))
                wa(vec); wv(vec)
                sV.inc(vec.tensor_tensor(YG[:, cc * 2048 + t0:cc * 2048 + t0 + TQ], YQ[:], PB[:, 0:TQ], A.mult))

    if not phase_b:
        wv(sync)
        for cc in range(16):
            sST.dinc(sync.dma_start(out=o_hf[cc * 128:(cc + 1) * 128, :], in_=CARRY[:, cc * 16:(cc + 1) * 16]))
            sST.dinc(sync.dma_start(out=o_sf[cc * 128:(cc + 1) * 128, :], in_=SFA[:, cc:cc + 1]))
    else:
        for ts2 in range(4):
            a0 = 512 * ts2
            for oc in range(8):
                pp = (pA, pB)[oc % 2]
                wv(ten)
                instr = None
                for kc in range(16):
                    wpe(gp)
                    sG.dinc(gp.dma_start(out=WOPB[:, (kc % 2) * 128:(kc % 2) * 128 + 128],
                                         in_=wopT[kc * 128:(kc + 1) * 128, oc * 128:(oc + 1) * 128]))
                    sG.wait(ten)
                    instr = ten.matmul(pp[:, 0:512], WOPB[:, (kc % 2) * 128:(kc % 2) * 128 + 128],
                                       YG[:, kc * 2048 + a0:kc * 2048 + a0 + 512],
                                       start=(kc == 0), stop=(kc == 15))
                    sPE.inc(instr)
                wpe(vec)
                sL.dinc(sync.dma_start(out=X2S[:], in_=x2own[oc * 128:(oc + 1) * 128, a0:a0 + 512]))
                sL.wait(vec)
                slot = oc % 2
                if ("x3", slot) in tsl:
                    sST.wait(vec)
                sV.inc(vec.scalar_tensor_tensor(TO[:, slot * 512:(slot + 1) * 512], pp[:, 0:512], 1.0, X2S[:], A.mult, A.add))
                wv(sync)
                sST.dinc(sync.dma_start(out=x3own[oc * 128:(oc + 1) * 128, a0:a0 + 512], in_=TO[:, slot * 512:(slot + 1) * 512]))
                sync.nop()
                tsl[("x3", slot)] = sST.v

    sST.wait(sync)
    sST.wait(gp)
    es.close()
    return nc


def mamba_maps_a(inp):
    return dict(
        wiT=np.ascontiguousarray(inp["L0_attn_b_in_proj_w"].T.astype(np.float32)),
        convw=np.ascontiguousarray(inp["L0_attn_b_conv_w"][:, 0, :].reshape(16, 128, 4).transpose(1, 0, 2).reshape(128, 64).astype(np.float32)),
        convb=np.ascontiguousarray(inp["L0_attn_b_conv_b"].reshape(16, 128).T.astype(np.float32)),
        wxT=np.ascontiguousarray(inp["L0_attn_b_x_proj_w"].T.astype(np.float32)),
        dtw=np.ascontiguousarray(inp["L0_attn_b_dt_proj_w"][:, 0].reshape(16, 128).T.astype(np.float32)),
        dtb=np.ascontiguousarray(inp["L0_attn_b_dt_proj_b"].reshape(16, 128).T.astype(np.float32)),
        alog=np.ascontiguousarray(inp["L0_attn_b_A_log"].astype(np.float32)),
        ob1f=np.ones((128, 1), np.float32), obcf=np.ones((1, 128), np.float32),
    )


def run_mamba(progs, x2_full, inp):
    ma = mamba_maps_a(inp)
    in_maps = []
    for c in range(NCORES):
        b0 = TOWN * c
        x2e = np.zeros((DM, TOWN + 4), np.float32)
        lo = max(0, b0 - 4)
        x2e[:, 4 - (b0 - lo):] = x2_full[:, lo:b0 + TOWN]
        in_maps.append(dict(x2e=x2e, **ma))
    resA = run_bass_kernel_spmd(progs["mambaA"], in_maps, core_ids=list(range(NCORES)))
    ra = [dict(r) for r in (dict(x) if isinstance(x, dict) else x for x in resA.results)] if False else resA.results
    hf = [np.asarray(r["o_hf"]) for r in ra]
    sf = [np.asarray(r["o_sf"]) for r in ra]
    in_maps_b = []
    mb = dict(alog=ma["alog"],
              dvec=np.ascontiguousarray(inp["L0_attn_b_D"].astype(np.float32)[:, None]),
              wopT=np.ascontiguousarray(inp["L0_attn_b_out_proj_w"].T.astype(np.float32)),
              ob1f=ma["ob1f"], obcf=ma["obcf"])
    for c in range(NCORES):
        b0 = TOWN * c
        hq = np.zeros((2048, 7 * 16), np.float32)
        sq = np.zeros((2048, 7), np.float32)
        for i, q in enumerate(range(7 - c, 7)):
            hq[:, q * 16:(q + 1) * 16] = hf[i + (c - c)] if False else hf[c - (7 - q)]
            sq[:, q] = sf[c - (7 - q)][:, 0]
        in_maps_b.append(dict(
            i_dt=np.asarray(ra[c]["o_dt"]), i_xc=np.asarray(ra[c]["o_xc"]),
            i_z=np.asarray(ra[c]["o_z"]), i_bc=np.asarray(ra[c]["o_bc"]),
            i_hq=hq, i_sq=sq,
            x2own=np.ascontiguousarray(x2_full[:, b0:b0 + TOWN]), **mb))
    resB = run_bass_kernel_spmd(progs["mambaB"], in_maps_b, core_ids=list(range(NCORES)))
    return np.concatenate([np.asarray(r["x3own"]) for r in resB.results], axis=1)

def build_aprep():
    from contextlib import ExitStack
    nc = bass.Bass("TRN2", target_bir_lowering=False, debug=False)
    es = ExitStack()
    T = lambda name, shape, dt: es.enter_context(nc.sbuf_tensor(name, shape, dt))
    xin = nc.dram_tensor("xin", [DM, TOWN], F32, kind="ExternalInput").ap()
    wdkv = nc.dram_tensor("wdkv", [DM, 256], F32, kind="ExternalInput").ap()
    wuk = nc.dram_tensor("wuk", [256, 512], F32, kind="ExternalInput").ap()
    wuv = nc.dram_tensor("wuv", [256, 512], F32, kind="ExternalInput").ap()
    wkr = nc.dram_tensor("wkr", [DM, 32], F32, kind="ExternalInput").ap()
    wik = nc.dram_tensor("wik", [DM, 64], F32, kind="ExternalInput").ap()
    cl_ = nc.dram_tensor("cl_", [32, TOWN], BF16, kind="ExternalInput").ap()
    sl_ = nc.dram_tensor("sl_", [32, TOWN], BF16, kind="ExternalInput").ap()
    ci_ = nc.dram_tensor("ci_", [32, TOWN], BF16, kind="ExternalInput").ap()
    si_ = nc.dram_tensor("si_", [32, TOWN], BF16, kind="ExternalInput").ap()
    r32 = nc.dram_tensor("r32", [32, 32], BF16, kind="ExternalInput").ap()
    ob1f = nc.dram_tensor("ob1f", [128, 1], F32, kind="ExternalInput").ap()
    obcf = nc.dram_tensor("obcf", [1, 128], F32, kind="ExternalInput").ap()
    wiq_ = nc.dram_tensor("wiq_", [DM, 256], F32, kind="ExternalInput").ap()
    o_ki = nc.dram_tensor("o_ki", [64, TOWN], F32, kind="ExternalOutput").ap()
    o_qi = nc.dram_tensor("o_qi", [256, TOWN], F32, kind="ExternalOutput").ap()
    o_k = nc.dram_tensor("o_k", [512, TOWN], BF16, kind="ExternalOutput").ap()
    o_v = nc.dram_tensor("o_v", [512, TOWN], BF16, kind="ExternalOutput").ap()
    o_mu = nc.dram_tensor("o_mu", [32, TOWN], BF16, kind="ExternalOutput").ap()
    o_krl = nc.dram_tensor("o_krl", [32, TOWN], BF16, kind="ExternalOutput").ap()
    o_kri = nc.dram_tensor("o_kri", [32, TOWN], BF16, kind="ExternalOutput").ap()
    o_lnx = nc.dram_tensor("o_lnx", [DM, TOWN], BF16, kind="ExternalOutput").ap()

    XA = T("XA", [128, 8 * 512], F32)
    SQ = T("SQ", [128, 8 * 512], F32)
    XLN = T("XLN", [128, 8 * 512], F32)
    WIKF = T("WIKF", [128, 8 * 64], F32)
    WIQF = T("WIQF", [128, 8 * 256], F32)
    KIF = T("KIF", [128, 512], F32)
    ROWS = T("ROWS", [1, 2048], F32)
    RSS = T("RSS", [128, 512], F32)
    LNB = T("LNB", [128, 8 * TOWN], BF16)
    WD = T("WD", [128, 8 * 256], BF16)
    WUK = T("WUK", [128, 2 * 512], BF16)
    WUV = T("WUV", [128, 2 * 512], BF16)
    WKR = T("WKR", [128, 8 * 32], BF16)
    CKV = T("CKV", [128, 2 * TOWN], BF16)
    KO = T("KO", [128, 4 * 512], BF16)
    MU = T("MU", [32, TOWN], BF16)
    MUS = T("MUS", [32, 3 * TOWN], BF16)
    R32 = T("R32", [32, 32], BF16)
    CL = T("CL", [32, TOWN], BF16)
    SL = T("SL", [32, TOWN], BF16)
    CI = T("CI", [32, TOWN], BF16)
    SI = T("SI", [32, TOWN], BF16)
    OB1F = T("OB1F", [128, 1], F32)
    OBCF = T("OBCF", [1, 128], F32)
    EPS1 = T("EPS1", [1, 1], F32)
    TO = T("TO", [128, 512], F32)

    pR = es.enter_context(nc.psum_tensor("pR", [128, 512], F32))
    pQ = es.enter_context(nc.psum_tensor("pQ", [128, 512], F32))
    pA = es.enter_context(nc.psum_tensor("pA", [128, 512], F32))
    pB = es.enter_context(nc.psum_tensor("pB", [128, 512], F32))

    sL = Sem(nc, es, "sL")
    sG = Sem(nc, es, "sG")
    sPE = Sem(nc, es, "sPE")
    sV = Sem(nc, es, "sV")
    sA = Sem(nc, es, "sA")
    sST = Sem(nc, es, "sST")
    es.enter_context(nc.Block())
    sync, ten, vec, act, gp = nc.sync, nc.tensor, nc.vector, nc.scalar, nc.gpsimd

    def wv(*e):
        for x in e: sV.wait(x)

    def wpe(*e):
        for x in e: sPE.wait(x)

    def wa(*e):
        for x in e: sA.wait(x)

    sL.dinc(sync.dma_start(out=OB1F[:], in_=ob1f))
    sL.dinc(sync.dma_start(out=OBCF[:], in_=obcf))
    sL.dinc(sync.dma_start(out=R32[:], in_=r32))
    sL.dinc(sync.dma_start(out=CL[:], in_=cl_))
    sL.dinc(sync.dma_start(out=SL[:], in_=sl_))
    sL.dinc(sync.dma_start(out=CI[:], in_=ci_))
    sL.dinc(sync.dma_start(out=SI[:], in_=si_))
    sV.inc(vec.memset(EPS1[:], 1e-5))
    sG.dinc(gp.dma_start(out=WD[:].rearrange("p (c n) -> p c n", c=8), in_=wdkv.rearrange("(c p) n -> p c n", p=128)))
    sG.dinc(gp.dma_start(out=WUK[:].rearrange("p (c n) -> p c n", c=2), in_=wuk.rearrange("(c p) n -> p c n", p=128)))
    sG.dinc(gp.dma_start(out=WUV[:].rearrange("p (c n) -> p c n", c=2), in_=wuv.rearrange("(c p) n -> p c n", p=128)))
    sG.dinc(gp.dma_start(out=WKR[:].rearrange("p (c n) -> p c n", c=8), in_=wkr.rearrange("(c p) n -> p c n", p=128)))
    sG.dinc(gp.dma_start(out=WIKF[:].rearrange("p (c n) -> p c n", c=8), in_=wik.rearrange("(c p) n -> p c n", p=128)))
    sG.dinc(gp.dma_start(out=WIQF[:].rearrange("p (c n) -> p c n", c=8), in_=wiq_.rearrange("(c p) n -> p c n", p=128)))
    for e in (sync, ten, vec, act, gp):
        sL.wait(e); sG.wait(e); sV.wait(e)

    tsl = {}
    for s in range(4):
        c0 = 512 * s
        wv(sync); wa(sync); wpe(sync)
        for c in range(8):
            sL.dinc(sync.dma_start(out=XA[:, c * 512:(c + 1) * 512], in_=xin[c * 128:(c + 1) * 128, c0:c0 + 512]))
        sL.wait(act); sL.wait(ten); sL.wait(vec)
        sA.inc(act.square(SQ[:], XA[:]))
        wa(ten); wv(ten)
        for c in range(8):
            ten.matmul(pR[0:1, 0:512], OB1F[:], XA[:, c * 512:(c + 1) * 512], start=(c == 0), stop=(c == 7))
        for c in range(8):
            instr = ten.matmul(pQ[0:1, 0:512], OB1F[:], SQ[:, c * 512:(c + 1) * 512], start=(c == 0), stop=(c == 7))
        sPE.inc(instr)
        wpe(vec)
        sV.inc(vec.tensor_scalar_mul(ROWS[0:1, 0:512], pR[0:1, 0:512], 1.0 / 1024.0))
        wv(vec)
        sV.inc(vec.tensor_scalar_mul(ROWS[0:1, 512:1024], pQ[0:1, 0:512], 1.0 / 1024.0))
        wv(vec)
        sV.inc(vec.tensor_mul(ROWS[0:1, 1024:1536], ROWS[0:1, 0:512], ROWS[0:1, 0:512]))
        wv(vec)
        sV.inc(vec.tensor_sub(ROWS[0:1, 1024:1536], ROWS[0:1, 512:1024], ROWS[0:1, 1024:1536]))
        wv(act)
        sA.inc(act.activation(out=ROWS[0:1, 1536:2048], in_=ROWS[0:1, 1024:1536], func=AF.Sqrt,
                              bias=EPS1[0:1, 0:1], scale=1.0))
        wa(vec)
        sV.inc(vec.reciprocal(ROWS[0:1, 1024:1536], ROWS[0:1, 1536:2048]))
        wv(ten)
        ten.matmul(pR[:, 0:512], OBCF[:], ROWS[0:1, 0:512], start=True, stop=True)
        instr = ten.matmul(pQ[:, 0:512], OBCF[:], ROWS[0:1, 1024:1536], start=True, stop=True)
        sPE.inc(instr)
        wpe(vec)
        sV.inc(vec.tensor_copy(RSS[:, 0:512], pQ[:, 0:512]))
        for c in range(8):
            wv(vec)
            sV.inc(vec.tensor_sub(TO[:, 0:512], XA[:, c * 512:(c + 1) * 512], pR[:, 0:512]))
            wv(vec)
            sV.inc(vec.tensor_mul(XLN[:, c * 512:(c + 1) * 512], TO[:, 0:512], RSS[:, 0:512]))
            wv(vec)
            sV.inc(vec.tensor_copy(LNB[:, c * TOWN + c0:c * TOWN + c0 + 512], XLN[:, c * 512:(c + 1) * 512]))
        wv(ten)
        for kc in range(8):
            instr = ten.matmul(pQ[0:64, 0:512], WIKF[:, kc * 64:(kc + 1) * 64],
                               XLN[:, kc * 512:(kc + 1) * 512], start=(kc == 0), stop=(kc == 7))
        sPE.inc(instr)
        wpe(vec)
        sST.wait(vec)
        sV.inc(vec.tensor_copy(KIF[0:64, 0:512], pQ[0:64, 0:512]))
        wv(sync)
        sST.dinc(sync.dma_start(out=o_ki[:, c0:c0 + 512], in_=KIF[0:64, 0:512]))
        for oc in range(2):
            pp = (pA, pB)[oc]
            wv(ten)
            for kc in range(8):
                instr = ten.matmul(pp[:, 0:512], WIQF[:, kc * 256 + oc * 128:kc * 256 + oc * 128 + 128],
                                   XLN[:, kc * 512:(kc + 1) * 512], start=(kc == 0), stop=(kc == 7))
            sPE.inc(instr)
            wpe(vec)
            sST.wait(vec)
            sV.inc(vec.tensor_copy(KIF[:, 0:512], pp[:, 0:512]))
            wv(sync)
            sST.dinc(sync.dma_start(out=o_qi[oc * 128:(oc + 1) * 128, c0:c0 + 512], in_=KIF[:, 0:512]))
    wv(sync)
    for c in range(8):
        sST.dinc(sync.dma_start(out=o_lnx[c * 128:(c + 1) * 128, :], in_=LNB[:, c * TOWN:(c + 1) * TOWN]))
    for ts2 in range(4):
        a0 = 512 * ts2
        for oc in range(2):
            pp = (pA, pB)[oc]
            wv(ten)
            instr = None
            for kc in range(8):
                instr = ten.matmul(pp[:, 0:512], WD[:, kc * 256 + oc * 128:kc * 256 + oc * 128 + 128],
                                   LNB[:, kc * TOWN + a0:kc * TOWN + a0 + 512], start=(kc == 0), stop=(kc == 7))
            sPE.inc(instr)
            wpe(vec)
            sV.inc(vec.tensor_copy(CKV[:, oc * TOWN + a0:oc * TOWN + a0 + 512], pp[:, 0:512]))
        wv(ten)
        instr = None
        for kc in range(8):
            instr = ten.matmul(pQ[0:32, 0:512], WKR[:, kc * 32:(kc + 1) * 32],
                               LNB[:, kc * TOWN + a0:kc * TOWN + a0 + 512], start=(kc == 0), stop=(kc == 7))
        sPE.inc(instr)
        wpe(act); wv(act)
        sA.inc(act.activation(out=MUS[0:32, 0:512], in_=pQ[0:32, 0:512], func=AF.Abs, scale=1.0))
        wa(act)
        sA.inc(act.activation(out=MUS[0:32, 512:1024], in_=MUS[0:32, 0:512], func=AF.Exp, scale=-1.0))
        wa(act)
        sA.inc(act.activation(out=MUS[0:32, 1024:1536], in_=MUS[0:32, 512:1024], func=AF.Ln, bias=OB1F[0:32, 0:1], scale=1.0))
        wa(vec); wpe(vec)
        sV.inc(vec.tensor_single_scalar(MUS[0:32, 0:512], pQ[0:32, 0:512], 0.0, A.max))
        wv(vec)
        sV.inc(vec.tensor_add(MU[0:32, a0:a0 + 512], MUS[0:32, 0:512], MUS[0:32, 1024:1536]))
        wv(sync)
        sST.dinc(sync.dma_start(out=o_mu[:, a0:a0 + 512], in_=MU[0:32, a0:a0 + 512]))
    for ts2 in range(4):
        a0 = 512 * ts2
        for oc in range(4):
            pp = (pA, pB)[oc % 2]
            wv(ten)
            instr = None
            for kc in range(2):
                instr = ten.matmul(pp[:, 0:512], WUK[:, kc * 512 + oc * 128:kc * 512 + oc * 128 + 128],
                                   CKV[:, kc * TOWN + a0:kc * TOWN + a0 + 512], start=(kc == 0), stop=(kc == 1))
            sPE.inc(instr)
            wpe(vec)
            sST.wait(vec)
            sV.inc(vec.tensor_copy(KO[:, (oc % 2) * 512:(oc % 2) * 512 + 512], pp[:, 0:512]))
            wv(sync)
            sST.dinc(sync.dma_start(out=o_k[oc * 128:(oc + 1) * 128, a0:a0 + 512], in_=KO[:, (oc % 2) * 512:(oc % 2) * 512 + 512]))
            tsl[("k", oc % 2)] = sST.v
        for oc in range(4):
            pp = (pA, pB)[oc % 2]
            wv(ten)
            instr = None
            for kc in range(2):
                instr = ten.matmul(pp[:, 0:512], WUV[:, kc * 512 + oc * 128:kc * 512 + oc * 128 + 128],
                                   CKV[:, kc * TOWN + a0:kc * TOWN + a0 + 512], start=(kc == 0), stop=(kc == 1))
            sPE.inc(instr)
            wpe(vec)
            sST.wait(vec)
            sV.inc(vec.tensor_copy(KO[:, 2 * 512 + (oc % 2) * 512:2 * 512 + (oc % 2) * 512 + 512], pp[:, 0:512]))
            wv(sync)
            sST.dinc(sync.dma_start(out=o_v[oc * 128:(oc + 1) * 128, a0:a0 + 512], in_=KO[:, 2 * 512 + (oc % 2) * 512:2 * 512 + (oc % 2) * 512 + 512]))
            tsl[("v", oc % 2)] = sST.v
    instr = None
    for ts2 in range(4):
        a0 = 512 * ts2
        wv(ten); wa(ten)
        instr = ten.matmul(pR[0:32, 0:512], R32[:], MU[0:32, a0:a0 + 512], start=True, stop=True)
        sPE.inc(instr)
        wpe(vec)
        sV.inc(vec.tensor_mul(MUS[0:32, 0:512], MU[0:32, a0:a0 + 512], CL[0:32, a0:a0 + 512]))
        wv(vec)
        sV.inc(vec.scalar_tensor_tensor(MUS[0:32, 512:1024], pR[0:32, 0:512], 1.0, SL[0:32, a0:a0 + 512], A.mult, A.mult))
        wv(vec)
        sST.wait(vec)
        sV.inc(vec.tensor_add(MUS[0:32, 1024:1536], MUS[0:32, 0:512], MUS[0:32, 512:1024]))
        wv(sync)
        sST.dinc(sync.dma_start(out=o_krl[:, a0:a0 + 512], in_=MUS[0:32, 1024:1536]))
        wv(vec)
        sV.inc(vec.tensor_mul(MUS[0:32, 0:512], MU[0:32, a0:a0 + 512], CI[0:32, a0:a0 + 512]))
        wv(vec)
        sV.inc(vec.scalar_tensor_tensor(MUS[0:32, 512:1024], pR[0:32, 0:512], 1.0, SI[0:32, a0:a0 + 512], A.mult, A.mult))
        wv(vec)
        sV.inc(vec.tensor_add(MUS[0:32, 1536:2048], MUS[0:32, 0:512], MUS[0:32, 512:1024]))
        wv(sync)
        sST.dinc(sync.dma_start(out=o_kri[:, a0:a0 + 512], in_=MUS[0:32, 1536:2048]))
    sST.wait(sync)
    sST.wait(gp)
    es.close()
    return nc

def tile_span(k):
    return min(L, ((128 * (8 * k + 8) + 511) // 512) * 512)



def build_attn(ntiles=16):
    from contextlib import ExitStack
    nc = bass.Bass("TRN2", target_bir_lowering=False, debug=False)
    es = ExitStack()
    T = lambda name, shape, dt: es.enter_context(nc.sbuf_tensor(name, shape, dt))

    kit = nc.dram_tensor("kit", [128, L], F32, kind="ExternalInput").ap()
    qiP = nc.dram_tensor("qiP", [256, ntiles * 128], F32, kind="ExternalInput").ap()
    gt = nc.dram_tensor("gt", [L, 1056], BF16, kind="ExternalInput").ap()
    kclP = nc.dram_tensor("kclP", [512, ntiles * 256], BF16, kind="ExternalInput").ap()
    vltP = nc.dram_tensor("vltP", [ntiles * 256, 512], BF16, kind="ExternalInput").ap()
    krlP = nc.dram_tensor("krlP", [128, ntiles * 256], BF16, kind="ExternalInput").ap()
    k96 = nc.dram_tensor("k96", [512, 96], BF16, kind="ExternalInput").ap()
    v96 = nc.dram_tensor("v96", [96, 512], BF16, kind="ExternalInput").ap()
    kr96 = nc.dram_tensor("kr96", [128, 96], BF16, kind="ExternalInput").ap()
    lnxP = nc.dram_tensor("lnxP", [DM, ntiles * 128], BF16, kind="ExternalInput").ap()
    wdq = nc.dram_tensor("wdq", [DM, 512], F32, kind="ExternalInput").ap()
    wuq = nc.dram_tensor("wuq", [512, 512], F32, kind="ExternalInput").ap()
    wqr = nc.dram_tensor("wqr", [512, 512], F32, kind="ExternalInput").ap()
    wout = nc.dram_tensor("wout", [512, DM], F32, kind="ExternalInput").ap()
    idxw = nc.dram_tensor("idxw", [128, 4], F32, kind="ExternalInput").ap()
    c2t = nc.dram_tensor("c2t", [128, ntiles * 128], BF16, kind="ExternalInput").ap()
    s2t = nc.dram_tensor("s2t", [128, ntiles * 128], BF16, kind="ExternalInput").ap()
    cit = nc.dram_tensor("cit", [128, ntiles * 128], BF16, kind="ExternalInput").ap()
    sit = nc.dram_tensor("sit", [128, ntiles * 128], BF16, kind="ExternalInput").ap()
    r128 = nc.dram_tensor("r128", [128, 128], BF16, kind="ExternalInput").ap()
    crot = nc.dram_tensor("crot", [128, 1024], BF16, kind="ExternalInput").ap()
    srot = nc.dram_tensor("srot", [128, 1024], BF16, kind="ExternalInput").ap()
    iden = nc.dram_tensor("iden", [128, 128], BF16, kind="ExternalInput").ap()
    mloc = nc.dram_tensor("mloc", [ntiles * 128, 256], BF16, kind="ExternalInput").ap()
    m96 = nc.dram_tensor("m96", [ntiles * 128, 96], BF16, kind="ExternalInput").ap()
    mg = nc.dram_tensor("mg", [ntiles * 128, 32], BF16, kind="ExternalInput").ap()
    kidx = nc.dram_tensor("kidx", [128, 512], F32, kind="ExternalInput").ap()
    thrs1 = nc.dram_tensor("thrs1", [128, ntiles * 32], F32, kind="ExternalInput").ap()
    thrs2 = nc.dram_tensor("thrs2", [128, ntiles * 32], F32, kind="ExternalInput").ap()
    o1o = nc.dram_tensor("o1o", [DM, ntiles * 128], F32, kind="ExternalOutput").ap()

    IB = T("IB", [128, L], F32)
    KIT = T("KIT", [128, L], F32)
    IBb = IB[:].bitcast(BF16)
    VGA = IBb[:, 0:32 * 512]
    GD = IBb[:, 32 * 512:32 * 512 + 8 * 1056]
    TG = IBb[:, 24832:24832 + 6144]
    KCL = T("KCL", [128, 4 * 256], BF16)
    VLT = T("VLT", [128, 2 * 512], BF16)
    KRL = T("KRL", [128, 256], BF16)
    K96 = T("K96", [128, 4 * 96], BF16)
    V96 = T("V96", [96, 512], BF16)
    KR96 = T("KR96", [128, 96], BF16)
    WDQ = T("WDQ", [128, 8 * 512], BF16)
    WUQ = T("WUQ", [128, 4 * 512], BF16)
    WQR = T("WQR", [128, 4 * 512], BF16)
    WOUT = T("WOUT", [128, 4 * DM], BF16)
    IDXW = T("IDXW", [128, 4], F32)
    C2T = T("C2T", [128, 128], BF16)
    S2T = T("S2T", [128, 128], BF16)
    CIT = T("CIT", [128, 128], BF16)
    SIT = T("SIT", [128, 128], BF16)
    R128 = T("R128", [128, 128], BF16)
    CROT = T("CROT", [128, 1024], BF16)
    SROT = T("SROT", [128, 1024], BF16)
    IDN = T("IDN", [128, 128], BF16)
    MLOC = T("MLOC", [128, 256], BF16)
    M96 = T("M96", [128, 96], BF16)
    MG = T("MG", [128, 32], BF16)
    KIDX = T("KIDX", [128, 512], F32)
    THRS1 = T("THRS1", [128, ntiles * 32], F32)
    THRS2 = T("THRS2", [128, ntiles * 32], F32)
    LNXT = T("LNXT", [128, 8 * 128], BF16)
    CQ = T("CQ", [128, 4 * 128], BF16)
    QC = T("QC", [128, 4 * 128], BF16)
    QIT = T("QIT", [128, 2 * 128], F32)
    MUQF = [T(f"MUQF{i}", [128, 128], BF16) for i in range(4)]
    QRAF = [T(f"QRAF{i}", [128, 128], BF16) for i in range(4)]
    QRBF = [T(f"QRBF{i}", [128, 128], BF16) for i in range(4)]
    MUS = T("MUS", [128, 4 * 128], F32)
    QCT = T("QCT", [128, 512], BF16)
    QRBT = T("QRBT", [128, 512], BF16)
    SJ = T("SJ", [128, 384], F32)
    SG8 = T("SG8", [128, 256], F32)
    PX = T("PX", [128, 384], BF16)
    PTX = T("PTX", [128, 384], BF16)
    IDXT = T("IDXT", [128, 32], U32)
    IDXI = T("IDXI", [128, 32], I32)
    NM = T("NM", [128, 1], F32)
    DEN = T("DEN", [128, 1], F32)
    REC = T("REC", [128, 1], F32)
    M8 = T("M8", [128, 8], F32)
    AOT = T("AOT", [128, 512], F32)
    AOB = T("AOB", [128, 512], BF16)
    AOF = T("AOF", [128, 4 * 128], BF16)
    TF = T("TF", [128, 512], F32)
    TFB = T("TFB", [128, 512], F32)
    TH = T("TH", [128, 512], F32)
    OE = T("OE", [128, 2 * 128], F32)
    ONE1 = T("ONE1", [128, 1], F32)

    pS = es.enter_context(nc.psum_tensor("pS", [128, 384], F32))
    pV = es.enter_context(nc.psum_tensor("pV", [128, 64], F32))
    pT = es.enter_context(nc.psum_tensor("pT", [128, 384], BF16))
    pA = es.enter_context(nc.psum_tensor("pA", [128, 512], F32))
    pB = es.enter_context(nc.psum_tensor("pB", [128, 512], F32))
    pO = es.enter_context(nc.psum_tensor("pO", [128, 128], F32))
    pR = es.enter_context(nc.psum_tensor("pR", [128, 128], F32))

    sL = Sem(nc, es, "sL")
    sG = Sem(nc, es, "sG")
    sPE = Sem(nc, es, "sPE")
    sV = Sem(nc, es, "sV")
    sA = Sem(nc, es, "sA")
    sST = Sem(nc, es, "sST")
    es.enter_context(nc.Block())
    sync, ten, vec, act, gp = nc.sync, nc.tensor, nc.vector, nc.scalar, nc.gpsimd

    def wv(*e):
        for x in e: sV.wait(x)

    def wpe(*e):
        for x in e: sPE.wait(x)

    def wa(*e):
        for x in e: sA.wait(x)

    for dst, src_, nchunk in ((WDQ, wdq, 8), (WUQ, wuq, 4), (WQR, wqr, 4), (WOUT, wout, 4)):
        sG.dinc(gp.dma_start(out=dst[:].rearrange("p (c n) -> p c n", c=nchunk),
                             in_=src_.rearrange("(c p) n -> p c n", p=128)))
    sG.dinc(gp.dma_start(out=K96[:].rearrange("p (c n) -> p c n", c=4),
                         in_=k96.rearrange("(c p) n -> p c n", p=128)))
    sG.dinc(gp.dma_start(out=KIT[:].rearrange("p (c n) -> p c n", c=16),
                         in_=kit.rearrange("p (c n) -> p c n", c=16)))
    for dst, src_ in ((V96, v96), (KR96, kr96), (IDXW, idxw), (R128, r128),
                      (CROT, crot), (SROT, srot), (IDN, iden), (KIDX, kidx),
                      (THRS1, thrs1), (THRS2, thrs2)):
        sL.dinc(sync.dma_start(out=dst[:], in_=src_))
    sV.inc(vec.memset(ONE1[:], 1.0))
    for e in (sync, ten, vec, act, gp):
        sL.wait(e); sG.wait(e); sV.wait(e)

    SCALE = float((64 + 32) ** -0.5)

    def bc1(ap_col, n):
        return ap_col.rearrange("p u -> p u").to_broadcast([128, n])

    def rope128(dst, mu, cs, ss):
        wv(ten)
        sPE.inc(ten.matmul(pR[:, 0:128], R128[:], mu[:], start=True, stop=True))
        wv(vec)
        sV.inc(vec.tensor_mul(MUS[:, 384:512], mu[:], cs[:]))
        wpe(vec)
        sV.inc(vec.scalar_tensor_tensor(MUS[:, 256:384], pR[:, 0:128], SCALE, ss[:], A.mult, A.mult))
        wv(vec)
        sV.inc(vec.scalar_tensor_tensor(dst[:], MUS[:, 384:512], SCALE, MUS[:, 256:384], A.mult, A.add))

    for k in range(ntiles):
        tc = 128 * k
        span = tile_span(k)
        nblk = span // 512
        wv(sync); wa(sync); wpe(sync)
        for c in range(8):
            sL.dinc(sync.dma_start(out=LNXT[:, c * 128:(c + 1) * 128], in_=lnxP[c * 128:(c + 1) * 128, tc:tc + 128]))
        for c in range(4):
            sL.dinc(sync.dma_start(out=KCL[:, c * 256:(c + 1) * 256], in_=kclP[c * 128:(c + 1) * 128, k * 256:(k + 1) * 256]))
        for c in range(2):
            sL.dinc(sync.dma_start(out=VLT[:, c * 512:(c + 1) * 512], in_=vltP[(2 * k + c) * 128:(2 * k + c + 1) * 128, :]))
        sL.dinc(sync.dma_start(out=KRL[:, 0:256], in_=krlP[:, k * 256:(k + 1) * 256]))
        sL.dinc(sync.dma_start(out=MLOC[:], in_=mloc[k * 128:(k + 1) * 128, :]))
        sL.dinc(sync.dma_start(out=M96[:], in_=m96[k * 128:(k + 1) * 128, :]))
        sL.dinc(sync.dma_start(out=MG[:], in_=mg[k * 128:(k + 1) * 128, :]))
        for dst, src_ in ((C2T, c2t), (S2T, s2t), (CIT, cit), (SIT, sit)):
            sL.dinc(sync.dma_start(out=dst[:], in_=src_[:, tc:tc + 128]))
        for oc in range(2):
            sL.dinc(sync.dma_start(out=QIT[:, oc * 128:(oc + 1) * 128], in_=qiP[oc * 128:(oc + 1) * 128, tc:tc + 128]))
        sL.wait(ten); sL.wait(vec); sL.wait(act)
        for oc in range(4):
            pp = (pA, pB)[oc % 2]
            wv(ten)
            for kc in range(8):
                instr = ten.matmul(pp[:, 0:128], WDQ[:, kc * 512 + oc * 128:kc * 512 + oc * 128 + 128],
                                   LNXT[:, kc * 128:(kc + 1) * 128], start=(kc == 0), stop=(kc == 7))
            sPE.inc(instr)
            wpe(vec)
            sV.inc(vec.tensor_copy(CQ[:, oc * 128:(oc + 1) * 128], pp[:, 0:128]))
        for oc in range(4):
            pp = (pA, pB)[oc % 2]
            wv(ten)
            for kc in range(4):
                instr = ten.matmul(pp[:, 0:128], WUQ[:, kc * 512 + oc * 128:kc * 512 + oc * 128 + 128],
                                   CQ[:, kc * 128:(kc + 1) * 128], start=(kc == 0), stop=(kc == 3))
            sPE.inc(instr)
            wpe(vec)
            sV.inc(vec.tensor_scalar_mul(QC[:, oc * 128:(oc + 1) * 128], pp[:, 0:128], SCALE))
        for oc in range(4):
            pp = (pA, pB)[oc % 2]
            wv(ten)
            for kc in range(4):
                instr = ten.matmul(pp[:, 0:128], WQR[:, kc * 512 + oc * 128:kc * 512 + oc * 128 + 128],
                                   CQ[:, kc * 128:(kc + 1) * 128], start=(kc == 0), stop=(kc == 3))
            sPE.inc(instr)
            wpe(act); wa(act); wv(act)
            sA.inc(act.activation(out=MUS[:, 0:128], in_=pp[:, 0:128], func=AF.Abs, scale=1.0))
            wa(act)
            sA.inc(act.activation(out=MUS[:, 128:256], in_=MUS[:, 0:128], func=AF.Exp, scale=-1.0))
            wa(act)
            sA.inc(act.activation(out=MUS[:, 0:128], in_=MUS[:, 128:256], func=AF.Ln,
                                  bias=ONE1[:, 0:1], scale=1.0))
            wpe(vec); wa(vec)
            sV.inc(vec.tensor_single_scalar(MUS[:, 128:256], pp[:, 0:128], 0.0, A.max))
            wv(vec)
            sV.inc(vec.tensor_add(MUQF[oc][:], MUS[:, 128:256], MUS[:, 0:128]))
        for oc in range(4):
            rope128(QRAF[oc], MUQF[oc], C2T, S2T)
            rope128(QRBF[oc], MUQF[oc], CIT, SIT)
        for oc in range(4):
            wv(ten)
            sPE.inc(ten.transpose(pT[:, 0:128], QC[:, oc * 128:(oc + 1) * 128], IDN[:]))
            wpe(vec)
            sV.inc(vec.tensor_copy(QCT[:, oc * 128:(oc + 1) * 128], pT[:, 0:128]))
            wv(ten)
            sPE.inc(ten.transpose(pT[:, 0:128], QRBF[oc][:], IDN[:]))
            wpe(vec)
            sV.inc(vec.tensor_copy(QRBT[:, oc * 128:(oc + 1) * 128], pT[:, 0:128]))
        for b in range(nblk):
            half = b % 2
            for h in range(4):
                pp = (pA, pB)[h % 2]
                wv(ten)
                sPE.inc(ten.matmul(pp[:, 0:512],
                                   QIT[(h % 2) * 64:(h % 2) * 64 + 64, (h // 2) * 128:(h // 2) * 128 + 128],
                                   KIT[(h % 2) * 64:(h % 2) * 64 + 64, b * 512:(b + 1) * 512],
                                   start=True, stop=True))
                wpe(vec)
                if h == 0:
                    sV.inc(vec.scalar_tensor_tensor(IB[:, b * 512:(b + 1) * 512], pp[:, 0:512], 0.0,
                                                    bc1(IDXW[:, 0:1], 512), A.max, A.mult))
                else:
                    sV.inc(vec.scalar_tensor_tensor(TH[:, 0:512], pp[:, 0:512], 0.0,
                                                    bc1(IDXW[:, h:h + 1], 512), A.max, A.mult))
                    wv(vec)
                    sV.inc(vec.tensor_add(IB[:, b * 512:(b + 1) * 512], IB[:, b * 512:(b + 1) * 512], TH[:, 0:512]))
            if b >= 2 * k - 1:
                col = k * 32 + b
                wv(vec)
                sV.inc(vec.scalar_tensor_tensor(TF[:, 0:512], KIDX[:], 1.0,
                                                bc1(THRS1[:, col:col + 1], 512), A.mult, A.is_gt))
                if b >= 2 * k:
                    sV.inc(vec.scalar_tensor_tensor(TFB[:, 0:512], KIDX[:], 1.0,
                                                    bc1(THRS2[:, col:col + 1], 512), A.mult, A.is_gt))
                    wv(vec)
                    sV.inc(vec.tensor_sub(TF[:, 0:512], TF[:, 0:512], TFB[:, 0:512]))
                    wv(vec)
                    sV.inc(vec.scalar_tensor_tensor(IB[:, b * 512:(b + 1) * 512], TFB[:, 0:512], -1.0e30,
                                                    IB[:, b * 512:(b + 1) * 512], A.mult, A.add))
                wv(vec)
                sV.inc(vec.scalar_tensor_tensor(IB[:, b * 512:(b + 1) * 512], TF[:, 0:512], -3.0e38,
                                                IB[:, b * 512:(b + 1) * 512], A.mult, A.add))
        for r in range(4):
            wv(vec)
            sV.inc(vec.max(out=M8[:], in_=IB[:, 0:span]))
            wv(vec)
            sV.inc(vec.max_index(out=IDXT[:, r * 8:(r + 1) * 8], in_max=M8[:], in_values=IB[:, 0:span]))
            wv(vec)
            sV.inc(vec.match_replace(out=IB[:, 0:span], in_to_replace=M8[:], in_values=IB[:, 0:span],
                                     imm_value=-3.0e38))
        wv(vec)
        sV.inc(vec.tensor_copy(IDXI[:], IDXT[:].bitcast(I32)))
        for grp in range(4):
            wv(gp)
            for c in range(8):
                slot = grp * 8 + c
                sG.dinc(gp.indirect_dma_start(
                    out=GD[:, c * 1056:(c + 1) * 1056], out_offset=None,
                    in_=gt, in_offset=bass.IndirectOffsetOnAxis(ap=IDXI[:, slot:slot + 1], axis=0)))
            sG.wait(vec)
            gd3 = GD.rearrange("p (a b) -> p a b", a=8)
            sV.inc(vec.tensor_tensor(TG[:, 4096:4096 + 256].rearrange("p (a b) -> p a b", a=8),
                                     gd3[:, :, 512:544],
                                     CROT[:, grp * 256:(grp + 1) * 256].rearrange("p (a b) -> p a b", a=8),
                                     A.mult))
            sV.inc(vec.tensor_tensor(TG[:, 4352:4352 + 128].rearrange("p (a b) -> p a b", a=8),
                                     gd3[:, :, 528:544],
                                     SROT[:, grp * 256:(grp + 1) * 256].rearrange("p (a b) -> p a b", a=8)[:, :, 0:16],
                                     A.mult))
            sV.inc(vec.tensor_tensor(TG[:, 4480:4480 + 128].rearrange("p (a b) -> p a b", a=8),
                                     gd3[:, :, 512:528],
                                     SROT[:, grp * 256:(grp + 1) * 256].rearrange("p (a b) -> p a b", a=8)[:, :, 16:32],
                                     A.mult))
            wv(vec)
            sV.inc(vec.tensor_sub(TG[:, 4096:4096 + 256].rearrange("p (a b) -> p a b", a=8)[:, :, 0:16],
                                  TG[:, 4096:4096 + 256].rearrange("p (a b) -> p a b", a=8)[:, :, 0:16],
                                  TG[:, 4352:4352 + 128].rearrange("p (a b) -> p a b", a=8)))
            wv(vec)
            sV.inc(vec.tensor_add(TG[:, 4096:4096 + 256].rearrange("p (a b) -> p a b", a=8)[:, :, 16:32],
                                  TG[:, 4096:4096 + 256].rearrange("p (a b) -> p a b", a=8)[:, :, 16:32],
                                  TG[:, 4480:4480 + 128].rearrange("p (a b) -> p a b", a=8)))
            for e in range(2):
                wv(vec)
                sV.inc(vec.tensor_tensor(
                    TG[:, 5120:5120 + 1024].rearrange("p (a c b) -> p a c b", a=8, c=4),
                    TG[:, 4096:4096 + 256].rearrange("p (a b) -> p a () b", a=8).to_broadcast([128, 8, 4, 32]),
                    QRBT[:].rearrange("p (c q b) -> p c q b", q=4, b=32)[:, :, 2 * e, :]
                        .rearrange("p c b -> p () c b").to_broadcast([128, 8, 4, 32]),
                    A.mult))
                wv(vec)
                sV.inc(vec.reduce_sum(
                    SG8[:, grp * 64:(grp + 1) * 64].rearrange("p (a c e) -> p a c e", c=4, e=2)[:, :, :, e],
                    TG[:, 5120:5120 + 1024].rearrange("p (a c b) -> p a c b", a=8, c=4), axis=AX.X))
            wv(vec)
            sV.inc(vec.tensor_tensor(TG[:, 0:4096].rearrange("p (a h b) -> p a h b", a=8, h=8),
                                     gd3[:, :, 0:512].rearrange("p a (h b) -> p a h b", h=8),
                                     QCT[:].rearrange("p (h b) -> p () h b", h=8).to_broadcast([128, 8, 8, 64]),
                                     A.mult))
            wv(vec)
            sV.inc(vec.reduce_sum(TF[:, 0:64].rearrange("p (a h) -> p a h", a=8),
                                  TG[:, 0:4096].rearrange("p (a h b) -> p a h b", a=8, h=8), axis=AX.X))
            wv(vec)
            sV.inc(vec.tensor_add(SG8[:, grp * 64:(grp + 1) * 64], SG8[:, grp * 64:(grp + 1) * 64], TF[:, 0:64]))
            sV.inc(vec.tensor_copy(VGA[:, grp * 4096:(grp + 1) * 4096].rearrange("p (a b) -> p a b", a=8),
                                   gd3[:, :, 544:1056]))
        for h in range(8):
            g2, hr, rr = h // 2, (h % 2) * 64, (h % 2) * 64
            qch = QC[hr:hr + 64, g2 * 128:(g2 + 1) * 128]
            wv(ten); wa(ten)
            ten.matmul(pS[:, 0:256], qch, KCL[hr:hr + 64, g2 * 256:(g2 + 1) * 256], start=True, stop=False)
            ten.matmul(pS[:, 0:256], QRAF[g2][rr:rr + 32, :], KRL[rr:rr + 32, 0:256], start=False, stop=False)
            sPE.inc(ten.matmul(pS[:, 0:256], IDN[:], MLOC[:], start=False, stop=True))
            ten.matmul(pS[:, 256:352], qch, K96[hr:hr + 64, g2 * 96:(g2 + 1) * 96], start=True, stop=False)
            ten.matmul(pS[:, 256:352], QRBF[g2][rr:rr + 32, :], KR96[rr:rr + 32, :], start=False, stop=False)
            sPE.inc(ten.matmul(pS[:, 256:352], IDN[:], M96[:], start=False, stop=True))
            wpe(vec); wv(vec)
            sV.inc(vec.tensor_copy(SJ[:, 0:352], pS[:, 0:352]))
            sV.inc(vec.tensor_copy(SJ[:, 352:384],
                                   SG8[:].rearrange("p (s hh) -> p s hh", hh=8)[:, :, h]))
            wv(vec)
            sV.inc(vec.scalar_tensor_tensor(SJ[:, 352:384], MG[:], 1.0, SJ[:, 352:384], A.mult, A.add))
            wv(vec)
            sV.inc(vec.reduce_max(NM[:], SJ[:], axis=AX.X, negate=True))
            wv(act)
            sA.inc(act.activation(out=PX[:], in_=SJ[:], func=AF.Exp, bias=NM[:, 0:1], scale=1.0,
                                  accum_out=DEN[:]))
            wa(vec)
            sV.inc(vec.reciprocal(REC[:], DEN[:]))
            wv(ten); wa(ten)
            for i2 in range(3):
                instr = ten.transpose(pT[:, i2 * 128:(i2 + 1) * 128], PX[:, i2 * 128:(i2 + 1) * 128], IDN[:])
            sPE.inc(instr)
            wpe(vec)
            sV.inc(vec.tensor_copy(PTX[:], pT[:, 0:384]))
            wv(ten)
            ten.matmul(pV[:, 0:64], PTX[:, 0:128], VLT[:, h * 64:h * 64 + 64], start=True, stop=False)
            ten.matmul(pV[:, 0:64], PTX[:, 128:256], VLT[:, 512 + h * 64:512 + h * 64 + 64], start=False, stop=False)
            sPE.inc(ten.matmul(pV[:, 0:64], PTX[0:96, 256:384], V96[0:96, h * 64:h * 64 + 64], start=False, stop=True))
            wv(vec)
            sV.inc(vec.tensor_tensor(TG[:, 0:2048].rearrange("p (a b) -> p a b", a=32),
                                     VGA.rearrange("p (a b) -> p a b", a=32)[:, :, h * 64:(h + 1) * 64],
                                     PX[:, 352:384].rearrange("p a -> p a ()").to_broadcast([128, 32, 64]),
                                     A.mult))
            wv(vec)
            sV.inc(vec.reduce_sum(AOT[:, h * 64:(h + 1) * 64],
                                  TG[:, 0:2048].rearrange("p (a b) -> p b a", a=32), axis=AX.X))
            wpe(vec); wv(vec)
            sV.inc(vec.tensor_add(AOT[:, h * 64:(h + 1) * 64], AOT[:, h * 64:(h + 1) * 64], pV[:, 0:64]))
            wv(vec)
            sV.inc(vec.scalar_tensor_tensor(AOT[:, h * 64:(h + 1) * 64], AOT[:, h * 64:(h + 1) * 64], 1.0,
                                            bc1(REC[:, 0:1], 64), A.mult, A.mult))
        wv(vec)
        sV.inc(vec.tensor_copy(AOB[:], AOT[:]))
        for oc in range(4):
            wv(ten)
            sPE.inc(ten.transpose(pT[:, 0:128], AOB[:, oc * 128:(oc + 1) * 128], IDN[:]))
            wpe(vec)
            sV.inc(vec.tensor_copy(AOF[:, oc * 128:(oc + 1) * 128], pT[:, 0:128]))
        for oc in range(8):
            slot = oc % 2
            wv(ten)
            for kc in range(4):
                instr = ten.matmul(pO[:, 0:128], WOUT[:, kc * DM + oc * 128:kc * DM + oc * 128 + 128],
                                   AOF[:, kc * 128:(kc + 1) * 128], start=(kc == 0), stop=(kc == 3))
            sPE.inc(instr)
            wpe(vec)
            sST.wait(vec)
            sV.inc(vec.tensor_copy(OE[:, slot * 128:(slot + 1) * 128], pO[:, 0:128]))
            wv(sync)
            sST.dinc(sync.dma_start(out=o1o[oc * 128:(oc + 1) * 128, tc:tc + 128],
                                    in_=OE[:, slot * 128:(slot + 1) * 128]))

    sST.wait(sync)
    es.close()
    return nc


ATH = (1.0 / 10000.0 ** (2.0 * np.arange(16, dtype=np.float64) / 32.0))

def rot32_mat():
    r = np.zeros((32, 32), np.float64)
    for m in range(16):
        r[m + 16, m] = -1.0
        r[m, m + 16] = 1.0
    return r

def aprep_maps(inp, core):
    x = inp["L0_attn_a_x"][0]
    t0 = TOWN * core
    pos = np.arange(t0, t0 + TOWN, dtype=np.float64)
    th = ATH[None, :]
    angl = (pos[:, None] + 63.0) * th
    angi = pos[:, None] * th
    mk = lambda a, f: np.ascontiguousarray(
        np.tile(f(a).T.astype(BF), (2, 1)))
    r32 = rot32_mat().astype(BF)
    return dict(
        xin=np.ascontiguousarray(x[t0:t0 + TOWN].T),
        wdkv=f32c(inp["L0_attn_a_W_dkv"]), wuk=f32c(inp["L0_attn_a_W_uk"]),
        wuv=f32c(inp["L0_attn_a_W_uv"]), wkr=f32c(inp["L0_attn_a_W_kr"]),
        wik=f32c(inp["L0_attn_a_idx_wk"]), wiq_=f32c(inp["L0_attn_a_idx_wq"]),
        cl_=mk(angl, np.cos), sl_=mk(angl, np.sin),
        ci_=mk(angi, np.cos), si_=mk(angi, np.sin),
        r32=r32, ob1f=np.ones((128, 1), np.float32), obcf=np.ones((1, 128), np.float32))

def run_aprep(progs, inp):
    in_maps = [aprep_maps(inp, c) for c in range(NCORES)]
    res = run_bass_kernel_spmd(progs["aprep"], in_maps, core_ids=list(range(NCORES)))
    cat = lambda key: np.concatenate([np.asarray(r[key]) for r in res.results], axis=1)
    return dict(lnx=cat("o_lnx"), ki=cat("o_ki"), K=cat("o_k"), V=cat("o_v"),
                mu=cat("o_mu"), krl=cat("o_krl"), kri=cat("o_kri"), qi=cat("o_qi"))

def attn_const_tables():
    r32 = rot32_mat()
    r128 = np.zeros((128, 128), np.float64)
    for b4 in range(4):
        r128[b4 * 32:(b4 + 1) * 32, b4 * 32:(b4 + 1) * 32] = r32
    g = np.arange(32, dtype=np.float64)
    d = np.arange(32)
    ang = (64.0 + g)[:, None] * ATH[d % 16][None, :]
    crot = np.tile(np.cos(ang).reshape(1, 1024).astype(BF), (128, 1))
    srot = np.tile(np.sin(ang).reshape(1, 1024).astype(BF), (128, 1))
    iden = np.eye(128, dtype=BF)
    kidx = np.tile(np.arange(512, dtype=np.float32)[None, :], (128, 1))
    return dict(r128=np.ascontiguousarray(r128.astype(BF)), crot=np.ascontiguousarray(crot),
                srot=np.ascontiguousarray(srot), iden=iden, kidx=np.ascontiguousarray(kidx))

def attn_core_tables(core, ntiles=16):
    p = np.arange(128)
    tglob = np.concatenate([128 * (8 * k + core) + p for k in range(ntiles)]).astype(np.float64)
    rowth = ATH[np.arange(128) % 16][:, None]
    c2t = np.cos(2.0 * tglob[None, :] * rowth).astype(BF)
    s2t = np.sin(2.0 * tglob[None, :] * rowth).astype(BF)
    cit = np.cos(tglob[None, :] * rowth).astype(BF)
    sit = np.sin(tglob[None, :] * rowth).astype(BF)
    thrs1 = np.zeros((128, ntiles * 32), np.float32)
    thrs2 = np.zeros((128, ntiles * 32), np.float32)
    for k in range(ntiles):
        t = 128 * (8 * k + core) + p
        for b in range(32):
            thrs1[:, k * 32 + b] = t - 64 - 512 * b
            thrs2[:, k * 32 + b] = t - 512 * b
    NEG = np.float32(-1e30)
    mloc = np.full((ntiles * 128, 256), NEG, np.float32)
    m96 = np.full((ntiles * 128, 96), NEG, np.float32)
    mg = np.zeros((ntiles * 128, 32), np.float32)
    j = np.arange(256)
    for k in range(ntiles):
        t = 128 * (8 * k + core) + p
        for pp_ in range(128):
            if t[pp_] <= 63:
                m96[k * 128 + pp_, :] = 0.0
                mg[k * 128 + pp_, :] = NEG
            else:
                sel = (j >= pp_ + 65) & (j <= pp_ + 128)
                mloc[k * 128 + pp_, sel] = 0.0
    return dict(c2t=np.ascontiguousarray(c2t), s2t=np.ascontiguousarray(s2t),
                cit=np.ascontiguousarray(cit), sit=np.ascontiguousarray(sit),
                thrs1=thrs1, thrs2=thrs2,
                mloc=np.ascontiguousarray(mloc.astype(BF)),
                m96=np.ascontiguousarray(m96.astype(BF)),
                mg=np.ascontiguousarray(mg.astype(BF)))

def attn_weight_maps(inp):
    wqr = np.asarray(inp["L0_attn_a_W_qr"], np.float32)
    wqrp = np.zeros((512, 512), np.float32)
    for oc in range(4):
        wqrp[:, oc * 128 + 0:oc * 128 + 32] = wqr[:, (2 * oc) * 32:(2 * oc) * 32 + 32]
        wqrp[:, oc * 128 + 64:oc * 128 + 96] = wqr[:, (2 * oc + 1) * 32:(2 * oc + 1) * 32 + 32]
    idxw = np.tile(np.asarray(inp["L0_attn_a_idx_w"], np.float32)[None, :], (128, 1))
    return dict(wdq=f32c(inp["L0_attn_a_W_dq"]), wuq=f32c(inp["L0_attn_a_W_uq"]),
                wqr=np.ascontiguousarray(wqrp), wout=f32c(inp["L0_attn_a_W_out"]),
                idxw=np.ascontiguousarray(idxw))

def attn_gather_maps(ap, core, ntiles=16):
    Kf, Vf, ki, mu, krl, kri, lnx = ap["K"], ap["V"], ap["ki"], ap["mu"], ap["krl"], ap["kri"], ap["lnx"]
    gt = np.ascontiguousarray(np.concatenate([Kf.T, mu.T, Vf.T], axis=1))
    kit = np.ascontiguousarray(np.tile(np.asarray(ki, np.float32), (2, 1)))
    qiP = np.zeros((256, ntiles * 128), np.float32)
    kclP = np.zeros((512, ntiles * 256), BF)
    vltP = np.zeros((ntiles * 256, 512), BF)
    krlP = np.zeros((128, ntiles * 256), BF)
    lnxP = np.zeros((DM, ntiles * 128), BF)
    for k in range(ntiles):
        tcg = 128 * (8 * k + core)
        lo = max(0, tcg - 128)
        off = lo - (tcg - 128)
        kclP[:, k * 256 + off:(k + 1) * 256] = Kf[:, lo:tcg + 128]
        vltP[k * 256 + off:(k + 1) * 256, :] = Vf.T[lo:tcg + 128, :]
        krlP[0:32, k * 256 + off:(k + 1) * 256] = krl[:, lo:tcg + 128]
        lnxP[:, k * 128:(k + 1) * 128] = lnx[:, tcg:tcg + 128]
        qiP[:, k * 128:(k + 1) * 128] = ap["qi"][:, tcg:tcg + 128]
    for rr in range(1, 4):
        krlP[rr * 32:(rr + 1) * 32, :] = krlP[0:32, :]
    k96 = np.ascontiguousarray(Kf[:, 0:96])
    v96 = np.ascontiguousarray(Vf.T[0:96, :])
    kr96 = np.ascontiguousarray(np.tile(kri[:, 0:96], (4, 1)))
    return dict(gt=gt, kit=kit, qiP=qiP, kclP=np.ascontiguousarray(kclP),
                vltP=np.ascontiguousarray(vltP), krlP=np.ascontiguousarray(krlP),
                lnxP=np.ascontiguousarray(lnxP), k96=k96, v96=v96, kr96=kr96)

def run_attn(progs, inp, ap, wcache, ntiles=16):
    if "attn_const" not in wcache:
        wcache["attn_const"] = attn_const_tables()
        wcache["attn_w"] = attn_weight_maps(inp)
        wcache["attn_core"] = [attn_core_tables(c, ntiles) for c in range(NCORES)]
    in_maps = []
    for c in range(NCORES):
        m = dict(wcache["attn_const"], **wcache["attn_w"], **wcache["attn_core"][c])
        m.update(attn_gather_maps(ap, c, ntiles))
        in_maps.append(m)
    res = run_bass_kernel_spmd(progs["attn"], in_maps, core_ids=list(range(NCORES)))
    o1 = np.zeros((DM, L), np.float32)
    for c in range(NCORES):
        oc_ = np.asarray(res.results[c]["o1o"])
        for k in range(ntiles):
            o1[:, 128 * (8 * k + c):128 * (8 * k + c) + 128] = oc_[:, k * 128:(k + 1) * 128]
    return o1

_PROGS = {}

def _get_progs():
    if not _PROGS:
        _PROGS["aprep"] = build_aprep()
        _PROGS["attn"] = build_attn()
        _PROGS["entry"] = build_entry()
        _PROGS["layer"] = build_ffn_layer(False)
        _PROGS["layer_exit"] = build_ffn_layer(True)
        _PROGS["mambaA"] = build_mamba(False)
        _PROGS["mambaB"] = build_mamba(True)
    return _PROGS


def kernel(**inputs):
    inp = inputs
    progs = _get_progs()
    wcache = {}
    x_fm = np.ascontiguousarray(np.asarray(inp["L0_attn_a_x"], np.float32)[0].T)
    ap = run_aprep(progs, inp)
    o1 = run_attn(progs, inp, ap, wcache)
    x2 = run_ffn_block(progs, x_fm, o1, inp, wcache)
    x3 = run_mamba(progs, x2, inp)
    zz = np.zeros((DM, L), np.float32)
    x4 = run_ffn_block(progs, x3, zz, inp, wcache)
    return np.ascontiguousarray(x4.T)[None].astype(np.float32)


# version 03 (final)
# speedup vs baseline: 1.0004x; 1.0004x over previous
import math
import numpy as np
import ml_dtypes
import concourse.bass as bass
import concourse.mybir as mybir
from concourse.bass_utils import run_bass_kernel_spmd

F32 = mybir.dt.float32
BF16 = mybir.dt.bfloat16
I32 = mybir.dt.int32
U32 = mybir.dt.uint32
A = mybir.AluOpType
AF = mybir.ActivationFunctionType
AX = mybir.AxisListType
BF = ml_dtypes.bfloat16

NCORES = 8
L = 16384
DM = 1024
TOWN = L // NCORES
HALO = 512
TEXT = TOWN + HALO
SEXT = TOWN + 384
MEXT = TOWN + 128
NQT = MEXT // 128
NIN = 2816
NINC = NIN // 128
FSL = 256
NFS = TOWN // FSL
SHN = 79744


class Sem:
    def __init__(self, nc, es, name):
        self.h = es.enter_context(nc.semaphore(name))
        self.v = 0

    def inc(self, instr, n=1):
        instr.then_inc(self.h, n)
        self.v += n
        return instr

    def dinc(self, instr):
        return self.inc(instr, 16)

    def wait(self, eng, val=None):
        eng.wait_ge(self.h, self.v if val is None else val)


def build_ffn_layer(exitv, dbg=False):
    from contextlib import ExitStack
    nc = bass.Bass("TRN2", target_bir_lowering=False, debug=False)
    es = ExitStack()

    yin = nc.dram_tensor("yin", [DM, TEXT], F32, kind="ExternalInput").ap()
    ynext = nc.dram_tensor("ynext", [DM, TOWN], F32, kind="ExternalOutput").ap()
    if exitv:
        xr = nc.dram_tensor("xr", [DM, TOWN], F32, kind="ExternalInput").ap()
        xnext = nc.dram_tensor("xnext", [DM, TOWN], F32, kind="ExternalOutput").ap()
    wqkv = nc.dram_tensor("wqkv", [DM, 3072], F32, kind="ExternalInput").ap()
    wo = nc.dram_tensor("wo", [DM, DM], F32, kind="ExternalInput").ap()
    w1p = nc.dram_tensor("w1p", [DM, 2 * NIN], F32, kind="ExternalInput").ap()
    w2p = nc.dram_tensor("w2p", [NIN, DM], F32, kind="ExternalInput").ap()
    b1p = nc.dram_tensor("b1p", [128, 2 * NINC], F32, kind="ExternalInput").ap()
    b2v = nc.dram_tensor("b2v", [128, 8], F32, kind="ExternalInput").ap()
    qsv = nc.dram_tensor("qsv", [128, 1], F32, kind="ExternalInput").ap()
    ksv = nc.dram_tensor("ksv", [128, 1], F32, kind="ExternalInput").ap()
    lngb = nc.dram_tensor("lngb", [128, 16], F32, kind="ExternalInput").ap()
    csq = nc.dram_tensor("csq", [128, SEXT], BF16, kind="ExternalInput").ap()
    ssq = nc.dram_tensor("ssq", [128, SEXT], BF16, kind="ExternalInput").ap()
    csk = nc.dram_tensor("csk", [128, SEXT], BF16, kind="ExternalInput").ap()
    ssk = nc.dram_tensor("ssk", [128, SEXT], BF16, kind="ExternalInput").ap()
    rmat = nc.dram_tensor("rmat", [128, 128], BF16, kind="ExternalInput").ap()
    ob2d = nc.dram_tensor("ob2d", [128, 2], BF16, kind="ExternalInput").ap()
    ob2td = nc.dram_tensor("ob2td", [2, 128], F32, kind="ExternalInput").ap()
    iden = nc.dram_tensor("iden", [128, 128], BF16, kind="ExternalInput").ap()
    masks = nc.dram_tensor("masks", [NQT * 128, 384], BF16, kind="ExternalInput").ap()
    ymid = nc.dram_tensor("ymid", [DM, MEXT], F32)
    if dbg:
        ymdbg = nc.dram_tensor("ymdbg", [DM, MEXT], F32, kind="ExternalOutput").ap()

    T = lambda name, shape, dt: es.enter_context(nc.sbuf_tensor(name, shape, dt))

    QSC = T("QSC", [128, 1], F32)
    KSC = T("KSC", [128, 1], F32)
    LGB = T("LGB", [128, 16], F32)
    B1 = T("B1", [128, 2 * NINC], F32)
    B2 = T("B2", [128, 8], F32)
    OB2 = T("OB2", [128, 2], BF16)
    OB2T = T("OB2T", [2, 128], F32)
    OBC = T("OBC", [1, 128], F32)
    OMB = T("OMB", [128, 1], BF16)
    RM = T("RM", [128, 128], BF16)
    IDN = T("IDN", [128, 128], BF16)
    SH = T("SH", [128, SHN], BF16)
    YS32 = T("YS32", [128, 8 * 264], F32)
    ROWSB = T("ROWSB", [2, 512], F32)
    ROWSC = T("ROWSC", [2, 512], F32)
    ROWS = T("ROWS", [1, 2048], F32)
    TA = T("TA", [128, 512], F32)
    RSS = T("RSS", [128, 512], F32)
    P32 = T("P32", [128, 384], BF16)
    PN = T("PN", [128, 384], BF16)
    PTs = T("PTs", [128, 384], BF16)
    DEN = T("DEN", [128, 1], F32)
    REC = T("REC", [128, 1], F32)
    NM = T("NM", [128, 1], F32)
    EPS1 = T("EPS1", [1, 1], F32)
    AO = T("AO", [128, 8 * 128], BF16)
    YOLD = T("YOLD", [128, 8 * 128], F32)
    XRB = T("XRB", [128, FSL], F32)
    XOUT = T("XOUT", [128, FSL], F32)
    XN = T("XN", [128, 8 * FSL], BF16)
    YMB = T("YMB", [128, 8 * FSL], BF16)
    SQB = T("SQB", [128, 8 * FSL], BF16)

    def sh(off, n):
        return SH[:, off:off + n]

    YB_O = 0
    WQKV_O = YB_O + 8 * TEXT
    WO_O = WQKV_O + 8 * 3072
    KR_O = WO_O + 8 * 1024
    VR_O = KR_O + 6 * 1024
    QS_O = VR_O + 6 * 1024
    R1_O = QS_O + 8 * 512
    R2_O = R1_O + 512
    R3_O = R2_O + 512
    TBL_O = R3_O + 512
    MSK_O = TBL_O + 4 * 512
    assert MSK_O + NQT * 384 <= SHN
    W1_O = 0
    W2_O = W1_O + 8 * 2 * NIN
    GH_O = W2_O + NINC * 1024
    AH_O = GH_O + NINC * FSL
    assert AH_O + NINC * FSL <= SHN

    ps = [es.enter_context(nc.psum_tensor(f"ps{i}", [128, 512], F32)) for i in range(4)]
    pS = es.enter_context(nc.psum_tensor("pS", [128, 384], F32))
    pT = es.enter_context(nc.psum_tensor("pT", [128, 384], BF16))
    pV = es.enter_context(nc.psum_tensor("pV", [64, 128], F32))
    pR = es.enter_context(nc.psum_tensor("pR", [128, 512], F32))

    sL = Sem(nc, es, "sL")
    sG = Sem(nc, es, "sG")
    sPE = Sem(nc, es, "sPE")
    sV = Sem(nc, es, "sV")
    sA = Sem(nc, es, "sA")
    sST = Sem(nc, es, "sST")

    es.enter_context(nc.Block())
    sync, ten, vec, act, gp = nc.sync, nc.tensor, nc.vector, nc.scalar, nc.gpsimd
    ALL = (sync, ten, vec, act, gp)

    def wv(*engs):
        for e in engs:
            sV.wait(e)

    def wpe(*engs):
        for e in engs:
            sPE.wait(e)

    def wa(*engs):
        for e in engs:
            sA.wait(e)

    def mmgroup(pdst, tiles, inc=True):
        n = len(tiles)
        for i, (lhsT, rhs) in enumerate(tiles):
            instr = ten.matmul(pdst, lhsT, rhs, start=(i == 0), stop=(i == n - 1))
        if inc:
            sPE.inc(instr)
        return instr

    def ld8(dst_off, n, t):
        dst = sh(dst_off, 8 * n).rearrange("p (c n) -> p c n", c=8)
        return gp.dma_start(out=dst, in_=t.rearrange("(c p) n -> p c n", p=128))

    sG.dinc(ld8(WQKV_O, 3072, wqkv))
    sG.dinc(ld8(WO_O, 1024, wo))
    sG.dinc(ld8(YB_O, TEXT, yin))
    sL.dinc(sync.dma_start(out=sh(MSK_O, NQT * 384).rearrange("p (q n) -> p q n", q=NQT),
                           in_=masks.rearrange("(q p) n -> p q n", p=128)))
    sL.dinc(sync.dma_start(out=RM[:], in_=rmat))
    sL.dinc(sync.dma_start(out=IDN[:], in_=iden))
    sL.dinc(sync.dma_start(out=QSC[:], in_=qsv))
    sL.dinc(sync.dma_start(out=KSC[:], in_=ksv))
    sL.dinc(sync.dma_start(out=LGB[:], in_=lngb))
    sL.dinc(sync.dma_start(out=B1[:], in_=b1p))
    sL.dinc(sync.dma_start(out=B2[:], in_=b2v))
    sV.inc(vec.memset(OBC[:], 1.0))
    sV.inc(vec.memset(EPS1[:], 1e-5))
    sV.inc(vec.memset(OMB[:], 1.0 / 1024.0))
    sL.dinc(sync.dma_start(out=OB2[:], in_=ob2d))
    sL.dinc(sync.dma_start(out=OB2T[:], in_=ob2td))
    for e in ALL:
        sL.wait(e)
        sG.wait(e)
        sV.wait(e)

    def ybv(c, a, w, shifted):
        off = YB_O + c * TEXT + a - (1 if (shifted and c >= 4) else 0)
        return SH[:, off:off + w]

    def rope_side(src, w, cs_off, ss_off, scale_col, pieces, pewait=None):
        raw = sh(R1_O, 512)[:, 0:w]
        qn = sh(R2_O, 512)[:, 0:w]
        sq = sh(R3_O, 512)[:, 0:w]
        if pewait is None:
            wpe(vec)
        else:
            sPE.wait(vec, pewait)
        sV.inc(vec.tensor_copy(raw, src[:, 0:w]))
        rec = sV.v
        wv(act)
        sA.inc(act.square(sq, raw))
        wa(ten)
        wv(ten)
        sPE.inc(ten.matmul(pR[0:2, 0:w], OB2[:], sq, start=True, stop=True))
        wpe(act)
        sA.inc(act.sqrt(ROWSB[0:2, 0:w], pR[0:2, 0:w]))
        wa(vec)
        sV.inc(vec.tensor_scalar_max(ROWSB[0:2, 0:w], ROWSB[0:2, 0:w], 1e-12))
        wv(vec)
        sV.inc(vec.reciprocal(ROWSC[0:2, 0:w], ROWSB[0:2, 0:w]))
        wv(ten)
        wa(ten)
        sPE.inc(ten.matmul(pR[0:128, 0:w], OB2T[:], ROWSC[0:2, 0:w], start=True, stop=True))
        wpe(vec)
        sV.inc(vec.scalar_tensor_tensor(qn, raw, scale_col, pR[:, 0:w], A.mult, A.mult))
        wv(ten)
        sPE.inc(ten.matmul(pR[0:128, 0:w], RM[:], qn, start=True, stop=True))
        sV.inc(vec.tensor_mul(raw, qn, sh(cs_off, 512)[:, 0:w]))
        wpe(vec)
        wv(vec)
        sV.inc(vec.tensor_mul(qn, pR[:, 0:w], sh(ss_off, 512)[:, 0:w]))
        wv(vec)
        for (doff, coff, cw) in pieces:
            sV.inc(vec.tensor_add(sh(doff, cw), raw[:, coff:coff + cw], qn[:, coff:coff + cw]))
        return rec

    taslot = {}
    slotv = {}
    qslabs = [(128, 512), (640, 512), (1152, 512), (1664, 512), (2176, 384)]

    for (a, w) in qslabs:
        tcol = a - 128
        t0 = tcol // 128
        ntile = w // 128
        wv(sync)
        for k, tb in enumerate((csq, ssq, csk, ssk)):
            sL.dinc(sync.dma_start(out=sh(TBL_O + k * 512, 512)[:, 0:w], in_=tb[:, tcol:tcol + w]))
        sL.wait(ten)
        sL.wait(vec)
        for ti in range(ntile):
            slot = (t0 + ti) % 6
            for half in range(2):
                pp = ps[half]
                sV.wait(ten, slotv.get(half, 0))
                mmgroup(pp[:, 0:512],
                        [(ybv(kc, a + ti * 128, 128, True),
                          sh(WQKV_O + kc * 3072 + 2048 + half * 512, 512)) for kc in range(8)])
                wpe(vec)
                sV.inc(vec.tensor_copy(sh(VR_O + slot * 1024 + half * 512, 512), pp[:, 0:512]))
                slotv[half] = sV.v
        pend = None
        for g in range(8):
            si = 2 + (g % 2)
            pp = ps[si]
            sV.wait(ten, slotv.get(si, 0))
            mmgroup(pp[:, 0:w],
                    [(sh(WQKV_O + kc * 3072 + g * 128, 128), ybv(kc, a, w, True)) for kc in range(8)])
            pev = sPE.v
            if pend is not None:
                gi, psrc, pw = pend
                slotv[2 + (gi % 2)] = rope_side(psrc, w, TBL_O, TBL_O + 512, QSC[:, 0:1],
                                                [(QS_O + gi * 512, 0, w)], pewait=pw)
            pend = (g, pp, pev)
        gi, psrc, pw = pend
        slotv[2 + (gi % 2)] = rope_side(psrc, w, TBL_O, TBL_O + 512, QSC[:, 0:1],
                                        [(QS_O + gi * 512, 0, w)], pewait=pw)
        pend = None
        for g in range(8):
            si = 2 + (g % 2)
            pp = ps[si]
            sV.wait(ten, slotv.get(si, 0))
            mmgroup(pp[:, 0:w],
                    [(sh(WQKV_O + kc * 3072 + 1024 + g * 128, 128), ybv(kc, a, w, True)) for kc in range(8)])
            pev = sPE.v
            if pend is not None:
                gi, psrc, pw = pend
                pieces = [(KR_O + ((t0 + ti) % 6) * 1024 + gi * 128, ti * 128, 128) for ti in range(ntile)]
                slotv[2 + (gi % 2)] = rope_side(psrc, w, TBL_O + 1024, TBL_O + 1536, KSC[:, 0:1],
                                                pieces, pewait=pw)
            pend = (g, pp, pev)
        gi, psrc, pw = pend
        pieces = [(KR_O + ((t0 + ti) % 6) * 1024 + gi * 128, ti * 128, 128) for ti in range(ntile)]
        slotv[2 + (gi % 2)] = rope_side(psrc, w, TBL_O + 1024, TBL_O + 1536, KSC[:, 0:1],
                                        pieces, pewait=pw)
        jlo = max(0, t0 - 2)
        jhi = min(NQT - 1, t0 + ntile - 3)
        for j in range(jlo, jhi + 1):
            wv(sync)
            for c in range(8):
                sL.dinc(sync.dma_start(out=YOLD[:, c * 128:(c + 1) * 128],
                                       in_=yin[c * 128:(c + 1) * 128, 384 + 128 * j:384 + 128 * (j + 1)]))
            qcol = 384 + 128 * j - a
            for h in range(16):
                g2 = h // 2
                hr = (h % 2) * 64
                wv(ten)
                wa(ten)
                for ti2 in range(3):
                    slot = (j + ti2) % 6
                    ten.matmul(pS[:, ti2 * 128:(ti2 + 1) * 128],
                               sh(QS_O + g2 * 512 + qcol, 128)[hr:hr + 64, :],
                               sh(KR_O + slot * 1024 + g2 * 128, 128)[hr:hr + 64, :],
                               start=(ti2 == 0), stop=False)
                sPE.inc(ten.matmul(pS[:, 0:384], IDN[:], sh(MSK_O + j * 384, 384), start=False, stop=True))
                wpe(vec)
                wa(vec)
                sV.inc(vec.reduce_max(NM[:], pS[:, 0:384], axis=AX.X, negate=True))
                wv(act)
                sA.inc(act.activation(out=P32[:], in_=pS[:, 0:384], func=AF.Exp,
                                      bias=NM[:, 0:1], scale=1.0, accum_out=DEN[:]))
                wa(vec)
                sV.inc(vec.reciprocal(REC[:], DEN[:]))
                wv(vec)
                wpe(vec)
                sV.inc(vec.tensor_scalar_mul(PN[:], P32[:], REC[:, 0:1]))
                wv(ten)
                for ti2 in range(3):
                    instr = ten.transpose(pT[:, ti2 * 128:(ti2 + 1) * 128], PN[:, ti2 * 128:(ti2 + 1) * 128], IDN[:])
                sPE.inc(instr)
                wpe(vec)
                sV.inc(vec.tensor_copy(PTs[:], pT[:, 0:384]))
                wv(ten)
                mmgroup(pV[:, 0:128],
                        [(sh(VR_O + ((j + ti2) % 6) * 1024 + h * 64, 64), PTs[:, ti2 * 128:(ti2 + 1) * 128])
                         for ti2 in range(3)])
                wpe(vec)
                sV.inc(vec.tensor_copy(AO[hr:hr + 64, g2 * 128:(g2 + 1) * 128], pV[:, 0:128]))
            sL.wait(vec)
            for oc in range(8):
                pp = ps[oc % 2]
                slot = oc % 4
                wv(ten)
                mmgroup(pp[:, 0:128],
                        [(sh(WO_O + kc * 1024 + oc * 128, 128), AO[:, kc * 128:(kc + 1) * 128]) for kc in range(8)])
                wpe(vec)
                if slot in taslot:
                    sST.wait(vec)
                sV.inc(vec.scalar_tensor_tensor(TA[:, slot * 128:slot * 128 + 128], pp[:, 0:128], 1.0,
                                                YOLD[:, oc * 128:(oc + 1) * 128], A.mult, A.add))
                wv(sync)
                sST.dinc(sync.dma_start(out=ymid[oc * 128:(oc + 1) * 128, 128 * j:128 * (j + 1)], in_=TA[:, slot * 128:slot * 128 + 128]))
                taslot[slot] = sST.v

    sPE.wait(gp)
    sV.wait(gp)
    sA.wait(gp)
    sG.dinc(ld8(W1_O, 2 * NIN, w1p))
    sG.dinc(gp.dma_start(out=sh(W2_O, NINC * 1024).rearrange("p (c n) -> p c n", c=NINC),
                         in_=w2p.rearrange("(c p) n -> p c n", p=128)))
    for e in (ten, vec, act, sync):
        sG.wait(e)
    sST.wait(sync)

    for s in range(NFS):
        c0 = 128 + FSL * s
        wv(sync)
        wa(sync)
        for c in range(8):
            sL.dinc(sync.dma_start(out=YS32[:, c * 264:c * 264 + FSL + 1],
                                   in_=ymid[c * 128:(c + 1) * 128, c0 - 1:c0 + FSL]))
        sL.wait(vec)
        wpe(vec)
        for half in range(2):
            off = 1 - half
            src = YS32[:].rearrange("p (c n) -> p c n", c=8)[:, 4 * half:4 * half + 4, off:off + FSL]
            dst = YMB[:].rearrange("p (c n) -> p c n", c=8)[:, 4 * half:4 * half + 4, 0:FSL]
            sV.inc(vec.tensor_copy(dst, src))
        wv(act, ten)
        sA.inc(act.square(SQB[:], YMB[:]))
        wa(ten)
        wv(ten)
        for c in range(8):
            ten.matmul(pR[0:1, 0:FSL], OMB[:], YMB[:, c * FSL:(c + 1) * FSL],
                       start=(c == 0), stop=(c == 7))
        for c in range(8):
            instr = ten.matmul(pR[0:1, FSL:2 * FSL], OMB[:], SQB[:, c * FSL:(c + 1) * FSL],
                               start=(c == 0), stop=(c == 7))
        sPE.inc(instr)
        wpe(vec)
        sV.inc(vec.tensor_copy(ROWS[0:1, 0:512], pR[0:1, 0:512]))
        wv(vec)
        sV.inc(vec.tensor_mul(ROWS[0:1, 512:512 + FSL], ROWS[0:1, 0:FSL], ROWS[0:1, 0:FSL]))
        wv(vec)
        sV.inc(vec.tensor_sub(ROWS[0:1, 512:512 + FSL], ROWS[0:1, FSL:2 * FSL], ROWS[0:1, 512:512 + FSL]))
        wv(act)
        sA.inc(act.activation(out=ROWS[0:1, 768:768 + FSL], in_=ROWS[0:1, 512:512 + FSL], func=AF.Sqrt, bias=EPS1[0:1, 0:1], scale=1.0))
        wa(vec)
        sV.inc(vec.reciprocal(ROWS[0:1, 512:512 + FSL], ROWS[0:1, 768:768 + FSL]))
        wv(ten)
        ten.matmul(pR[:, 0:FSL], OBC[:], ROWS[0:1, 0:FSL], start=True, stop=True)
        sPE.inc(ten.matmul(pR[:, FSL:2 * FSL], OBC[:], ROWS[0:1, 512:512 + FSL], start=True, stop=True))
        wpe(vec)
        sV.inc(vec.tensor_copy(RSS[:, 0:FSL], pR[:, FSL:2 * FSL]))
        for c in range(8):
            off = 1 if c < 4 else 0
            ysrc = YS32[:, c * 264 + off:c * 264 + off + FSL]
            wv(vec)
            if ("w2", 1) in taslot:
                sST.wait(vec)
            sV.inc(vec.tensor_sub(TA[:, FSL:2 * FSL], ysrc, pR[:, 0:FSL]))
            wv(vec)
            sV.inc(vec.scalar_tensor_tensor(TA[:, FSL:2 * FSL], TA[:, FSL:2 * FSL], LGB[:, c:c + 1], RSS[:, 0:FSL], A.mult, A.mult))
            wv(vec)
            wpe(vec)
            sV.inc(vec.tensor_scalar_add(XN[:, c * FSL:(c + 1) * FSL], TA[:, FSL:2 * FSL], LGB[:, 8 + c:9 + c]))
        for oc in range(2 * NINC):
            pp = ps[oc % 2]
            wv(ten)
            wa(ten)
            mmgroup(pp[:, 0:FSL],
                    [(sh(W1_O + kc * 2 * NIN + oc * 128, 128), XN[:, kc * FSL:(kc + 1) * FSL]) for kc in range(8)])
            if oc < NINC:
                wpe(vec)
                sV.inc(vec.tensor_scalar_add(sh(AH_O + oc * FSL, FSL), pp[:, 0:FSL], B1[:, oc:oc + 1]))
            else:
                wpe(act)
                sA.inc(act.activation(out=sh(GH_O + (oc - NINC) * FSL, FSL), in_=pp[:, 0:FSL],
                                      func=AF.Gelu, bias=B1[:, oc:oc + 1], scale=1.0))
        wv(vec)
        wa(vec)
        sV.inc(vec.tensor_mul(sh(AH_O, NINC * FSL), sh(AH_O, NINC * FSL), sh(GH_O, NINC * FSL)))
        for oc in range(8):
            pp = ps[2 + oc % 2]
            wv(ten)
            mmgroup(pp[:, 0:FSL],
                    [(sh(W2_O + kc * 1024 + oc * 128, 128), sh(AH_O + kc * FSL, FSL)) for kc in range(NINC)])
            wpe(vec)
            slot2 = oc % 2
            tv = TA[:, slot2 * FSL:(slot2 + 1) * FSL]
            if ("w2", slot2) in taslot:
                sST.wait(vec)
            ysrc = YS32[:, oc * 264 + 1:oc * 264 + 1 + FSL]
            sV.inc(vec.scalar_tensor_tensor(tv, pp[:, 0:FSL], 1.0, ysrc, A.mult, A.add))
            wv(vec)
            sV.inc(vec.tensor_scalar_add(tv, tv, B2[:, oc:oc + 1]))
            wv(sync)
            sST.dinc(sync.dma_start(out=ynext[oc * 128:(oc + 1) * 128, FSL * s:FSL * (s + 1)], in_=tv))
            taslot[("w2", slot2)] = sST.v
            if exitv:
                sL.dinc(sync.dma_start(out=XRB[:, 0:FSL], in_=xr[oc * 128:(oc + 1) * 128, FSL * s:FSL * (s + 1)]))
                sL.wait(vec)
                if ("xo",) in taslot:
                    sST.wait(vec)
                sV.inc(vec.tensor_add(XOUT[:, 0:FSL], tv, XRB[:, 0:FSL]))
                wv(sync)
                sST.dinc(sync.dma_start(out=xnext[oc * 128:(oc + 1) * 128, FSL * s:FSL * (s + 1)], in_=XOUT[:, 0:FSL]))
                taslot[("xo",)] = sST.v

    if dbg:
        sST.wait(gp)
        for c in range(8):
            sST.dinc(gp.dma_start(out=ymdbg[c * 128:(c + 1) * 128, :], in_=ymid[c * 128:(c + 1) * 128, :]))
    sST.wait(sync)
    sST.wait(gp)
    es.close()
    return nc


def ffn_tables(base):
    t = np.arange(base - 384, base + 2048, dtype=np.float64)
    t = np.clip(t, 0, None)
    halfd = np.arange(0, DH_ if False else 64, 2, dtype=np.float64)
    inv_freq = 1.0 / 10000.0 ** (halfd / 64.0)
    freqs = t[:, None] * inv_freq[None, :]
    freqs = np.concatenate([freqs, freqs], -1)
    base_scale = (halfd + 0.4 * 64) / (1.4 * 64)
    power = (t - L // 2) / (256 // 2)
    scale = base_scale[None, :] ** power[:, None]
    scale = np.concatenate([scale, scale], -1)
    cos, sin = np.cos(freqs), np.sin(freqs)
    csq = (cos * scale * 8.0).T
    ssq = (sin * scale * 8.0).T
    csk = (cos / scale).T
    ssk = (sin / scale).T
    out = []
    for tab in (csq, ssq, csk, ssk):
        out.append(np.tile(tab, (2, 1)).astype(BF))
    return out


def ffn_masks(base):
    j = np.arange(NQT)[:, None, None]
    p = np.arange(128)[None, :, None]
    k = np.arange(384)[None, None, :]
    tq = base - 128 + 128 * j + p
    tk = base - 384 + 128 * j + k
    valid = (tk >= 0) & (tk <= tq) & (tq - tk <= 256) & (tq >= 0)
    return np.where(valid, np.float32(0), np.float32(-1e30)).astype(BF)


def rot_mat():
    m = np.zeros((128, 128), np.float32)
    for h in range(2):
        for i in range(32):
            m[h * 64 + 32 + i, h * 64 + i] = -1.0
            m[h * 64 + i, h * 64 + 32 + i] = 1.0
    return m.astype(BF)


def f32c(x):
    return np.ascontiguousarray(x, dtype=np.float32)


def ffn_weight_maps(inp, li):
    Wqkv = f32c(inp["L0_ffn_Wqkv"][li])
    Wo = f32c(inp["L0_ffn_Wo"][li])
    W1 = inp["L0_ffn_W1"][li]
    b1 = inp["L0_ffn_b1"][li]
    W2 = inp["L0_ffn_W2"][li]
    b2 = inp["L0_ffn_b2"][li]
    NI = 2730
    W1p = np.zeros((DM, 2 * NIN), np.float32)
    W1p[:, :NI] = W1[:, :NI]
    W1p[:, NIN:NIN + NI] = W1[:, NI:]
    b1pad = np.zeros((2 * NIN,), np.float32)
    b1pad[:NI] = b1[:NI]
    b1pad[NIN:NIN + NI] = b1[NI:]
    W2p = np.zeros((NIN, DM), np.float32)
    W2p[:NI] = W2
    qs = np.tile(inp["L0_ffn_q_scale"][li], 2).astype(np.float32)[:, None]
    ks = np.tile(inp["L0_ffn_k_scale"][li], 2).astype(np.float32)[:, None]
    lngb = np.stack([inp["L0_ffn_ln_g"][li].reshape(8, 128),
                     inp["L0_ffn_ln_b"][li].reshape(8, 128)], 0).reshape(16, 128).T
    return dict(
        wqkv=Wqkv, wo=Wo,
        w1p=W1p, w2p=W2p,
        b1p=np.ascontiguousarray(b1pad.reshape(2 * NINC, 128).T),
        b2v=np.ascontiguousarray(b2.reshape(8, 128).T),
        qsv=f32c(qs), ksv=f32c(ks),
        lngb=f32c(lngb),
        rmat=rot_mat(), iden=np.eye(128, dtype=BF),
        ob2d=np.kron(np.eye(2), np.ones((64, 1))).astype(BF),
        ob2td=np.kron(np.eye(2), np.ones((1, 64))).astype(np.float32),
    )


def run_ffn_layer(progs, y_full, li, inp, wcache, exitv=False, xr_full=None):
    key = ("layer", li)
    if key not in wcache:
        wcache[key] = ffn_weight_maps(inp, li)
    wmaps = wcache[key]
    if "tbl" not in wcache:
        wcache["tbl"] = [ffn_tables(TOWN * c) for c in range(NCORES)]
        wcache["msk"] = [np.ascontiguousarray(ffn_masks(TOWN * c).reshape(NQT * 128, 384)) for c in range(NCORES)]
    yext = np.zeros((NCORES, DM, TEXT), np.float32)
    for c in range(NCORES):
        b = TOWN * c
        lo = max(0, b - HALO)
        yext[c, :, HALO - (b - lo):] = y_full[:, lo:b + TOWN]
    in_maps = []
    for c in range(NCORES):
        m = dict(yin=yext[c], **wmaps)
        t4 = wcache["tbl"][c]
        m.update(csq=t4[0], ssq=t4[1], csk=t4[2], ssk=t4[3], masks=wcache["msk"][c])
        if exitv:
            m["xr"] = np.ascontiguousarray(xr_full[:, TOWN * c:TOWN * (c + 1)])
        in_maps.append(m)
    nc = progs["layer_exit" if exitv else "layer"]
    res = run_bass_kernel_spmd(nc, in_maps, core_ids=list(range(NCORES)))
    ynext = np.concatenate([np.asarray(r["ynext"]) for r in res.results], axis=1)
    if exitv:
        xnext = np.concatenate([np.asarray(r["xnext"]) for r in res.results], axis=1)
        return ynext, xnext
    return ynext, None


DH_ = 64

if __name__ == "__main__":
    nc = build_ffn_layer(False)
    print("built", len(list(nc.all_instructions())), "instructions")


def build_entry():
    from contextlib import ExitStack
    nc = bass.Bass("TRN2", target_bir_lowering=False, debug=False)
    es = ExitStack()
    xa = nc.dram_tensor("xa", [DM, TOWN], F32, kind="ExternalInput").ap()
    xb = nc.dram_tensor("xb", [DM, TOWN], F32, kind="ExternalInput").ap()
    x1own = nc.dram_tensor("x1own", [DM, TOWN], F32, kind="ExternalOutput").ap()
    y0own = nc.dram_tensor("y0own", [DM, TOWN], F32, kind="ExternalOutput").ap()
    ob1f = nc.dram_tensor("ob1f", [128, 1], F32, kind="ExternalInput").ap()
    obcf = nc.dram_tensor("obcf", [1, 128], F32, kind="ExternalInput").ap()

    T = lambda name, shape, dt: es.enter_context(nc.sbuf_tensor(name, shape, dt))
    XA = T("XA", [128, 8 * 512], F32)
    XB = T("XB", [128, 8 * 512], F32)
    SQ = T("SQ", [128, 8 * 512], F32)
    ROWS = T("ROWS", [1, 2048], F32)
    RSS = T("RSS", [128, 512], F32)
    OB1F = T("OB1F", [128, 1], F32)
    OBCF = T("OBCF", [1, 128], F32)
    EPS1 = T("EPS1", [1, 1], F32)
    TO = T("TO", [128, 2 * 512], F32)
    pR = es.enter_context(nc.psum_tensor("pR", [128, 512], F32))
    pQ = es.enter_context(nc.psum_tensor("pQ", [128, 512], F32))

    sL = Sem(nc, es, "sL")
    sPE = Sem(nc, es, "sPE")
    sV = Sem(nc, es, "sV")
    sA = Sem(nc, es, "sA")
    sST = Sem(nc, es, "sST")
    es.enter_context(nc.Block())
    sync, ten, vec, act, gp = nc.sync, nc.tensor, nc.vector, nc.scalar, nc.gpsimd

    sL.dinc(sync.dma_start(out=OB1F[:], in_=ob1f))
    sL.dinc(sync.dma_start(out=OBCF[:], in_=obcf))
    sV.inc(vec.memset(EPS1[:], 1e-5))
    for e in (sync, ten, vec, act):
        sL.wait(e)
        sV.wait(e)

    tsl = {}
    for s in range(4):
        c0 = 512 * s
        sV.wait(sync)
        sA.wait(sync)
        sPE.wait(sync)
        for c in range(8):
            sL.dinc(sync.dma_start(out=XA[:, c * 512:(c + 1) * 512], in_=xa[c * 128:(c + 1) * 128, c0:c0 + 512]))
            sL.dinc(sync.dma_start(out=XB[:, c * 512:(c + 1) * 512], in_=xb[c * 128:(c + 1) * 128, c0:c0 + 512]))
        sL.wait(vec)
        if "x1" in tsl:
            sST.wait(vec)
        sV.inc(vec.tensor_add(XA[:], XA[:], XB[:]))
        sV.wait(sync)
        for c in range(8):
            sST.dinc(sync.dma_start(out=x1own[c * 128:(c + 1) * 128, c0:c0 + 512], in_=XA[:, c * 512:(c + 1) * 512]))
        sync.nop()
        tsl["x1"] = sST.v
        sV.wait(act)
        sA.inc(act.square(SQ[:], XA[:]))
        sV.wait(ten)
        sA.wait(ten)
        for c in range(8):
            ten.matmul(pR[0:1, 0:512], OB1F[:], XA[:, c * 512:(c + 1) * 512], start=(c == 0), stop=(c == 7))
        for c in range(8):
            instr = ten.matmul(pQ[0:1, 0:512], OB1F[:], SQ[:, c * 512:(c + 1) * 512], start=(c == 0), stop=(c == 7))
        sPE.inc(instr)
        sPE.wait(vec)
        sV.inc(vec.tensor_scalar_mul(ROWS[0:1, 0:512], pR[0:1, 0:512], 1.0 / 1024.0))
        sV.wait(vec)
        sV.inc(vec.tensor_scalar_mul(ROWS[0:1, 512:1024], pQ[0:1, 0:512], 1.0 / 1024.0))
        sV.wait(vec)
        sV.inc(vec.tensor_mul(ROWS[0:1, 1024:1536], ROWS[0:1, 0:512], ROWS[0:1, 0:512]))
        sV.wait(vec)
        sV.inc(vec.tensor_sub(ROWS[0:1, 1024:1536], ROWS[0:1, 512:1024], ROWS[0:1, 1024:1536]))
        sV.wait(act)
        sA.inc(act.activation(out=ROWS[0:1, 1536:2048], in_=ROWS[0:1, 1024:1536], func=AF.Sqrt,
                              bias=EPS1[0:1, 0:1], scale=1.0))
        sA.wait(vec)
        sV.inc(vec.reciprocal(ROWS[0:1, 1024:1536], ROWS[0:1, 1536:2048]))
        sV.wait(ten)
        ten.matmul(pR[:, 0:512], OBCF[:], ROWS[0:1, 0:512], start=True, stop=True)
        instr = ten.matmul(pQ[:, 0:512], OBCF[:], ROWS[0:1, 1024:1536], start=True, stop=True)
        sPE.inc(instr)
        sPE.wait(vec)
        sV.inc(vec.tensor_copy(RSS[:, 0:512], pQ[:, 0:512]))
        for c in range(8):
            slot = c % 2
            tv = TO[:, slot * 512:(slot + 1) * 512]
            sV.wait(vec)
            if ("y", slot) in tsl:
                sST.wait(vec)
            sV.inc(vec.tensor_sub(tv, XA[:, c * 512:(c + 1) * 512], pR[:, 0:512]))
            sV.wait(vec)
            sV.inc(vec.tensor_mul(tv, tv, RSS[:, 0:512]))
            sV.wait(sync)
            sST.dinc(sync.dma_start(out=y0own[c * 128:(c + 1) * 128, c0:c0 + 512], in_=tv))
            sync.nop()
            tsl[("y", slot)] = sST.v

    sST.wait(sync)
    es.close()
    return nc


def entry_maps():
    return dict(ob1f=np.ones((128, 1), np.float32), obcf=np.ones((1, 128), np.float32))


def run_entry(progs, xa_full, xb_full):
    in_maps = []
    em = entry_maps()
    for c in range(NCORES):
        sl = slice(TOWN * c, TOWN * (c + 1))
        in_maps.append(dict(xa=np.ascontiguousarray(xa_full[:, sl]),
                            xb=np.ascontiguousarray(xb_full[:, sl]), **em))
    res = run_bass_kernel_spmd(progs["entry"], in_maps, core_ids=list(range(NCORES)))
    x1 = np.concatenate([np.asarray(r["x1own"]) for r in res.results], axis=1)
    y0 = np.concatenate([np.asarray(r["y0own"]) for r in res.results], axis=1)
    return x1, y0


def run_ffn_block(progs, x_full, o_full, inp, wcache):
    x1, y = run_entry(progs, x_full, o_full)
    for li in range(6):
        exitv = (li == 5)
        y, xn = run_ffn_layer(progs, y, li, inp, wcache, exitv=exitv, xr_full=x1 if exitv else None)
    return xn



TQ = 256
NQ = TOWN // TQ
NSER = 16 * (TQ + 1)


def build_mamba(phase_b, upto=9):
    from contextlib import ExitStack
    nc = bass.Bass("TRN2", target_bir_lowering=False, debug=False)
    es = ExitStack()
    T = lambda name, shape, dt: es.enter_context(nc.sbuf_tensor(name, shape, dt))

    if not phase_b:
        x2e = nc.dram_tensor("x2e", [DM, TOWN + 4], F32, kind="ExternalInput").ap()
        wiT = nc.dram_tensor("wiT", [DM, 4096], F32, kind="ExternalInput").ap()
        convw = nc.dram_tensor("convw", [128, 16 * 4], F32, kind="ExternalInput").ap()
        convb = nc.dram_tensor("convb", [128, 16], F32, kind="ExternalInput").ap()
        wxT = nc.dram_tensor("wxT", [2048, 33], F32, kind="ExternalInput").ap()
        dtw = nc.dram_tensor("dtw", [128, 16], F32, kind="ExternalInput").ap()
        dtb = nc.dram_tensor("dtb", [128, 16], F32, kind="ExternalInput").ap()
        alog = nc.dram_tensor("alog", [2048, 16], F32, kind="ExternalInput").ap()
        o_dt = nc.dram_tensor("o_dt", [2048, TOWN], F32, kind="ExternalOutput").ap()
        o_xc = nc.dram_tensor("o_xc", [2048, TOWN], BF16, kind="ExternalOutput").ap()
        o_z = nc.dram_tensor("o_z", [2048, TOWN], BF16, kind="ExternalOutput").ap()
        o_bc = nc.dram_tensor("o_bc", [32, TOWN], F32, kind="ExternalOutput").ap()
        o_hf = nc.dram_tensor("o_hf", [2048, 16], F32, kind="ExternalOutput").ap()
        o_sf = nc.dram_tensor("o_sf", [2048, 1], F32, kind="ExternalOutput").ap()
        dtin, xcin = o_dt, o_xc
    else:
        i_dt = nc.dram_tensor("i_dt", [2048, TOWN], F32, kind="ExternalInput").ap()
        i_xc = nc.dram_tensor("i_xc", [2048, TOWN], BF16, kind="ExternalInput").ap()
        i_z = nc.dram_tensor("i_z", [2048, TOWN], BF16, kind="ExternalInput").ap()
        i_bc = nc.dram_tensor("i_bc", [32, TOWN], F32, kind="ExternalInput").ap()
        i_hq = nc.dram_tensor("i_hq", [2048, 7 * 16], F32, kind="ExternalInput").ap()
        i_sq = nc.dram_tensor("i_sq", [2048, 7], F32, kind="ExternalInput").ap()
        alog = nc.dram_tensor("alog", [2048, 16], F32, kind="ExternalInput").ap()
        dvec = nc.dram_tensor("dvec", [2048, 1], F32, kind="ExternalInput").ap()
        x2own = nc.dram_tensor("x2own", [DM, TOWN], F32, kind="ExternalInput").ap()
        wopT = nc.dram_tensor("wopT", [2048, DM], F32, kind="ExternalInput").ap()
        x3own = nc.dram_tensor("x3own", [DM, TOWN], F32, kind="ExternalOutput").ap()
        dtin, xcin = i_dt, i_xc
    ob1f = nc.dram_tensor("ob1f", [128, 1], F32, kind="ExternalInput").ap()
    obcf = nc.dram_tensor("obcf", [1, 128], F32, kind="ExternalInput").ap()

    sL = Sem(nc, es, "sL")
    sG = Sem(nc, es, "sG")
    sPE = Sem(nc, es, "sPE")
    sV = Sem(nc, es, "sV")
    sA = Sem(nc, es, "sA")
    sST = Sem(nc, es, "sST")

    ABUF = T("ABUF", [128, NSER], F32)
    BBUF = T("BBUF", [128, NSER], BF16)
    HBUF = T("HBUF", [128, NSER], F32)
    BRE = T("BRE", [128, 16 * TQ], BF16)
    UB = T("UB", [128, TQ], F32)
    DTQ = T("DTQ", [128, TQ], F32)
    XCQ = T("XCQ", [128, TQ], BF16)
    AC = T("AC", [128, 16], F32)
    CARRY = T("CARRY", [128, 16 * 16], F32)
    SFA = T("SFA", [128, 16], F32)
    BCROW = T("BCROW", [1, 16 * TQ], F32)
    OB1F = T("OB1F", [128, 1], F32)
    OBCF = T("OBCF", [1, 128], F32)
    EPS1 = T("EPS1", [1, 1], F32)
    ONE1F = T("ONE1F", [128, 1], F32)

    if not phase_b:
        WI1 = T("WI1", [128, 2 * 1024], BF16)
        WX = T("WX", [128, 16 * 33], BF16)
        XLN = T("XLN", [128, 8 * 2052], BF16)
        CW = T("CW", [128, 16 * 4], F32)
        CB = T("CB", [128, 16], F32)
        DTW = T("DTW", [128, 16], F32)
        DTB = T("DTB", [128, 16], F32)
        XSL = T("XSL", [128, 8 * 520], F32)
        SQF = T("SQF", [128, 8 * 520], F32)
        ROWS = T("ROWS", [1, 2080], F32)
        RSS = T("RSS", [128, 520], F32)
        XT32 = T("XT32", [128, 2052], F32)
        DTR = T("DTR", [128, 512], F32)
        TB = T("TB", [128, 2048], F32)
        BC33 = T("BC33", [33, 512], F32)
        XCF = T("XCF", [128, 2048], BF16)
    else:
        ZQ = T("ZQ", [128, TQ], BF16)
        CRE = T("CRE", [128, 16 * TQ], BF16)
        CCROW = T("CCROW", [1, 16 * TQ], F32)
        HQ = T("HQ", [128, 7 * 16], F32)
        SQC = T("SQC", [128, 7], F32)
        TM = T("TM", [128, 16], F32)
        YG = T("YG", [128, 16 * 2048], BF16)
        PB = T("PB", [128, 16 * TQ], BF16)
        YQ = T("YQ", [128, TQ], F32)
        DV = T("DV", [128, 16], F32)
        X2S = T("X2S", [128, 512], F32)
        TO = T("TO", [128, 2 * 512], F32)
        WOPB = T("WOPB", [128, 2 * 128], BF16)

    pR = es.enter_context(nc.psum_tensor("pR", [128, 512], F32))
    pQ = es.enter_context(nc.psum_tensor("pQ", [128, 512], F32))
    pA = es.enter_context(nc.psum_tensor("pA", [128, 512], F32))
    pB = es.enter_context(nc.psum_tensor("pB", [128, 512], F32))

    es.enter_context(nc.Block())
    sync, ten, vec, act, gp = nc.sync, nc.tensor, nc.vector, nc.scalar, nc.gpsimd

    def wv(*engs):
        for e in engs: sV.wait(e)

    def wpe(*engs):
        for e in engs: sPE.wait(e)

    def wa(*engs):
        for e in engs: sA.wait(e)

    tsl = {}
    sL.dinc(sync.dma_start(out=OB1F[:], in_=ob1f))
    sL.dinc(sync.dma_start(out=OBCF[:], in_=obcf))
    sV.inc(vec.memset(EPS1[:], 1e-5))
    sV.inc(vec.memset(ONE1F[:], 1.0))
    sV.inc(vec.memset(CARRY[:], 0.0))
    sV.inc(vec.memset(SFA[:], 0.0))
    sV.inc(vec.memset(ABUF[:], 0.0))
    sV.inc(vec.memset(BBUF[:], 0.0))
    sV.inc(vec.memset(BRE[:], 0.0))
    sV.inc(vec.memset(BCROW[:], 0.0))

    if not phase_b:
        sG.dinc(gp.dma_start(out=WX[:].rearrange("p (c n) -> p c n", c=16),
                             in_=wxT.rearrange("(c p) n -> p c n", p=128)))
        sL.dinc(sync.dma_start(out=CW[:], in_=convw))
        sL.dinc(sync.dma_start(out=CB[:], in_=convb))
        sL.dinc(sync.dma_start(out=DTW[:], in_=dtw))
        sL.dinc(sync.dma_start(out=DTB[:], in_=dtb))
        for e in (sync, ten, vec, act, gp):
            sL.wait(e); sG.wait(e); sV.wait(e)

        sV.inc(vec.memset(XSL[:], 0.0))
        sV.inc(vec.memset(XLN[:], 0.0))
        sV.inc(vec.memset(XT32[:], 0.0))
        wv(vec, act, ten, sync)
        for (c0, w) in [(0, 512), (512, 512), (1024, 512), (1536, 512), (2048, 4)]:
            wv(sync); wa(sync); wpe(sync)
            for c in range(8):
                sL.dinc(sync.dma_start(out=XSL[:, c * 520:c * 520 + w],
                                       in_=x2e[c * 128:(c + 1) * 128, c0:c0 + w]))
            sL.wait(act); sL.wait(ten); sL.wait(vec)
            sA.inc(act.square(SQF[:], XSL[:]))
            wa(ten); wv(ten)
            for c in range(8):
                ten.matmul(pR[0:1, 0:w], OB1F[:], XSL[:, c * 520:c * 520 + w], start=(c == 0), stop=(c == 7))
            for c in range(8):
                instr = ten.matmul(pQ[0:1, 0:w], OB1F[:], SQF[:, c * 520:c * 520 + w], start=(c == 0), stop=(c == 7))
            sPE.inc(instr)
            wpe(vec)
            sV.inc(vec.tensor_scalar_mul(ROWS[0:1, 0:w], pR[0:1, 0:w], 1.0 / 1024.0))
            wv(vec)
            sV.inc(vec.tensor_scalar_mul(ROWS[0:1, 520:520 + w], pQ[0:1, 0:w], 1.0 / 1024.0))
            wv(vec)
            sV.inc(vec.tensor_mul(ROWS[0:1, 1040:1040 + w], ROWS[0:1, 0:w], ROWS[0:1, 0:w]))
            wv(vec)
            sV.inc(vec.tensor_sub(ROWS[0:1, 1040:1040 + w], ROWS[0:1, 520:520 + w], ROWS[0:1, 1040:1040 + w]))
            wv(act)
            sA.inc(act.activation(out=ROWS[0:1, 1560:1560 + w], in_=ROWS[0:1, 1040:1040 + w],
                                  func=AF.Sqrt, bias=EPS1[0:1, 0:1], scale=1.0))
            wa(vec)
            sV.inc(vec.reciprocal(ROWS[0:1, 1040:1040 + w], ROWS[0:1, 1560:1560 + w]))
            wv(ten)
            ten.matmul(pR[:, 0:w], OBCF[:], ROWS[0:1, 0:w], start=True, stop=True)
            instr = ten.matmul(pQ[:, 0:w], OBCF[:], ROWS[0:1, 1040:1040 + w], start=True, stop=True)
            sPE.inc(instr)
            wpe(vec)
            sV.inc(vec.tensor_copy(RSS[:, 0:w], pQ[:, 0:w]))
            for c in range(8):
                wv(vec)
                sV.inc(vec.tensor_sub(XT32[:, 0:w], XSL[:, c * 520:c * 520 + w], pR[:, 0:w]))
                wv(vec)
                sV.inc(vec.tensor_mul(XLN[:, c * 2052 + c0:c * 2052 + c0 + w], XT32[:, 0:w], RSS[:, 0:w]))

        for oc in (range(32) if upto >= 2 else []):
            wpe(gp)
            for kc in range(8):
                sG.dinc(gp.dma_start(out=WI1[:, (oc % 2) * 1024 + kc * 128:(oc % 2) * 1024 + kc * 128 + 128],
                                     in_=wiT[kc * 128:(kc + 1) * 128, oc * 128:(oc + 1) * 128]))
            sG.wait(ten)
            for ts2 in range(5):
                a0 = 512 * ts2
                w = min(512, 2052 - a0)
                pp = (pA, pB)[ts2 % 2]
                wv(ten)
                instr = None
                for kc in range(8):
                    instr = ten.matmul(pp[:, 0:w],
                                       WI1[:, (oc % 2) * 1024 + kc * 128:(oc % 2) * 1024 + kc * 128 + 128],
                                       XLN[:, kc * 2052 + a0:kc * 2052 + a0 + w],
                                       start=(kc == 0), stop=(kc == 7))
                sPE.inc(instr)
                wpe(vec)
                wa(vec)
                sV.inc(vec.tensor_copy(XT32[:, a0:a0 + w], pp[:, 0:w]))
            cc = oc % 16
            if oc < 16:
                wv(vec)
                wa(vec)
                sV.inc(vec.tensor_single_scalar(TB[:, 0:2048], XT32[:, 1:1 + 2048], CW[:, 4 * cc:4 * cc + 1], A.mult))
                for jj in range(1, 4):
                    wv(vec)
                    sV.inc(vec.scalar_tensor_tensor(TB[:, 0:2048], XT32[:, jj + 1:jj + 1 + 2048],
                                                    CW[:, 4 * cc + jj:4 * cc + jj + 1], TB[:, 0:2048], A.mult, A.add))
                wv(act)
                if "xcf" in tsl:
                    sST.wait(act)
                sA.inc(act.activation(out=XCF[:, 0:2048], in_=TB[:, 0:2048], func=AF.Silu,
                                      bias=CB[:, cc:cc + 1], scale=1.0))
                wa(sync)
                sST.dinc(sync.dma_start(out=o_xc[cc * 128:(cc + 1) * 128, :], in_=XCF[:, 0:2048]))
                sync.nop()
                tsl["xcf"] = sST.v
            else:
                wv(act)
                if "xcf" in tsl:
                    sST.wait(act)
                sA.inc(act.activation(out=XCF[:, 0:2048], in_=XT32[:, 4:4 + 2048], func=AF.Identity, scale=1.0))
                wa(sync)
                sST.dinc(sync.dma_start(out=o_z[cc * 128:(cc + 1) * 128, :], in_=XCF[:, 0:2048]))
                sync.nop()
                tsl["xcf"] = sST.v

        sST.wait(sync)
        for ts2 in (range(4) if upto >= 3 else []):
            a0 = 512 * ts2
            wpe(sync); wv(sync); wa(sync)
            instr = None
            for kc in range(16):
                sL.dinc(sync.dma_start(out=XCF[:, 0:512], in_=o_xc[kc * 128:(kc + 1) * 128, a0:a0 + 512]))
                sL.wait(ten)
                instr = ten.matmul(pR[0:33, 0:512], WX[:, kc * 33:(kc + 1) * 33], XCF[:, 0:512],
                                   start=(kc == 0), stop=(kc == 15))
                sPE.inc(instr)
                if kc < 15:
                    wpe(sync)
            wpe(vec)
            sV.inc(vec.tensor_copy(BC33[:, 0:512], pR[0:33, 0:512]))
            wv(sync)
            sST.dinc(sync.dma_start(out=o_bc[:, a0:a0 + 512], in_=BC33[1:33, 0:512]))
            wv(ten)
            instr = ten.matmul(pQ[:, 0:512], OBCF[:], BC33[0:1, 0:512], start=True, stop=True)
            sPE.inc(instr)
            wpe(vec)
            for cc in range(16):
                wv(vec)
                if "dt" in tsl:
                    sST.wait(vec)
                sV.inc(vec.tensor_scalar(DTR[:, 0:512], pQ[:, 0:512], DTW[:, cc:cc + 1], DTB[:, cc:cc + 1], A.mult, A.add))
                wv(act)
                sA.inc(act.activation(out=TB[:, 0:512], in_=DTR[:, 0:512], func=AF.Abs, scale=1.0))
                wa(act)
                sA.inc(act.activation(out=TB[:, 512:1024], in_=TB[:, 0:512], func=AF.Exp, scale=-1.0))
                wa(act)
                sA.inc(act.activation(out=TB[:, 1024:1536], in_=TB[:, 512:1024], func=AF.Ln, bias=ONE1F[:, 0:1], scale=1.0))
                wa(vec)
                sV.inc(vec.tensor_single_scalar(TB[:, 1536:2048], DTR[:, 0:512], 0.0, A.max))
                wv(vec)
                wa(vec)
                sV.inc(vec.tensor_add(DTR[:, 0:512], TB[:, 1536:2048], TB[:, 1024:1536]))
                wv(sync)
                sST.dinc(sync.dma_start(out=o_dt[cc * 128:(cc + 1) * 128, a0:a0 + 512], in_=DTR[:, 0:512]))
                sync.nop()
                tsl["dt"] = sST.v
        sST.wait(sync)

    if phase_b:
        for e in (sync, ten, vec, act, gp):
            sL.wait(e); sV.wait(e)
        for cc in range(16):
            wv(sync); wa(sync)
            sL.dinc(sync.dma_start(out=AC[:], in_=alog[cc * 128:(cc + 1) * 128, :]))
            sL.dinc(sync.dma_start(out=HQ[:], in_=i_hq[cc * 128:(cc + 1) * 128, :]))
            sL.dinc(sync.dma_start(out=SQC[:], in_=i_sq[cc * 128:(cc + 1) * 128, :]))
            sL.wait(act); sL.wait(vec)
            sA.inc(act.activation(out=AC[:], in_=AC[:], func=AF.Exp, scale=1.0))
            wa(vec)
            sV.inc(vec.tensor_scalar_mul(AC[:], AC[:], -1.0))
            for q in range(7):
                wv(act)
                sA.inc(act.activation(out=TM[:], in_=AC[:], func=AF.Exp, scale=SQC[:, q:q + 1]))
                wa(vec)
                if q == 0:
                    sV.inc(vec.tensor_copy(CARRY[:, cc * 16:(cc + 1) * 16], HQ[:, 0:16]))
                else:
                    sV.inc(vec.tensor_mul(CARRY[:, cc * 16:(cc + 1) * 16], CARRY[:, cc * 16:(cc + 1) * 16], TM[:]))
                    wv(vec)
                    sV.inc(vec.tensor_add(CARRY[:, cc * 16:(cc + 1) * 16], CARRY[:, cc * 16:(cc + 1) * 16], HQ[:, q * 16:(q + 1) * 16]))
                wv(act, vec)

    for qq in (range(NQ) if upto >= 4 else []):
        t0 = TQ * qq
        wv(sync)
        bsl = o_bc if not phase_b else i_bc
        import os as _os
        if _os.environ.get("SKIP_BCDMA"):
            sV.inc(vec.memset(BCROW[:], 0.5))
        else:
            sL.dinc(sync.dma_start(out=BCROW[0:1, :].rearrange("p (a b) -> p a b", a=16),
                                   in_=bsl[0:16, t0:t0 + TQ].rearrange("a b -> () a b")))
        if phase_b:
            sL.dinc(sync.dma_start(out=CCROW[0:1, :].rearrange("p (a b) -> p a b", a=16),
                                   in_=bsl[16:32, t0:t0 + TQ].rearrange("a b -> () a b")))
        sL.wait(ten)
        import os as _os2
        for nn in (range(16) if not _os2.environ.get("SKIP_NN") else []):
            wv(ten)
            instr = ten.matmul(pA[:, 0:TQ], OBCF[:], BCROW[0:1, nn * TQ:(nn + 1) * TQ], start=True, stop=True)
            sPE.inc(instr)
            wpe(vec)
            sV.inc(vec.tensor_copy(BRE[:, nn * TQ:(nn + 1) * TQ], pA[:, 0:TQ]))
            if phase_b:
                instr = ten.matmul(pB[:, 0:TQ], OBCF[:], CCROW[0:1, nn * TQ:(nn + 1) * TQ], start=True, stop=True)
                sPE.inc(instr)
                wpe(vec)
                sV.inc(vec.tensor_copy(CRE[:, nn * TQ:(nn + 1) * TQ], pB[:, 0:TQ]))
        for cc in (range(16) if upto >= 5 else []):
            wv(sync); wa(sync)
            sL.dinc(sync.dma_start(out=AC[:], in_=alog[cc * 128:(cc + 1) * 128, :]))
            sL.dinc(sync.dma_start(out=DTQ[:], in_=dtin[cc * 128:(cc + 1) * 128, t0:t0 + TQ]))
            sL.dinc(sync.dma_start(out=XCQ[:], in_=xcin[cc * 128:(cc + 1) * 128, t0:t0 + TQ]))
            if phase_b:
                sL.dinc(sync.dma_start(out=ZQ[:], in_=i_z[cc * 128:(cc + 1) * 128, t0:t0 + TQ]))
                sL.dinc(sync.dma_start(out=DV[:, 0:1], in_=dvec[cc * 128:(cc + 1) * 128, :]))
            sL.wait(act); sL.wait(vec)
            sA.inc(act.activation(out=AC[:], in_=AC[:], func=AF.Exp, scale=1.0))
            wa(vec)
            sV.inc(vec.tensor_scalar_mul(AC[:], AC[:], -1.0))
            wv(vec)
            sV.inc(vec.tensor_mul(UB[:], DTQ[:], XCQ[:]))
            wv(act, vec)
            sV.inc(vec.memset(ABUF[:].rearrange("p (a b) -> p a b", a=16)[:, :, 0:1], 0.0))
            for nn in range(16):
                wv(act)
                sA.inc(act.activation(out=ABUF[:, nn * (TQ + 1) + 1:(nn + 1) * (TQ + 1)],
                                      in_=DTQ[:], func=AF.Exp, scale=AC[:, nn:nn + 1]))
            if upto < 6:
                continue
            wa(vec); wv(vec)
            sV.inc(vec.tensor_tensor(BBUF[:].rearrange("p (a b) -> p a b", a=16)[:, :, 1:TQ + 1],
                                     BRE[:].rearrange("p (a b) -> p a b", a=16),
                                     UB[:].rearrange("p b -> p () b").to_broadcast([128, 16, TQ]),
                                     A.mult))
            wv(vec)
            if qq == 0 and not phase_b:
                sV.inc(vec.memset(BBUF[:].rearrange("p (a b) -> p a b", a=16)[:, :, 0:1], 0.0))
            else:
                sV.inc(vec.tensor_copy(BBUF[:].rearrange("p (a b) -> p a b", a=16)[:, :, 0:1],
                                       CARRY[:, cc * 16:(cc + 1) * 16].rearrange("p (a b) -> p a b", a=16)))
            if upto < 7:
                continue
            wv(vec); wa(vec)
            sV.inc(vec.tensor_tensor_scan(HBUF[:], ABUF[:], BBUF[:], 0.0, A.mult, A.add))
            wv(vec)
            sV.inc(vec.tensor_copy(CARRY[:, cc * 16:(cc + 1) * 16],
                                   HBUF[:].rearrange("p (a b) -> p a b", a=16)[:, :, TQ:TQ + 1]))
            if not phase_b:
                wv(vec)
                sV.inc(vec.reduce_sum(UB[:, 0:1], DTQ[:], axis=AX.X))
                wv(vec)
                sV.inc(vec.tensor_add(SFA[:, cc:cc + 1], SFA[:, cc:cc + 1], UB[:, 0:1]))
            else:
                wv(vec)
                sV.inc(vec.tensor_tensor(PB[:].rearrange("p (a b) -> p a b", a=16),
                                         HBUF[:].rearrange("p (a b) -> p a b", a=16)[:, :, 1:TQ + 1],
                                         CRE[:].rearrange("p (a b) -> p a b", a=16),
                                         A.mult))
                wv(vec)
                sV.inc(vec.reduce_sum(YQ[:], PB[:].rearrange("p (a b) -> p b a", a=16), axis=AX.X))
                wv(vec)
                sV.inc(vec.scalar_tensor_tensor(YQ[:], XCQ[:], DV[:, 0:1], YQ[:], A.mult, A.add))
                wv(act)
                sA.inc(act.activation(out=PB[:, 0:TQ], in_=ZQ[:], func=AF.Silu, scale=1.0))
                wa(vec); wv(vec)
                sV.inc(vec.tensor_tensor(YG[:, cc * 2048 + t0:cc * 2048 + t0 + TQ], YQ[:], PB[:, 0:TQ], A.mult))

    if not phase_b:
        wv(sync)
        for cc in range(16):
            sST.dinc(sync.dma_start(out=o_hf[cc * 128:(cc + 1) * 128, :], in_=CARRY[:, cc * 16:(cc + 1) * 16]))
            sST.dinc(sync.dma_start(out=o_sf[cc * 128:(cc + 1) * 128, :], in_=SFA[:, cc:cc + 1]))
    else:
        for ts2 in range(4):
            a0 = 512 * ts2
            for oc in range(8):
                pp = (pA, pB)[oc % 2]
                wv(ten)
                instr = None
                for kc in range(16):
                    wpe(gp)
                    sG.dinc(gp.dma_start(out=WOPB[:, (kc % 2) * 128:(kc % 2) * 128 + 128],
                                         in_=wopT[kc * 128:(kc + 1) * 128, oc * 128:(oc + 1) * 128]))
                    sG.wait(ten)
                    instr = ten.matmul(pp[:, 0:512], WOPB[:, (kc % 2) * 128:(kc % 2) * 128 + 128],
                                       YG[:, kc * 2048 + a0:kc * 2048 + a0 + 512],
                                       start=(kc == 0), stop=(kc == 15))
                    sPE.inc(instr)
                wpe(vec)
                sL.dinc(sync.dma_start(out=X2S[:], in_=x2own[oc * 128:(oc + 1) * 128, a0:a0 + 512]))
                sL.wait(vec)
                slot = oc % 2
                if ("x3", slot) in tsl:
                    sST.wait(vec)
                sV.inc(vec.scalar_tensor_tensor(TO[:, slot * 512:(slot + 1) * 512], pp[:, 0:512], 1.0, X2S[:], A.mult, A.add))
                wv(sync)
                sST.dinc(sync.dma_start(out=x3own[oc * 128:(oc + 1) * 128, a0:a0 + 512], in_=TO[:, slot * 512:(slot + 1) * 512]))
                sync.nop()
                tsl[("x3", slot)] = sST.v

    sST.wait(sync)
    sST.wait(gp)
    es.close()
    return nc


def mamba_maps_a(inp):
    return dict(
        wiT=np.ascontiguousarray(inp["L0_attn_b_in_proj_w"].T.astype(np.float32)),
        convw=np.ascontiguousarray(inp["L0_attn_b_conv_w"][:, 0, :].reshape(16, 128, 4).transpose(1, 0, 2).reshape(128, 64).astype(np.float32)),
        convb=np.ascontiguousarray(inp["L0_attn_b_conv_b"].reshape(16, 128).T.astype(np.float32)),
        wxT=np.ascontiguousarray(inp["L0_attn_b_x_proj_w"].T.astype(np.float32)),
        dtw=np.ascontiguousarray(inp["L0_attn_b_dt_proj_w"][:, 0].reshape(16, 128).T.astype(np.float32)),
        dtb=np.ascontiguousarray(inp["L0_attn_b_dt_proj_b"].reshape(16, 128).T.astype(np.float32)),
        alog=np.ascontiguousarray(inp["L0_attn_b_A_log"].astype(np.float32)),
        ob1f=np.ones((128, 1), np.float32), obcf=np.ones((1, 128), np.float32),
    )


def run_mamba(progs, x2_full, inp):
    ma = mamba_maps_a(inp)
    in_maps = []
    for c in range(NCORES):
        b0 = TOWN * c
        x2e = np.zeros((DM, TOWN + 4), np.float32)
        lo = max(0, b0 - 4)
        x2e[:, 4 - (b0 - lo):] = x2_full[:, lo:b0 + TOWN]
        in_maps.append(dict(x2e=x2e, **ma))
    resA = run_bass_kernel_spmd(progs["mambaA"], in_maps, core_ids=list(range(NCORES)))
    ra = [dict(r) for r in (dict(x) if isinstance(x, dict) else x for x in resA.results)] if False else resA.results
    hf = [np.asarray(r["o_hf"]) for r in ra]
    sf = [np.asarray(r["o_sf"]) for r in ra]
    in_maps_b = []
    mb = dict(alog=ma["alog"],
              dvec=np.ascontiguousarray(inp["L0_attn_b_D"].astype(np.float32)[:, None]),
              wopT=np.ascontiguousarray(inp["L0_attn_b_out_proj_w"].T.astype(np.float32)),
              ob1f=ma["ob1f"], obcf=ma["obcf"])
    for c in range(NCORES):
        b0 = TOWN * c
        hq = np.zeros((2048, 7 * 16), np.float32)
        sq = np.zeros((2048, 7), np.float32)
        for i, q in enumerate(range(7 - c, 7)):
            hq[:, q * 16:(q + 1) * 16] = hf[i + (c - c)] if False else hf[c - (7 - q)]
            sq[:, q] = sf[c - (7 - q)][:, 0]
        in_maps_b.append(dict(
            i_dt=np.asarray(ra[c]["o_dt"]), i_xc=np.asarray(ra[c]["o_xc"]),
            i_z=np.asarray(ra[c]["o_z"]), i_bc=np.asarray(ra[c]["o_bc"]),
            i_hq=hq, i_sq=sq,
            x2own=np.ascontiguousarray(x2_full[:, b0:b0 + TOWN]), **mb))
    resB = run_bass_kernel_spmd(progs["mambaB"], in_maps_b, core_ids=list(range(NCORES)))
    return np.concatenate([np.asarray(r["x3own"]) for r in resB.results], axis=1)

def build_aprep():
    from contextlib import ExitStack
    nc = bass.Bass("TRN2", target_bir_lowering=False, debug=False)
    es = ExitStack()
    T = lambda name, shape, dt: es.enter_context(nc.sbuf_tensor(name, shape, dt))
    xin = nc.dram_tensor("xin", [DM, TOWN], F32, kind="ExternalInput").ap()
    wdkv = nc.dram_tensor("wdkv", [DM, 256], F32, kind="ExternalInput").ap()
    wuk = nc.dram_tensor("wuk", [256, 512], F32, kind="ExternalInput").ap()
    wuv = nc.dram_tensor("wuv", [256, 512], F32, kind="ExternalInput").ap()
    wkr = nc.dram_tensor("wkr", [DM, 32], F32, kind="ExternalInput").ap()
    wik = nc.dram_tensor("wik", [DM, 64], F32, kind="ExternalInput").ap()
    cl_ = nc.dram_tensor("cl_", [32, TOWN], BF16, kind="ExternalInput").ap()
    sl_ = nc.dram_tensor("sl_", [32, TOWN], BF16, kind="ExternalInput").ap()
    ci_ = nc.dram_tensor("ci_", [32, TOWN], BF16, kind="ExternalInput").ap()
    si_ = nc.dram_tensor("si_", [32, TOWN], BF16, kind="ExternalInput").ap()
    r32 = nc.dram_tensor("r32", [32, 32], BF16, kind="ExternalInput").ap()
    ob1f = nc.dram_tensor("ob1f", [128, 1], F32, kind="ExternalInput").ap()
    obcf = nc.dram_tensor("obcf", [1, 128], F32, kind="ExternalInput").ap()
    wiq_ = nc.dram_tensor("wiq_", [DM, 256], F32, kind="ExternalInput").ap()
    o_ki = nc.dram_tensor("o_ki", [64, TOWN], F32, kind="ExternalOutput").ap()
    o_qi = nc.dram_tensor("o_qi", [256, TOWN], F32, kind="ExternalOutput").ap()
    o_k = nc.dram_tensor("o_k", [512, TOWN], BF16, kind="ExternalOutput").ap()
    o_v = nc.dram_tensor("o_v", [512, TOWN], BF16, kind="ExternalOutput").ap()
    o_mu = nc.dram_tensor("o_mu", [32, TOWN], BF16, kind="ExternalOutput").ap()
    o_krl = nc.dram_tensor("o_krl", [32, TOWN], BF16, kind="ExternalOutput").ap()
    o_kri = nc.dram_tensor("o_kri", [32, TOWN], BF16, kind="ExternalOutput").ap()
    o_lnx = nc.dram_tensor("o_lnx", [DM, TOWN], BF16, kind="ExternalOutput").ap()

    XA = T("XA", [128, 8 * 512], F32)
    SQ = T("SQ", [128, 8 * 512], F32)
    XLN = T("XLN", [128, 8 * 512], F32)
    WIKF = T("WIKF", [128, 8 * 64], F32)
    WIQF = T("WIQF", [128, 8 * 256], F32)
    KIF = T("KIF", [128, 512], F32)
    ROWS = T("ROWS", [1, 2048], F32)
    RSS = T("RSS", [128, 512], F32)
    LNB = T("LNB", [128, 8 * TOWN], BF16)
    WD = T("WD", [128, 8 * 256], BF16)
    WUK = T("WUK", [128, 2 * 512], BF16)
    WUV = T("WUV", [128, 2 * 512], BF16)
    WKR = T("WKR", [128, 8 * 32], BF16)
    CKV = T("CKV", [128, 2 * TOWN], BF16)
    KO = T("KO", [128, 4 * 512], BF16)
    MU = T("MU", [32, TOWN], BF16)
    MUS = T("MUS", [32, 3 * TOWN], BF16)
    R32 = T("R32", [32, 32], BF16)
    CL = T("CL", [32, TOWN], BF16)
    SL = T("SL", [32, TOWN], BF16)
    CI = T("CI", [32, TOWN], BF16)
    SI = T("SI", [32, TOWN], BF16)
    OB1F = T("OB1F", [128, 1], F32)
    OBCF = T("OBCF", [1, 128], F32)
    EPS1 = T("EPS1", [1, 1], F32)
    TO = T("TO", [128, 512], F32)

    pR = es.enter_context(nc.psum_tensor("pR", [128, 512], F32))
    pQ = es.enter_context(nc.psum_tensor("pQ", [128, 512], F32))
    pA = es.enter_context(nc.psum_tensor("pA", [128, 512], F32))
    pB = es.enter_context(nc.psum_tensor("pB", [128, 512], F32))

    sL = Sem(nc, es, "sL")
    sG = Sem(nc, es, "sG")
    sPE = Sem(nc, es, "sPE")
    sV = Sem(nc, es, "sV")
    sA = Sem(nc, es, "sA")
    sST = Sem(nc, es, "sST")
    es.enter_context(nc.Block())
    sync, ten, vec, act, gp = nc.sync, nc.tensor, nc.vector, nc.scalar, nc.gpsimd

    def wv(*e):
        for x in e: sV.wait(x)

    def wpe(*e):
        for x in e: sPE.wait(x)

    def wa(*e):
        for x in e: sA.wait(x)

    sL.dinc(sync.dma_start(out=OB1F[:], in_=ob1f))
    sL.dinc(sync.dma_start(out=OBCF[:], in_=obcf))
    sL.dinc(sync.dma_start(out=R32[:], in_=r32))
    sL.dinc(sync.dma_start(out=CL[:], in_=cl_))
    sL.dinc(sync.dma_start(out=SL[:], in_=sl_))
    sL.dinc(sync.dma_start(out=CI[:], in_=ci_))
    sL.dinc(sync.dma_start(out=SI[:], in_=si_))
    sV.inc(vec.memset(EPS1[:], 1e-5))
    sG.dinc(gp.dma_start(out=WD[:].rearrange("p (c n) -> p c n", c=8), in_=wdkv.rearrange("(c p) n -> p c n", p=128)))
    sG.dinc(gp.dma_start(out=WUK[:].rearrange("p (c n) -> p c n", c=2), in_=wuk.rearrange("(c p) n -> p c n", p=128)))
    sG.dinc(gp.dma_start(out=WUV[:].rearrange("p (c n) -> p c n", c=2), in_=wuv.rearrange("(c p) n -> p c n", p=128)))
    sG.dinc(gp.dma_start(out=WKR[:].rearrange("p (c n) -> p c n", c=8), in_=wkr.rearrange("(c p) n -> p c n", p=128)))
    sG.dinc(gp.dma_start(out=WIKF[:].rearrange("p (c n) -> p c n", c=8), in_=wik.rearrange("(c p) n -> p c n", p=128)))
    sG.dinc(gp.dma_start(out=WIQF[:].rearrange("p (c n) -> p c n", c=8), in_=wiq_.rearrange("(c p) n -> p c n", p=128)))
    for e in (sync, ten, vec, act, gp):
        sL.wait(e); sG.wait(e); sV.wait(e)

    tsl = {}
    for s in range(4):
        c0 = 512 * s
        wv(sync); wa(sync); wpe(sync)
        for c in range(8):
            sL.dinc(sync.dma_start(out=XA[:, c * 512:(c + 1) * 512], in_=xin[c * 128:(c + 1) * 128, c0:c0 + 512]))
        sL.wait(act); sL.wait(ten); sL.wait(vec)
        sA.inc(act.square(SQ[:], XA[:]))
        wa(ten); wv(ten)
        for c in range(8):
            ten.matmul(pR[0:1, 0:512], OB1F[:], XA[:, c * 512:(c + 1) * 512], start=(c == 0), stop=(c == 7))
        for c in range(8):
            instr = ten.matmul(pQ[0:1, 0:512], OB1F[:], SQ[:, c * 512:(c + 1) * 512], start=(c == 0), stop=(c == 7))
        sPE.inc(instr)
        wpe(vec)
        sV.inc(vec.tensor_scalar_mul(ROWS[0:1, 0:512], pR[0:1, 0:512], 1.0 / 1024.0))
        wv(vec)
        sV.inc(vec.tensor_scalar_mul(ROWS[0:1, 512:1024], pQ[0:1, 0:512], 1.0 / 1024.0))
        wv(vec)
        sV.inc(vec.tensor_mul(ROWS[0:1, 1024:1536], ROWS[0:1, 0:512], ROWS[0:1, 0:512]))
        wv(vec)
        sV.inc(vec.tensor_sub(ROWS[0:1, 1024:1536], ROWS[0:1, 512:1024], ROWS[0:1, 1024:1536]))
        wv(act)
        sA.inc(act.activation(out=ROWS[0:1, 1536:2048], in_=ROWS[0:1, 1024:1536], func=AF.Sqrt,
                              bias=EPS1[0:1, 0:1], scale=1.0))
        wa(vec)
        sV.inc(vec.reciprocal(ROWS[0:1, 1024:1536], ROWS[0:1, 1536:2048]))
        wv(ten)
        ten.matmul(pR[:, 0:512], OBCF[:], ROWS[0:1, 0:512], start=True, stop=True)
        instr = ten.matmul(pQ[:, 0:512], OBCF[:], ROWS[0:1, 1024:1536], start=True, stop=True)
        sPE.inc(instr)
        wpe(vec)
        sV.inc(vec.tensor_copy(RSS[:, 0:512], pQ[:, 0:512]))
        for c in range(8):
            wv(vec)
            sV.inc(vec.tensor_sub(TO[:, 0:512], XA[:, c * 512:(c + 1) * 512], pR[:, 0:512]))
            wv(vec)
            sV.inc(vec.tensor_mul(XLN[:, c * 512:(c + 1) * 512], TO[:, 0:512], RSS[:, 0:512]))
            wv(vec)
            sV.inc(vec.tensor_copy(LNB[:, c * TOWN + c0:c * TOWN + c0 + 512], XLN[:, c * 512:(c + 1) * 512]))
        wv(ten)
        for kc in range(8):
            instr = ten.matmul(pQ[0:64, 0:512], WIKF[:, kc * 64:(kc + 1) * 64],
                               XLN[:, kc * 512:(kc + 1) * 512], start=(kc == 0), stop=(kc == 7))
        sPE.inc(instr)
        wpe(vec)
        sST.wait(vec)
        sV.inc(vec.tensor_copy(KIF[0:64, 0:512], pQ[0:64, 0:512]))
        wv(sync)
        sST.dinc(sync.dma_start(out=o_ki[:, c0:c0 + 512], in_=KIF[0:64, 0:512]))
        for oc in range(2):
            pp = (pA, pB)[oc]
            wv(ten)
            for kc in range(8):
                instr = ten.matmul(pp[:, 0:512], WIQF[:, kc * 256 + oc * 128:kc * 256 + oc * 128 + 128],
                                   XLN[:, kc * 512:(kc + 1) * 512], start=(kc == 0), stop=(kc == 7))
            sPE.inc(instr)
            wpe(vec)
            sST.wait(vec)
            sV.inc(vec.tensor_copy(KIF[:, 0:512], pp[:, 0:512]))
            wv(sync)
            sST.dinc(sync.dma_start(out=o_qi[oc * 128:(oc + 1) * 128, c0:c0 + 512], in_=KIF[:, 0:512]))
    wv(sync)
    for c in range(8):
        sST.dinc(sync.dma_start(out=o_lnx[c * 128:(c + 1) * 128, :], in_=LNB[:, c * TOWN:(c + 1) * TOWN]))
    for ts2 in range(4):
        a0 = 512 * ts2
        for oc in range(2):
            pp = (pA, pB)[oc]
            wv(ten)
            instr = None
            for kc in range(8):
                instr = ten.matmul(pp[:, 0:512], WD[:, kc * 256 + oc * 128:kc * 256 + oc * 128 + 128],
                                   LNB[:, kc * TOWN + a0:kc * TOWN + a0 + 512], start=(kc == 0), stop=(kc == 7))
            sPE.inc(instr)
            wpe(vec)
            sV.inc(vec.tensor_copy(CKV[:, oc * TOWN + a0:oc * TOWN + a0 + 512], pp[:, 0:512]))
        wv(ten)
        instr = None
        for kc in range(8):
            instr = ten.matmul(pQ[0:32, 0:512], WKR[:, kc * 32:(kc + 1) * 32],
                               LNB[:, kc * TOWN + a0:kc * TOWN + a0 + 512], start=(kc == 0), stop=(kc == 7))
        sPE.inc(instr)
        wpe(act); wv(act)
        sA.inc(act.activation(out=MUS[0:32, 0:512], in_=pQ[0:32, 0:512], func=AF.Abs, scale=1.0))
        wa(act)
        sA.inc(act.activation(out=MUS[0:32, 512:1024], in_=MUS[0:32, 0:512], func=AF.Exp, scale=-1.0))
        wa(act)
        sA.inc(act.activation(out=MUS[0:32, 1024:1536], in_=MUS[0:32, 512:1024], func=AF.Ln, bias=OB1F[0:32, 0:1], scale=1.0))
        wa(vec); wpe(vec)
        sV.inc(vec.tensor_single_scalar(MUS[0:32, 0:512], pQ[0:32, 0:512], 0.0, A.max))
        wv(vec)
        sV.inc(vec.tensor_add(MU[0:32, a0:a0 + 512], MUS[0:32, 0:512], MUS[0:32, 1024:1536]))
        wv(sync)
        sST.dinc(sync.dma_start(out=o_mu[:, a0:a0 + 512], in_=MU[0:32, a0:a0 + 512]))
    for ts2 in range(4):
        a0 = 512 * ts2
        for oc in range(4):
            pp = (pA, pB)[oc % 2]
            wv(ten)
            instr = None
            for kc in range(2):
                instr = ten.matmul(pp[:, 0:512], WUK[:, kc * 512 + oc * 128:kc * 512 + oc * 128 + 128],
                                   CKV[:, kc * TOWN + a0:kc * TOWN + a0 + 512], start=(kc == 0), stop=(kc == 1))
            sPE.inc(instr)
            wpe(vec)
            sST.wait(vec)
            sV.inc(vec.tensor_copy(KO[:, (oc % 2) * 512:(oc % 2) * 512 + 512], pp[:, 0:512]))
            wv(sync)
            sST.dinc(sync.dma_start(out=o_k[oc * 128:(oc + 1) * 128, a0:a0 + 512], in_=KO[:, (oc % 2) * 512:(oc % 2) * 512 + 512]))
            tsl[("k", oc % 2)] = sST.v
        for oc in range(4):
            pp = (pA, pB)[oc % 2]
            wv(ten)
            instr = None
            for kc in range(2):
                instr = ten.matmul(pp[:, 0:512], WUV[:, kc * 512 + oc * 128:kc * 512 + oc * 128 + 128],
                                   CKV[:, kc * TOWN + a0:kc * TOWN + a0 + 512], start=(kc == 0), stop=(kc == 1))
            sPE.inc(instr)
            wpe(vec)
            sST.wait(vec)
            sV.inc(vec.tensor_copy(KO[:, 2 * 512 + (oc % 2) * 512:2 * 512 + (oc % 2) * 512 + 512], pp[:, 0:512]))
            wv(sync)
            sST.dinc(sync.dma_start(out=o_v[oc * 128:(oc + 1) * 128, a0:a0 + 512], in_=KO[:, 2 * 512 + (oc % 2) * 512:2 * 512 + (oc % 2) * 512 + 512]))
            tsl[("v", oc % 2)] = sST.v
    instr = None
    for ts2 in range(4):
        a0 = 512 * ts2
        wv(ten); wa(ten)
        instr = ten.matmul(pR[0:32, 0:512], R32[:], MU[0:32, a0:a0 + 512], start=True, stop=True)
        sPE.inc(instr)
        wpe(vec)
        sV.inc(vec.tensor_mul(MUS[0:32, 0:512], MU[0:32, a0:a0 + 512], CL[0:32, a0:a0 + 512]))
        wv(vec)
        sV.inc(vec.scalar_tensor_tensor(MUS[0:32, 512:1024], pR[0:32, 0:512], 1.0, SL[0:32, a0:a0 + 512], A.mult, A.mult))
        wv(vec)
        sST.wait(vec)
        sV.inc(vec.tensor_add(MUS[0:32, 1024:1536], MUS[0:32, 0:512], MUS[0:32, 512:1024]))
        wv(sync)
        sST.dinc(sync.dma_start(out=o_krl[:, a0:a0 + 512], in_=MUS[0:32, 1024:1536]))
        wv(vec)
        sV.inc(vec.tensor_mul(MUS[0:32, 0:512], MU[0:32, a0:a0 + 512], CI[0:32, a0:a0 + 512]))
        wv(vec)
        sV.inc(vec.scalar_tensor_tensor(MUS[0:32, 512:1024], pR[0:32, 0:512], 1.0, SI[0:32, a0:a0 + 512], A.mult, A.mult))
        wv(vec)
        sV.inc(vec.tensor_add(MUS[0:32, 1536:2048], MUS[0:32, 0:512], MUS[0:32, 512:1024]))
        wv(sync)
        sST.dinc(sync.dma_start(out=o_kri[:, a0:a0 + 512], in_=MUS[0:32, 1536:2048]))
    sST.wait(sync)
    sST.wait(gp)
    es.close()
    return nc

def tile_span(k):
    return min(L, ((128 * (8 * k + 8) + 511) // 512) * 512)



def build_attn(ntiles=16):
    from contextlib import ExitStack
    nc = bass.Bass("TRN2", target_bir_lowering=False, debug=False)
    es = ExitStack()
    T = lambda name, shape, dt: es.enter_context(nc.sbuf_tensor(name, shape, dt))

    kit = nc.dram_tensor("kit", [128, L], F32, kind="ExternalInput").ap()
    qiP = nc.dram_tensor("qiP", [256, ntiles * 128], F32, kind="ExternalInput").ap()
    gt = nc.dram_tensor("gt", [L, 1056], BF16, kind="ExternalInput").ap()
    kclP = nc.dram_tensor("kclP", [512, ntiles * 256], BF16, kind="ExternalInput").ap()
    vltP = nc.dram_tensor("vltP", [ntiles * 256, 512], BF16, kind="ExternalInput").ap()
    krlP = nc.dram_tensor("krlP", [128, ntiles * 256], BF16, kind="ExternalInput").ap()
    k96 = nc.dram_tensor("k96", [512, 96], BF16, kind="ExternalInput").ap()
    v96 = nc.dram_tensor("v96", [96, 512], BF16, kind="ExternalInput").ap()
    kr96 = nc.dram_tensor("kr96", [128, 96], BF16, kind="ExternalInput").ap()
    lnxP = nc.dram_tensor("lnxP", [DM, ntiles * 128], BF16, kind="ExternalInput").ap()
    wdq = nc.dram_tensor("wdq", [DM, 512], F32, kind="ExternalInput").ap()
    wuq = nc.dram_tensor("wuq", [512, 512], F32, kind="ExternalInput").ap()
    wqr = nc.dram_tensor("wqr", [512, 512], F32, kind="ExternalInput").ap()
    wout = nc.dram_tensor("wout", [512, DM], F32, kind="ExternalInput").ap()
    idxw = nc.dram_tensor("idxw", [128, 4], F32, kind="ExternalInput").ap()
    c2t = nc.dram_tensor("c2t", [128, ntiles * 128], BF16, kind="ExternalInput").ap()
    s2t = nc.dram_tensor("s2t", [128, ntiles * 128], BF16, kind="ExternalInput").ap()
    cit = nc.dram_tensor("cit", [128, ntiles * 128], BF16, kind="ExternalInput").ap()
    sit = nc.dram_tensor("sit", [128, ntiles * 128], BF16, kind="ExternalInput").ap()
    r128 = nc.dram_tensor("r128", [128, 128], BF16, kind="ExternalInput").ap()
    crot = nc.dram_tensor("crot", [128, 1024], BF16, kind="ExternalInput").ap()
    srot = nc.dram_tensor("srot", [128, 1024], BF16, kind="ExternalInput").ap()
    iden = nc.dram_tensor("iden", [128, 128], BF16, kind="ExternalInput").ap()
    mloc = nc.dram_tensor("mloc", [ntiles * 128, 256], BF16, kind="ExternalInput").ap()
    m96 = nc.dram_tensor("m96", [ntiles * 128, 96], BF16, kind="ExternalInput").ap()
    mg = nc.dram_tensor("mg", [ntiles * 128, 32], BF16, kind="ExternalInput").ap()
    kidx = nc.dram_tensor("kidx", [128, 512], F32, kind="ExternalInput").ap()
    thrs1 = nc.dram_tensor("thrs1", [128, ntiles * 32], F32, kind="ExternalInput").ap()
    thrs2 = nc.dram_tensor("thrs2", [128, ntiles * 32], F32, kind="ExternalInput").ap()
    o1o = nc.dram_tensor("o1o", [DM, ntiles * 128], F32, kind="ExternalOutput").ap()

    IB = T("IB", [128, L], F32)
    KIT = T("KIT", [128, L], F32)
    IBb = IB[:].bitcast(BF16)
    VGA = IBb[:, 0:32 * 512]
    GD = IBb[:, 32 * 512:32 * 512 + 8 * 1056]
    TG = IBb[:, 24832:24832 + 6144]
    KCL = T("KCL", [128, 4 * 256], BF16)
    VLT = T("VLT", [128, 2 * 512], BF16)
    KRL = T("KRL", [128, 256], BF16)
    K96 = T("K96", [128, 4 * 96], BF16)
    V96 = T("V96", [96, 512], BF16)
    KR96 = T("KR96", [128, 96], BF16)
    WDQ = T("WDQ", [128, 8 * 512], BF16)
    WUQ = T("WUQ", [128, 4 * 512], BF16)
    WQR = T("WQR", [128, 4 * 512], BF16)
    WOUT = T("WOUT", [128, 4 * DM], BF16)
    IDXW = T("IDXW", [128, 4], F32)
    C2T = T("C2T", [128, 128], BF16)
    S2T = T("S2T", [128, 128], BF16)
    CIT = T("CIT", [128, 128], BF16)
    SIT = T("SIT", [128, 128], BF16)
    R128 = T("R128", [128, 128], BF16)
    CROT = T("CROT", [128, 1024], BF16)
    SROT = T("SROT", [128, 1024], BF16)
    IDN = T("IDN", [128, 128], BF16)
    MLOC = T("MLOC", [128, 256], BF16)
    M96 = T("M96", [128, 96], BF16)
    MG = T("MG", [128, 32], BF16)
    KIDX = T("KIDX", [128, 512], F32)
    THRS1 = T("THRS1", [128, ntiles * 32], F32)
    THRS2 = T("THRS2", [128, ntiles * 32], F32)
    LNXT = T("LNXT", [128, 8 * 128], BF16)
    CQ = T("CQ", [128, 4 * 128], BF16)
    QC = T("QC", [128, 4 * 128], BF16)
    QIT = T("QIT", [128, 2 * 128], F32)
    MUQF = [T(f"MUQF{i}", [128, 128], BF16) for i in range(4)]
    QRAF = [T(f"QRAF{i}", [128, 128], BF16) for i in range(4)]
    QRBF = [T(f"QRBF{i}", [128, 128], BF16) for i in range(4)]
    MUS = T("MUS", [128, 4 * 128], F32)
    QCT = T("QCT", [128, 512], BF16)
    QRBT = T("QRBT", [128, 512], BF16)
    SJ = T("SJ", [128, 384], F32)
    SG8 = T("SG8", [128, 256], F32)
    PX = T("PX", [128, 384], BF16)
    PTX = T("PTX", [128, 384], BF16)
    IDXT = T("IDXT", [128, 32], U32)
    IDXI = T("IDXI", [128, 32], I32)
    NM = T("NM", [128, 1], F32)
    DEN = T("DEN", [128, 1], F32)
    REC = T("REC", [128, 1], F32)
    M8 = T("M8", [128, 8], F32)
    AOT = T("AOT", [128, 512], F32)
    AOB = T("AOB", [128, 512], BF16)
    AOF = T("AOF", [128, 4 * 128], BF16)
    TF = T("TF", [128, 512], F32)
    TFB = T("TFB", [128, 512], F32)
    TH = T("TH", [128, 512], F32)
    OE = T("OE", [128, 2 * 128], F32)
    ONE1 = T("ONE1", [128, 1], F32)

    pS = es.enter_context(nc.psum_tensor("pS", [128, 384], F32))
    pV = es.enter_context(nc.psum_tensor("pV", [128, 64], F32))
    pT = es.enter_context(nc.psum_tensor("pT", [128, 384], BF16))
    pA = es.enter_context(nc.psum_tensor("pA", [128, 512], F32))
    pB = es.enter_context(nc.psum_tensor("pB", [128, 512], F32))
    pO = es.enter_context(nc.psum_tensor("pO", [128, 128], F32))
    pR = es.enter_context(nc.psum_tensor("pR", [128, 128], F32))

    sL = Sem(nc, es, "sL")
    sG = Sem(nc, es, "sG")
    sPE = Sem(nc, es, "sPE")
    sV = Sem(nc, es, "sV")
    sA = Sem(nc, es, "sA")
    sST = Sem(nc, es, "sST")
    es.enter_context(nc.Block())
    sync, ten, vec, act, gp = nc.sync, nc.tensor, nc.vector, nc.scalar, nc.gpsimd

    def wv(*e):
        for x in e: sV.wait(x)

    def wpe(*e):
        for x in e: sPE.wait(x)

    def wa(*e):
        for x in e: sA.wait(x)

    for dst, src_, nchunk in ((WDQ, wdq, 8), (WUQ, wuq, 4), (WQR, wqr, 4), (WOUT, wout, 4)):
        sG.dinc(gp.dma_start(out=dst[:].rearrange("p (c n) -> p c n", c=nchunk),
                             in_=src_.rearrange("(c p) n -> p c n", p=128)))
    sG.dinc(gp.dma_start(out=K96[:].rearrange("p (c n) -> p c n", c=4),
                         in_=k96.rearrange("(c p) n -> p c n", p=128)))
    sG.dinc(gp.dma_start(out=KIT[:].rearrange("p (c n) -> p c n", c=16),
                         in_=kit.rearrange("p (c n) -> p c n", c=16)))
    for dst, src_ in ((V96, v96), (KR96, kr96), (IDXW, idxw), (R128, r128),
                      (CROT, crot), (SROT, srot), (IDN, iden), (KIDX, kidx),
                      (THRS1, thrs1), (THRS2, thrs2)):
        sL.dinc(sync.dma_start(out=dst[:], in_=src_))
    sV.inc(vec.memset(ONE1[:], 1.0))
    for e in (sync, ten, vec, act, gp):
        sL.wait(e); sG.wait(e); sV.wait(e)

    SCALE = float((64 + 32) ** -0.5)

    def bc1(ap_col, n):
        return ap_col.rearrange("p u -> p u").to_broadcast([128, n])

    def rope128(dst, mu, cs, ss):
        wv(ten)
        sPE.inc(ten.matmul(pR[:, 0:128], R128[:], mu[:], start=True, stop=True))
        wv(vec)
        sV.inc(vec.tensor_mul(MUS[:, 384:512], mu[:], cs[:]))
        wpe(vec)
        sV.inc(vec.scalar_tensor_tensor(MUS[:, 256:384], pR[:, 0:128], SCALE, ss[:], A.mult, A.mult))
        wv(vec)
        sV.inc(vec.scalar_tensor_tensor(dst[:], MUS[:, 384:512], SCALE, MUS[:, 256:384], A.mult, A.add))

    for k in range(ntiles):
        tc = 128 * k
        span = tile_span(k)
        nblk = span // 512
        wv(sync); wa(sync); wpe(sync)
        for c in range(8):
            sL.dinc(sync.dma_start(out=LNXT[:, c * 128:(c + 1) * 128], in_=lnxP[c * 128:(c + 1) * 128, tc:tc + 128]))
        for c in range(4):
            sL.dinc(sync.dma_start(out=KCL[:, c * 256:(c + 1) * 256], in_=kclP[c * 128:(c + 1) * 128, k * 256:(k + 1) * 256]))
        for c in range(2):
            sL.dinc(sync.dma_start(out=VLT[:, c * 512:(c + 1) * 512], in_=vltP[(2 * k + c) * 128:(2 * k + c + 1) * 128, :]))
        sL.dinc(sync.dma_start(out=KRL[:, 0:256], in_=krlP[:, k * 256:(k + 1) * 256]))
        sL.dinc(sync.dma_start(out=MLOC[:], in_=mloc[k * 128:(k + 1) * 128, :]))
        sL.dinc(sync.dma_start(out=M96[:], in_=m96[k * 128:(k + 1) * 128, :]))
        sL.dinc(sync.dma_start(out=MG[:], in_=mg[k * 128:(k + 1) * 128, :]))
        for dst, src_ in ((C2T, c2t), (S2T, s2t), (CIT, cit), (SIT, sit)):
            sL.dinc(sync.dma_start(out=dst[:], in_=src_[:, tc:tc + 128]))
        for oc in range(2):
            sL.dinc(sync.dma_start(out=QIT[:, oc * 128:(oc + 1) * 128], in_=qiP[oc * 128:(oc + 1) * 128, tc:tc + 128]))
        sL.wait(ten); sL.wait(vec); sL.wait(act)
        for oc in range(4):
            pp = (pA, pB)[oc % 2]
            wv(ten)
            for kc in range(8):
                instr = ten.matmul(pp[:, 0:128], WDQ[:, kc * 512 + oc * 128:kc * 512 + oc * 128 + 128],
                                   LNXT[:, kc * 128:(kc + 1) * 128], start=(kc == 0), stop=(kc == 7))
            sPE.inc(instr)
            wpe(vec)
            sV.inc(vec.tensor_copy(CQ[:, oc * 128:(oc + 1) * 128], pp[:, 0:128]))
        for oc in range(4):
            pp = (pA, pB)[oc % 2]
            wv(ten)
            for kc in range(4):
                instr = ten.matmul(pp[:, 0:128], WUQ[:, kc * 512 + oc * 128:kc * 512 + oc * 128 + 128],
                                   CQ[:, kc * 128:(kc + 1) * 128], start=(kc == 0), stop=(kc == 3))
            sPE.inc(instr)
            wpe(vec)
            sV.inc(vec.tensor_scalar_mul(QC[:, oc * 128:(oc + 1) * 128], pp[:, 0:128], SCALE))
        for oc in range(4):
            pp = (pA, pB)[oc % 2]
            wv(ten)
            for kc in range(4):
                instr = ten.matmul(pp[:, 0:128], WQR[:, kc * 512 + oc * 128:kc * 512 + oc * 128 + 128],
                                   CQ[:, kc * 128:(kc + 1) * 128], start=(kc == 0), stop=(kc == 3))
            sPE.inc(instr)
            wpe(act); wa(act); wv(act)
            sA.inc(act.activation(out=MUS[:, 0:128], in_=pp[:, 0:128], func=AF.Abs, scale=1.0))
            wa(act)
            sA.inc(act.activation(out=MUS[:, 128:256], in_=MUS[:, 0:128], func=AF.Exp, scale=-1.0))
            wa(act)
            sA.inc(act.activation(out=MUS[:, 0:128], in_=MUS[:, 128:256], func=AF.Ln,
                                  bias=ONE1[:, 0:1], scale=1.0))
            wpe(vec); wa(vec)
            sV.inc(vec.tensor_single_scalar(MUS[:, 128:256], pp[:, 0:128], 0.0, A.max))
            wv(vec)
            sV.inc(vec.tensor_add(MUQF[oc][:], MUS[:, 128:256], MUS[:, 0:128]))
        for oc in range(4):
            rope128(QRAF[oc], MUQF[oc], C2T, S2T)
            rope128(QRBF[oc], MUQF[oc], CIT, SIT)
        for oc in range(4):
            wv(ten)
            sPE.inc(ten.transpose(pT[:, 0:128], QC[:, oc * 128:(oc + 1) * 128], IDN[:]))
            wpe(vec)
            sV.inc(vec.tensor_copy(QCT[:, oc * 128:(oc + 1) * 128], pT[:, 0:128]))
            wv(ten)
            sPE.inc(ten.transpose(pT[:, 0:128], QRBF[oc][:], IDN[:]))
            wpe(vec)
            sV.inc(vec.tensor_copy(QRBT[:, oc * 128:(oc + 1) * 128], pT[:, 0:128]))
        for b in range(nblk):
            half = b % 2
            for h in range(4):
                pp = (pA, pB)[h % 2]
                wv(ten)
                sPE.inc(ten.matmul(pp[:, 0:512],
                                   QIT[(h % 2) * 64:(h % 2) * 64 + 64, (h // 2) * 128:(h // 2) * 128 + 128],
                                   KIT[(h % 2) * 64:(h % 2) * 64 + 64, b * 512:(b + 1) * 512],
                                   start=True, stop=True))
                wpe(vec)
                if h == 0:
                    sV.inc(vec.scalar_tensor_tensor(IB[:, b * 512:(b + 1) * 512], pp[:, 0:512], 0.0,
                                                    bc1(IDXW[:, 0:1], 512), A.max, A.mult))
                else:
                    sV.inc(vec.scalar_tensor_tensor(TH[:, 0:512], pp[:, 0:512], 0.0,
                                                    bc1(IDXW[:, h:h + 1], 512), A.max, A.mult))
                    sV.inc(vec.tensor_add(IB[:, b * 512:(b + 1) * 512], IB[:, b * 512:(b + 1) * 512], TH[:, 0:512]))
            if b >= 2 * k - 1:
                col = k * 32 + b
                sV.inc(vec.scalar_tensor_tensor(TF[:, 0:512], KIDX[:], 1.0,
                                                bc1(THRS1[:, col:col + 1], 512), A.mult, A.is_gt))
                if b >= 2 * k:
                    sV.inc(vec.scalar_tensor_tensor(TFB[:, 0:512], KIDX[:], 1.0,
                                                    bc1(THRS2[:, col:col + 1], 512), A.mult, A.is_gt))
                    sV.inc(vec.tensor_sub(TF[:, 0:512], TF[:, 0:512], TFB[:, 0:512]))
                    sV.inc(vec.scalar_tensor_tensor(IB[:, b * 512:(b + 1) * 512], TFB[:, 0:512], -1.0e30,
                                                    IB[:, b * 512:(b + 1) * 512], A.mult, A.add))
                sV.inc(vec.scalar_tensor_tensor(IB[:, b * 512:(b + 1) * 512], TF[:, 0:512], -3.0e38,
                                                IB[:, b * 512:(b + 1) * 512], A.mult, A.add))
        for r in range(4):
            wv(vec)
            sV.inc(vec.max(out=M8[:], in_=IB[:, 0:span]))
            wv(vec)
            sV.inc(vec.max_index(out=IDXT[:, r * 8:(r + 1) * 8], in_max=M8[:], in_values=IB[:, 0:span]))
            wv(vec)
            sV.inc(vec.match_replace(out=IB[:, 0:span], in_to_replace=M8[:], in_values=IB[:, 0:span],
                                     imm_value=-3.0e38))
        wv(vec)
        sV.inc(vec.tensor_copy(IDXI[:], IDXT[:].bitcast(I32)))
        for grp in range(4):
            wv(gp)
            for c in range(8):
                slot = grp * 8 + c
                sG.dinc(gp.indirect_dma_start(
                    out=GD[:, c * 1056:(c + 1) * 1056], out_offset=None,
                    in_=gt, in_offset=bass.IndirectOffsetOnAxis(ap=IDXI[:, slot:slot + 1], axis=0)))
            sG.wait(vec)
            gd3 = GD.rearrange("p (a b) -> p a b", a=8)
            sV.inc(vec.tensor_tensor(TG[:, 4096:4096 + 256].rearrange("p (a b) -> p a b", a=8),
                                     gd3[:, :, 512:544],
                                     CROT[:, grp * 256:(grp + 1) * 256].rearrange("p (a b) -> p a b", a=8),
                                     A.mult))
            sV.inc(vec.tensor_tensor(TG[:, 4352:4352 + 128].rearrange("p (a b) -> p a b", a=8),
                                     gd3[:, :, 528:544],
                                     SROT[:, grp * 256:(grp + 1) * 256].rearrange("p (a b) -> p a b", a=8)[:, :, 0:16],
                                     A.mult))
            sV.inc(vec.tensor_tensor(TG[:, 4480:4480 + 128].rearrange("p (a b) -> p a b", a=8),
                                     gd3[:, :, 512:528],
                                     SROT[:, grp * 256:(grp + 1) * 256].rearrange("p (a b) -> p a b", a=8)[:, :, 16:32],
                                     A.mult))
            wv(vec)
            sV.inc(vec.tensor_sub(TG[:, 4096:4096 + 256].rearrange("p (a b) -> p a b", a=8)[:, :, 0:16],
                                  TG[:, 4096:4096 + 256].rearrange("p (a b) -> p a b", a=8)[:, :, 0:16],
                                  TG[:, 4352:4352 + 128].rearrange("p (a b) -> p a b", a=8)))
            wv(vec)
            sV.inc(vec.tensor_add(TG[:, 4096:4096 + 256].rearrange("p (a b) -> p a b", a=8)[:, :, 16:32],
                                  TG[:, 4096:4096 + 256].rearrange("p (a b) -> p a b", a=8)[:, :, 16:32],
                                  TG[:, 4480:4480 + 128].rearrange("p (a b) -> p a b", a=8)))
            for e in range(2):
                wv(vec)
                sV.inc(vec.tensor_tensor(
                    TG[:, 5120:5120 + 1024].rearrange("p (a c b) -> p a c b", a=8, c=4),
                    TG[:, 4096:4096 + 256].rearrange("p (a b) -> p a () b", a=8).to_broadcast([128, 8, 4, 32]),
                    QRBT[:].rearrange("p (c q b) -> p c q b", q=4, b=32)[:, :, 2 * e, :]
                        .rearrange("p c b -> p () c b").to_broadcast([128, 8, 4, 32]),
                    A.mult))
                wv(vec)
                sV.inc(vec.reduce_sum(
                    SG8[:, grp * 64:(grp + 1) * 64].rearrange("p (a c e) -> p a c e", c=4, e=2)[:, :, :, e],
                    TG[:, 5120:5120 + 1024].rearrange("p (a c b) -> p a c b", a=8, c=4), axis=AX.X))
            wv(vec)
            sV.inc(vec.tensor_tensor(TG[:, 0:4096].rearrange("p (a h b) -> p a h b", a=8, h=8),
                                     gd3[:, :, 0:512].rearrange("p a (h b) -> p a h b", h=8),
                                     QCT[:].rearrange("p (h b) -> p () h b", h=8).to_broadcast([128, 8, 8, 64]),
                                     A.mult))
            wv(vec)
            sV.inc(vec.reduce_sum(TF[:, 0:64].rearrange("p (a h) -> p a h", a=8),
                                  TG[:, 0:4096].rearrange("p (a h b) -> p a h b", a=8, h=8), axis=AX.X))
            wv(vec)
            sV.inc(vec.tensor_add(SG8[:, grp * 64:(grp + 1) * 64], SG8[:, grp * 64:(grp + 1) * 64], TF[:, 0:64]))
            sV.inc(vec.tensor_copy(VGA[:, grp * 4096:(grp + 1) * 4096].rearrange("p (a b) -> p a b", a=8),
                                   gd3[:, :, 544:1056]))
        for h in range(8):
            g2, hr, rr = h // 2, (h % 2) * 64, (h % 2) * 64
            qch = QC[hr:hr + 64, g2 * 128:(g2 + 1) * 128]
            wv(ten); wa(ten)
            ten.matmul(pS[:, 0:256], qch, KCL[hr:hr + 64, g2 * 256:(g2 + 1) * 256], start=True, stop=False)
            ten.matmul(pS[:, 0:256], QRAF[g2][rr:rr + 32, :], KRL[rr:rr + 32, 0:256], start=False, stop=False)
            sPE.inc(ten.matmul(pS[:, 0:256], IDN[:], MLOC[:], start=False, stop=True))
            ten.matmul(pS[:, 256:352], qch, K96[hr:hr + 64, g2 * 96:(g2 + 1) * 96], start=True, stop=False)
            ten.matmul(pS[:, 256:352], QRBF[g2][rr:rr + 32, :], KR96[rr:rr + 32, :], start=False, stop=False)
            sPE.inc(ten.matmul(pS[:, 256:352], IDN[:], M96[:], start=False, stop=True))
            wpe(vec); wv(vec)
            sV.inc(vec.tensor_copy(SJ[:, 0:352], pS[:, 0:352]))
            sV.inc(vec.tensor_copy(SJ[:, 352:384],
                                   SG8[:].rearrange("p (s hh) -> p s hh", hh=8)[:, :, h]))
            wv(vec)
            sV.inc(vec.scalar_tensor_tensor(SJ[:, 352:384], MG[:], 1.0, SJ[:, 352:384], A.mult, A.add))
            wv(vec)
            sV.inc(vec.reduce_max(NM[:], SJ[:], axis=AX.X, negate=True))
            wv(act)
            sA.inc(act.activation(out=PX[:], in_=SJ[:], func=AF.Exp, bias=NM[:, 0:1], scale=1.0,
                                  accum_out=DEN[:]))
            wa(vec)
            sV.inc(vec.reciprocal(REC[:], DEN[:]))
            wv(ten); wa(ten)
            for i2 in range(3):
                instr = ten.transpose(pT[:, i2 * 128:(i2 + 1) * 128], PX[:, i2 * 128:(i2 + 1) * 128], IDN[:])
            sPE.inc(instr)
            wpe(vec)
            sV.inc(vec.tensor_copy(PTX[:], pT[:, 0:384]))
            wv(ten)
            ten.matmul(pV[:, 0:64], PTX[:, 0:128], VLT[:, h * 64:h * 64 + 64], start=True, stop=False)
            ten.matmul(pV[:, 0:64], PTX[:, 128:256], VLT[:, 512 + h * 64:512 + h * 64 + 64], start=False, stop=False)
            sPE.inc(ten.matmul(pV[:, 0:64], PTX[0:96, 256:384], V96[0:96, h * 64:h * 64 + 64], start=False, stop=True))
            wv(vec)
            sV.inc(vec.tensor_tensor(TG[:, 0:2048].rearrange("p (a b) -> p a b", a=32),
                                     VGA.rearrange("p (a b) -> p a b", a=32)[:, :, h * 64:(h + 1) * 64],
                                     PX[:, 352:384].rearrange("p a -> p a ()").to_broadcast([128, 32, 64]),
                                     A.mult))
            wv(vec)
            sV.inc(vec.reduce_sum(AOT[:, h * 64:(h + 1) * 64],
                                  TG[:, 0:2048].rearrange("p (a b) -> p b a", a=32), axis=AX.X))
            wpe(vec); wv(vec)
            sV.inc(vec.tensor_add(AOT[:, h * 64:(h + 1) * 64], AOT[:, h * 64:(h + 1) * 64], pV[:, 0:64]))
            wv(vec)
            sV.inc(vec.scalar_tensor_tensor(AOT[:, h * 64:(h + 1) * 64], AOT[:, h * 64:(h + 1) * 64], 1.0,
                                            bc1(REC[:, 0:1], 64), A.mult, A.mult))
        wv(vec)
        sV.inc(vec.tensor_copy(AOB[:], AOT[:]))
        for oc in range(4):
            wv(ten)
            sPE.inc(ten.transpose(pT[:, 0:128], AOB[:, oc * 128:(oc + 1) * 128], IDN[:]))
            wpe(vec)
            sV.inc(vec.tensor_copy(AOF[:, oc * 128:(oc + 1) * 128], pT[:, 0:128]))
        for oc in range(8):
            slot = oc % 2
            wv(ten)
            for kc in range(4):
                instr = ten.matmul(pO[:, 0:128], WOUT[:, kc * DM + oc * 128:kc * DM + oc * 128 + 128],
                                   AOF[:, kc * 128:(kc + 1) * 128], start=(kc == 0), stop=(kc == 3))
            sPE.inc(instr)
            wpe(vec)
            sST.wait(vec)
            sV.inc(vec.tensor_copy(OE[:, slot * 128:(slot + 1) * 128], pO[:, 0:128]))
            wv(sync)
            sST.dinc(sync.dma_start(out=o1o[oc * 128:(oc + 1) * 128, tc:tc + 128],
                                    in_=OE[:, slot * 128:(slot + 1) * 128]))

    sST.wait(sync)
    es.close()
    return nc


ATH = (1.0 / 10000.0 ** (2.0 * np.arange(16, dtype=np.float64) / 32.0))

def rot32_mat():
    r = np.zeros((32, 32), np.float64)
    for m in range(16):
        r[m + 16, m] = -1.0
        r[m, m + 16] = 1.0
    return r

def aprep_maps(inp, core):
    x = inp["L0_attn_a_x"][0]
    t0 = TOWN * core
    pos = np.arange(t0, t0 + TOWN, dtype=np.float64)
    th = ATH[None, :]
    angl = (pos[:, None] + 63.0) * th
    angi = pos[:, None] * th
    mk = lambda a, f: np.ascontiguousarray(
        np.tile(f(a).T.astype(BF), (2, 1)))
    r32 = rot32_mat().astype(BF)
    return dict(
        xin=np.ascontiguousarray(x[t0:t0 + TOWN].T),
        wdkv=f32c(inp["L0_attn_a_W_dkv"]), wuk=f32c(inp["L0_attn_a_W_uk"]),
        wuv=f32c(inp["L0_attn_a_W_uv"]), wkr=f32c(inp["L0_attn_a_W_kr"]),
        wik=f32c(inp["L0_attn_a_idx_wk"]), wiq_=f32c(inp["L0_attn_a_idx_wq"]),
        cl_=mk(angl, np.cos), sl_=mk(angl, np.sin),
        ci_=mk(angi, np.cos), si_=mk(angi, np.sin),
        r32=r32, ob1f=np.ones((128, 1), np.float32), obcf=np.ones((1, 128), np.float32))

def run_aprep(progs, inp):
    in_maps = [aprep_maps(inp, c) for c in range(NCORES)]
    res = run_bass_kernel_spmd(progs["aprep"], in_maps, core_ids=list(range(NCORES)))
    cat = lambda key: np.concatenate([np.asarray(r[key]) for r in res.results], axis=1)
    return dict(lnx=cat("o_lnx"), ki=cat("o_ki"), K=cat("o_k"), V=cat("o_v"),
                mu=cat("o_mu"), krl=cat("o_krl"), kri=cat("o_kri"), qi=cat("o_qi"))

def attn_const_tables():
    r32 = rot32_mat()
    r128 = np.zeros((128, 128), np.float64)
    for b4 in range(4):
        r128[b4 * 32:(b4 + 1) * 32, b4 * 32:(b4 + 1) * 32] = r32
    g = np.arange(32, dtype=np.float64)
    d = np.arange(32)
    ang = (64.0 + g)[:, None] * ATH[d % 16][None, :]
    crot = np.tile(np.cos(ang).reshape(1, 1024).astype(BF), (128, 1))
    srot = np.tile(np.sin(ang).reshape(1, 1024).astype(BF), (128, 1))
    iden = np.eye(128, dtype=BF)
    kidx = np.tile(np.arange(512, dtype=np.float32)[None, :], (128, 1))
    return dict(r128=np.ascontiguousarray(r128.astype(BF)), crot=np.ascontiguousarray(crot),
                srot=np.ascontiguousarray(srot), iden=iden, kidx=np.ascontiguousarray(kidx))

def attn_core_tables(core, ntiles=16):
    p = np.arange(128)
    tglob = np.concatenate([128 * (8 * k + core) + p for k in range(ntiles)]).astype(np.float64)
    rowth = ATH[np.arange(128) % 16][:, None]
    c2t = np.cos(2.0 * tglob[None, :] * rowth).astype(BF)
    s2t = np.sin(2.0 * tglob[None, :] * rowth).astype(BF)
    cit = np.cos(tglob[None, :] * rowth).astype(BF)
    sit = np.sin(tglob[None, :] * rowth).astype(BF)
    thrs1 = np.zeros((128, ntiles * 32), np.float32)
    thrs2 = np.zeros((128, ntiles * 32), np.float32)
    for k in range(ntiles):
        t = 128 * (8 * k + core) + p
        for b in range(32):
            thrs1[:, k * 32 + b] = t - 64 - 512 * b
            thrs2[:, k * 32 + b] = t - 512 * b
    NEG = np.float32(-1e30)
    mloc = np.full((ntiles * 128, 256), NEG, np.float32)
    m96 = np.full((ntiles * 128, 96), NEG, np.float32)
    mg = np.zeros((ntiles * 128, 32), np.float32)
    j = np.arange(256)
    for k in range(ntiles):
        t = 128 * (8 * k + core) + p
        for pp_ in range(128):
            if t[pp_] <= 63:
                m96[k * 128 + pp_, :] = 0.0
                mg[k * 128 + pp_, :] = NEG
            else:
                sel = (j >= pp_ + 65) & (j <= pp_ + 128)
                mloc[k * 128 + pp_, sel] = 0.0
    return dict(c2t=np.ascontiguousarray(c2t), s2t=np.ascontiguousarray(s2t),
                cit=np.ascontiguousarray(cit), sit=np.ascontiguousarray(sit),
                thrs1=thrs1, thrs2=thrs2,
                mloc=np.ascontiguousarray(mloc.astype(BF)),
                m96=np.ascontiguousarray(m96.astype(BF)),
                mg=np.ascontiguousarray(mg.astype(BF)))

def attn_weight_maps(inp):
    wqr = np.asarray(inp["L0_attn_a_W_qr"], np.float32)
    wqrp = np.zeros((512, 512), np.float32)
    for oc in range(4):
        wqrp[:, oc * 128 + 0:oc * 128 + 32] = wqr[:, (2 * oc) * 32:(2 * oc) * 32 + 32]
        wqrp[:, oc * 128 + 64:oc * 128 + 96] = wqr[:, (2 * oc + 1) * 32:(2 * oc + 1) * 32 + 32]
    idxw = np.tile(np.asarray(inp["L0_attn_a_idx_w"], np.float32)[None, :], (128, 1))
    return dict(wdq=f32c(inp["L0_attn_a_W_dq"]), wuq=f32c(inp["L0_attn_a_W_uq"]),
                wqr=np.ascontiguousarray(wqrp), wout=f32c(inp["L0_attn_a_W_out"]),
                idxw=np.ascontiguousarray(idxw))

def attn_gather_maps(ap, core, ntiles=16):
    Kf, Vf, ki, mu, krl, kri, lnx = ap["K"], ap["V"], ap["ki"], ap["mu"], ap["krl"], ap["kri"], ap["lnx"]
    gt = np.ascontiguousarray(np.concatenate([Kf.T, mu.T, Vf.T], axis=1))
    kit = np.ascontiguousarray(np.tile(np.asarray(ki, np.float32), (2, 1)))
    qiP = np.zeros((256, ntiles * 128), np.float32)
    kclP = np.zeros((512, ntiles * 256), BF)
    vltP = np.zeros((ntiles * 256, 512), BF)
    krlP = np.zeros((128, ntiles * 256), BF)
    lnxP = np.zeros((DM, ntiles * 128), BF)
    for k in range(ntiles):
        tcg = 128 * (8 * k + core)
        lo = max(0, tcg - 128)
        off = lo - (tcg - 128)
        kclP[:, k * 256 + off:(k + 1) * 256] = Kf[:, lo:tcg + 128]
        vltP[k * 256 + off:(k + 1) * 256, :] = Vf.T[lo:tcg + 128, :]
        krlP[0:32, k * 256 + off:(k + 1) * 256] = krl[:, lo:tcg + 128]
        lnxP[:, k * 128:(k + 1) * 128] = lnx[:, tcg:tcg + 128]
        qiP[:, k * 128:(k + 1) * 128] = ap["qi"][:, tcg:tcg + 128]
    for rr in range(1, 4):
        krlP[rr * 32:(rr + 1) * 32, :] = krlP[0:32, :]
    k96 = np.ascontiguousarray(Kf[:, 0:96])
    v96 = np.ascontiguousarray(Vf.T[0:96, :])
    kr96 = np.ascontiguousarray(np.tile(kri[:, 0:96], (4, 1)))
    return dict(gt=gt, kit=kit, qiP=qiP, kclP=np.ascontiguousarray(kclP),
                vltP=np.ascontiguousarray(vltP), krlP=np.ascontiguousarray(krlP),
                lnxP=np.ascontiguousarray(lnxP), k96=k96, v96=v96, kr96=kr96)

def run_attn(progs, inp, ap, wcache, ntiles=16):
    if "attn_const" not in wcache:
        wcache["attn_const"] = attn_const_tables()
        wcache["attn_w"] = attn_weight_maps(inp)
        wcache["attn_core"] = [attn_core_tables(c, ntiles) for c in range(NCORES)]
    in_maps = []
    for c in range(NCORES):
        m = dict(wcache["attn_const"], **wcache["attn_w"], **wcache["attn_core"][c])
        m.update(attn_gather_maps(ap, c, ntiles))
        in_maps.append(m)
    res = run_bass_kernel_spmd(progs["attn"], in_maps, core_ids=list(range(NCORES)))
    o1 = np.zeros((DM, L), np.float32)
    for c in range(NCORES):
        oc_ = np.asarray(res.results[c]["o1o"])
        for k in range(ntiles):
            o1[:, 128 * (8 * k + c):128 * (8 * k + c) + 128] = oc_[:, k * 128:(k + 1) * 128]
    return o1

_PROGS = {}

def _get_progs():
    if not _PROGS:
        _PROGS["aprep"] = build_aprep()
        _PROGS["attn"] = build_attn()
        _PROGS["entry"] = build_entry()
        _PROGS["layer"] = build_ffn_layer(False)
        _PROGS["layer_exit"] = build_ffn_layer(True)
        _PROGS["mambaA"] = build_mamba(False)
        _PROGS["mambaB"] = build_mamba(True)
    return _PROGS


def kernel(**inputs):
    inp = inputs
    progs = _get_progs()
    wcache = {}
    x_fm = np.ascontiguousarray(np.asarray(inp["L0_attn_a_x"], np.float32)[0].T)
    ap = run_aprep(progs, inp)
    o1 = run_attn(progs, inp, ap, wcache)
    x2 = run_ffn_block(progs, x_fm, o1, inp, wcache)
    x3 = run_mamba(progs, x2, inp)
    zz = np.zeros((DM, L), np.float32)
    x4 = run_ffn_block(progs, x3, zz, inp, wcache)
    return np.ascontiguousarray(x4.T)[None].astype(np.float32)
